# version 26
# baseline (speedup 1.0000x reference)
import numpy as np
import ml_dtypes
from contextlib import ExitStack
import concourse.bass as bass
import concourse.mybir as mybir
from concourse.bass_utils import run_bass_kernel_spmd

F32 = mybir.dt.float32
BF16 = mybir.dt.bfloat16
I32 = mybir.dt.int32
ALU = mybir.AluOpType
AF = mybir.ActivationFunctionType
BF = ml_dtypes.bfloat16

T, D, H, DH, DFF = 4096, 1024, 16, 64, 4096
NT = T // 128
KC = D // 128
NP = H // 2
CC = 16
NCH = T // CC
TB = 512
NB = T // TB
EPS = 1e-5
NEG = -1e30
BETA = 0.9

ENGINES = ("sync", "scalar", "vector", "tensor", "gpsimd")


class Tok:
    __slots__ = ("s", "v")

    def __init__(self, s, v):
        self.s = s
        self.v = v


class BB:

    def __init__(self, nc, stack):
        self.nc = nc
        self.q = {e: [] for e in ENGINES}
        self.semh = {}
        self.semc = {}
        for name in ("dS", "dA", "dG", "oT", "oV", "oA", "oP", "oS"):
            self.semh[name] = stack.enter_context(nc.semaphore(name))
            self.semc[name] = 0
        self.dma_sem = {"sync": "dS", "scalar": "dA", "gpsimd": "dG"}
        self.chain = {"tensor": "oT", "vector": "oV", "scalar": "oA",
                      "gpsimd": "oP", "sync": "oS"}
        self._batch = {e: [] for e in ENGINES}

    def _close_batch(self, eng):
        bl = self._batch[eng]
        if bl:
            end = bl[-1].v
            for t in bl:
                t.v = end
            self._batch[eng] = []

    def w(self, eng, *toks):
        self._close_batch(eng)
        for t in toks:
            if t is None:
                continue
            h = self.semh[t.s]
            self.q[eng].append(lambda e, h=h, t=t: e.wait_ge(h, t.v))

    def _dma_common(self, eng, fn):
        s = self.dma_sem[eng]
        if not self._batch[eng] and self.semc[s] > 0:
            h = self.semh[s]
            v = self.semc[s]
            self.q[eng].append(lambda e, h=h, v=v: e.wait_ge(h, v))
        self.semc[s] += 16
        t = Tok(s, self.semc[s])
        self._batch[eng].append(t)
        self.q[eng].append(fn)
        return t

    def dma(self, eng, out, in_, **kw):
        h = self.semh[self.dma_sem[eng]]
        return self._dma_common(eng, lambda e, out=out, in_=in_, kw=kw, h=h:
                                e.dma_start(out=out, in_=in_, **kw).then_inc(h, 16))

    def dmat(self, eng, out, in_):
        h = self.semh[self.dma_sem[eng]]
        return self._dma_common(eng, lambda e, out=out, in_=in_, h=h:
                                e.dma_start_transpose(out, in_).then_inc(h, 16))

    def op(self, eng, meth, *args, **kw):
        self._close_batch(eng)
        s = self.chain[eng]
        self.semc[s] += 1
        t = Tok(s, self.semc[s])
        h = self.semh[s]
        lp = kw.pop("_lp", None)

        def fn(e, meth=meth, args=args, kw=kw, h=h, lp=lp):
            if lp:
                with self.nc.allow_low_precision(reason=lp):
                    getattr(e, meth)(*args, **kw).then_inc(h, 1)
            else:
                getattr(e, meth)(*args, **kw).then_inc(h, 1)
        self.q[eng].append(fn)
        return t

    def tok(self, eng):
        s = self.chain[eng]
        return Tok(s, self.semc[s])

    def barrier(self):
        for e in ENGINES:
            self._close_batch(e)
        toks = [Tok(s, c) for s, c in self.semc.items() if c > 0]
        for e in ENGINES:
            self.w(e, *toks)

    def dtok(self, eng):
        s = self.dma_sem[eng]
        return Tok(s, self.semc[s])

    def build_block(self, stack):
        block = stack.enter_context(self.nc.Block())
        q = self.q

        @block.sync
        def _(e):
            for fn in q["sync"]:
                fn(e)

        @block.scalar
        def _(e):
            for fn in q["scalar"]:
                fn(e)

        @block.vector
        def _(e):
            for fn in q["vector"]:
                fn(e)

        @block.tensor
        def _(e):
            for fn in q["tensor"]:
                fn(e)

        @block.gpsimd
        def _(e):
            for fn in q["gpsimd"]:
                fn(e)



def sbuf_image(w):
    K, M = w.shape
    t = w.reshape(K // 128, 128, M // 128, 128).transpose(1, 2, 0, 3)
    return np.ascontiguousarray(t.reshape(128, (M // 128) * (K // 128) * 128), dtype=np.float32)


def sbuf_image_tm(w):
    K, M = w.shape
    t = w.reshape(K // 128, 128, M // 128, 128).transpose(1, 0, 2, 3)
    return np.ascontiguousarray(t.reshape(128, (M // 128) * (K // 128) * 128), dtype=np.float32)


def col_tiles(bv):
    return np.ascontiguousarray(bv.reshape(-1, 128).T, dtype=np.float32)


def rope_perm():
    return np.arange(D).reshape(H, 32, 2).transpose(0, 2, 1).reshape(-1)


def prep_weights(inputs):
    g = {}
    f32 = lambda a: np.ascontiguousarray(np.asarray(a), dtype=np.float32)
    g["wqkv"] = sbuf_image(f32(inputs["L0_attn_Wqkv_w"]))
    g["bqkv"] = col_tiles(f32(inputs["L0_attn_Wqkv_b"]))
    g["gw1"] = sbuf_image(f32(inputs["L0_attn_gate_w1"]))
    g["gb1"] = col_tiles(f32(inputs["L0_attn_gate_b1"]))
    gw2 = f32(inputs["L0_attn_gate_w2"])
    g["gw2"] = np.ascontiguousarray(gw2.reshape(4, 128, 16).transpose(1, 0, 2).reshape(128, 64))
    g["gb2"] = f32(inputs["L0_attn_gate_b2"]).reshape(16, 1)
    g["outw"] = sbuf_image(f32(inputs["L0_attn_out_w"]))
    g["outb"] = col_tiles(f32(inputs["L0_attn_out_b"]))
    g["lamT"] = f32(np.transpose(inputs["L0_attn_Lambda_raw"], (0, 2, 1)))
    g["gamT"] = f32(np.transpose(inputs["L0_attn_Gamma"], (0, 2, 1)))
    g["lamN"] = f32(inputs["L0_attn_Lambda_raw"])
    g["gamN"] = f32(inputs["L0_attn_Gamma"])
    g["wstT"] = f32(np.transpose(inputs["L0_attn_W_static"], (0, 2, 1)))
    g["wfc"] = sbuf_image(f32(inputs["L0_ffn_W_fc"]).T)
    g["wproj"] = sbuf_image(f32(inputs["L0_ffn_W_proj"]).T)
    perm = rope_perm()
    g["w1q"] = sbuf_image_tm(f32(inputs["L1_attn_Wq"]).T[:, perm])
    g["w1k"] = sbuf_image_tm(f32(inputs["L1_attn_Wk"]).T[:, perm])
    g["w1v"] = sbuf_image(f32(inputs["L1_attn_Wv"]).T)
    g["w1o"] = sbuf_image(f32(inputs["L1_attn_Wo"]).T)
    g["f1w1"] = sbuf_image(f32(inputs["L1_ffn_w1_w"]).T)
    g["f1s1"] = col_tiles(f32(inputs["L1_ffn_w1_s"]))
    g["f1w2"] = sbuf_image(f32(inputs["L1_ffn_w2_w"]).T)
    g["f1s2"] = col_tiles(f32(inputs["L1_ffn_w2_s"]))
    g["f1h1"] = sbuf_image(f32(inputs["L1_ffn_h1_w"]).T)
    g["f1sh1"] = col_tiles(f32(inputs["L1_ffn_h1_s"]))
    g["f1h2"] = sbuf_image(f32(inputs["L1_ffn_h2_w"]).T)
    g["f1sh2"] = col_tiles(f32(inputs["L1_ffn_h2_s"]))
    g["f1w3"] = sbuf_image(f32(inputs["L1_ffn_w3_w"]).T)
    g["f1s3"] = col_tiles(f32(inputs["L1_ffn_w3_s"]))
    return g


WSHAPE = {
    "wqkv": [128, 24 * 8 * 128], "bqkv": [128, 24],
    "gw1": [128, 4 * 8 * 128], "gb1": [128, 4],
    "gw2": [128, 64], "gb2": [16, 1],
    "outw": [128, 8 * 8 * 128], "outb": [128, 8],
    "lamT": [H, 64, 64], "gamT": [H, 64, 64],
    "lamN": [H, 64, 64], "gamN": [H, 64, 64], "wstT": [H, 64, 64],
    "wfc": [128, 32 * 8 * 128], "wproj": [128, 8 * 32 * 128],
    "w1q": [128, 8 * 8 * 128], "w1k": [128, 8 * 8 * 128],
    "w1v": [128, 8 * 8 * 128], "w1o": [128, 8 * 8 * 128],
    "f1w1": [128, 32 * 8 * 128], "f1s1": [128, 32],
    "f1w2": [128, 32 * 8 * 128], "f1s2": [128, 32],
    "f1h1": [128, 32 * 32 * 128], "f1sh1": [128, 32],
    "f1h2": [128, 32 * 32 * 128], "f1sh2": [128, 32],
    "f1w3": [128, 8 * 32 * 128], "f1s3": [128, 8],
}

SCRATCH = {
    "xn0": ([KC, 128, T], BF16),
    "q0": ([NP, 128, T], BF16),
    "k0": ([NP, 128, T], BF16),
    "v0": ([NP, 128, T], BF16),
    "alpha": ([16, T], F32),
    "ffn0": ([KC, 128, T], BF16),
    "ysm0": ([NP, 128, T], BF16),
    "ystp": ([NP, 128, T], F32),
    "x1": ([KC, 128, T], F32),
    "xn1": ([KC, 128, T], BF16),
    "q1": ([NP, 128, T], BF16),
    "k1": ([NP, 128, T], BF16),
    "v1": ([NP, 128, T], BF16),
    "ysm1": ([NP, 128, T], BF16),
    "x2p": ([KC, 128, T], F32),
    "h1a": ([32, 128, T], BF16),
    "h1b": ([32, 128, T], BF16),
    "reprows": ([2, 17, 8192], BF16),
}


PHASES_ALL = ("p0a", "p0b", "p_mid", "p3", "p4a", "p4b", "p4c", "p5")


class Kern:
    def __init__(self, phases=PHASES_ALL, dbg_out=(), dbg_in=()):
        self.phases = phases
        self.dbg_out = set(dbg_out)
        self.dbg_in = set(dbg_in)
        self.nc = bass.Bass("TRN2", target_bir_lowering=False)

    def build(self):
        nc = self.nc
        self.stack = ExitStack()
        st = self.stack
        self.inp = {"x": nc.dram_tensor("x", [D, T], F32, kind="ExternalInput").ap()}
        for name, shape in WSHAPE.items():
            self.inp[name] = nc.dram_tensor(name, shape, F32, kind="ExternalInput").ap()
        self.outx = nc.dram_tensor("xout", [D, T], F32, kind="ExternalOutput").ap()
        self.scr = {}
        for name, (shape, dt) in SCRATCH.items():
            if name in self.dbg_in:
                h = nc.dram_tensor(name, shape, dt, kind="ExternalInput")
            elif name in self.dbg_out:
                h = nc.dram_tensor(name, shape, dt, kind="ExternalOutput")
            else:
                h = nc.dram_tensor(name, shape, dt)
            self.scr[name] = h.ap()

        b = BB(nc, st)
        self.b = b
        self.consts(st)
        for ph in self.phases:
            b.barrier()
            with ExitStack() as pst:
                getattr(self, ph)(pst)
        self.finish()
        b.build_block(st)
        self.stack.close()
        return nc

    def close(self):
        pass

    def consts(self, st):
        nc, b = self.nc, self.b
        en = st.enter_context
        self.idf = en(nc.sbuf_tensor("c_idf", [128, 128], F32))
        self.idb = en(nc.sbuf_tensor("c_idb", [128, 128], BF16))
        self.ones_col = en(nc.sbuf_tensor("c_onesc", [128, 1], F32))
        self.ones_rowf = en(nc.sbuf_tensor("c_onesrf", [1, 128], F32))
        self.ones_colb = en(nc.sbuf_tensor("c_onescb", [128, 1], BF16))
        self.masks = en(nc.sbuf_tensor("c_masks", [128, 4 * TB], BF16))
        self.epsc = en(nc.sbuf_tensor("c_eps", [1, 1], F32))
        self.erow17 = en(nc.sbuf_tensor("c_e17", [1, 17], F32))
        self.irow32 = en(nc.sbuf_tensor("c_i32", [1, 32], F32))
        self.idb8 = en(nc.sbuf_tensor("c_idb8", [128, 128], BF16))
        g = "gpsimd"
        t = b.op(g, "memset", self.idf[:], 1.0); b.w(g, t)
        t = b.op(g, "affine_select", out=self.idf[:], in_=self.idf[:],
                 compare_op=ALU.is_equal, fill=0.0, base=0,
                 pattern=[[-1, 128]], channel_multiplier=1); b.w(g, t)
        t = b.op(g, "memset", self.idb[:], 1.0); b.w(g, t)
        t = b.op(g, "affine_select", out=self.idb[:], in_=self.idb[:],
                 compare_op=ALU.is_equal, fill=0.0, base=0,
                 pattern=[[-1, 128]], channel_multiplier=1); b.w(g, t)
        t = b.op(g, "memset", self.ones_col[:], 1.0); b.w(g, t)
        t = b.op(g, "memset", self.ones_colb[:], 1.0); b.w(g, t)
        t = b.op(g, "memset", self.ones_rowf[:], 1.0); b.w(g, t)
        t = b.op(g, "memset", self.epsc[:], EPS); b.w(g, t)
        t = b.op(g, "iota", self.erow17[:], pattern=[[1, 17]], base=0,
                 channel_multiplier=0, allow_small_or_imprecise_dtypes=True); b.w(g, t)
        t = b.op(g, "iota", self.irow32[:], pattern=[[1, 32]], base=0,
                 channel_multiplier=0, allow_small_or_imprecise_dtypes=True); b.w(g, t)
        t = b.op(g, "memset", self.idb8[:], 0.125); b.w(g, t)
        t = b.op(g, "affine_select", out=self.idb8[:], in_=self.idb8[:],
                 compare_op=ALU.is_equal, fill=0.0, base=0,
                 pattern=[[-1, 128]], channel_multiplier=1); b.w(g, t)
        for r in range(4):
            sl = self.masks[:, r * TB:(r + 1) * TB]
            t = b.op(g, "memset", sl, 0.0); b.w(g, t)
            t = b.op(g, "affine_select", out=sl, in_=sl, compare_op=ALU.is_ge,
                     fill=NEG, base=-128 * r, pattern=[[1, TB]],
                     channel_multiplier=-1); b.w(g, t)
        self.ctok = b.tok(g)

    def load_weight_bf16(self, st, name, cols, tag, stage_cols=8192):
        nc, b = self.nc, self.b
        dst = st.enter_context(nc.sbuf_tensor(f"w_{tag}", [128, cols], BF16))
        rtoks = []
        with ExitStack() as tmp_st:
            stg = tmp_st.enter_context(nc.sbuf_tensor(f"wst_{tag}", [128, min(stage_cols, cols)], F32))
            off = 0
            prev_cast = None
            while off < cols:
                w = min(stage_cols, cols - off)
                lt = b.dma("sync", stg[:, 0:w], self.inp[name][:, off:off + w])
                b.w("vector", lt)
                if prev_cast is not None:
                    pass
                ct = b.op("vector", "tensor_copy", dst[:, off:off + w], stg[:, 0:w])
                b.w("vector", ct)
                b.w("sync", ct)
                prev_cast = ct
                off += w
            rtoks.append(prev_cast)
        return dst, rtoks[-1]

    def p0a(self, st):
        nc, b = self.nc, self.b
        en = st.enter_context
        x = self.inp["x"]
        wq, wqt = self.load_weight_bf16(st, "wqkv", 24 * 8 * 128, "qkv")
        gw1, gw1t = self.load_weight_bf16(st, "gw1", 4 * 8 * 128, "gw1")
        gw2, gw2t = self.load_weight_bf16(st, "gw2", 64, "gw2")
        bq = en(nc.sbuf_tensor("p0_bq", [128, 24], F32))
        gb1 = en(nc.sbuf_tensor("p0_gb1", [128, 4], F32))
        gb2 = en(nc.sbuf_tensor("p0_gb2", [16, 1], F32))
        bt1 = b.dma("sync", bq[:], self.inp["bqkv"])
        bt2 = b.dma("sync", gb1[:], self.inp["gb1"])
        bt3 = b.dma("sync", gb2[:], self.inp["gb2"])

        xt = [en(nc.sbuf_tensor(f"p0_x{i}", [128, TB], F32)) for i in range(KC)]
        sq = en(nc.sbuf_tensor("p0_sq", [128, TB], F32))
        rows = en(nc.sbuf_tensor("p0_rows", [1, 6 * TB], F32))
        xnb = [en(nc.sbuf_tensor(f"p0_xn{i}", [128, TB], BF16)) for i in range(KC)]
        gt = [en(nc.sbuf_tensor(f"p0_g{i}", [128, TB], BF16)) for i in range(4)]
        evb = [en(nc.sbuf_tensor(f"p0_ev{i}", [128, TB], BF16)) for i in range(3)]
        arow = en(nc.sbuf_tensor("p0_arow", [16, TB], F32))
        ps_sx = en(nc.psum_tensor("p0_psx", [128, 512], F32))
        ps_sx2 = en(nc.psum_tensor("p0_psx2", [128, 512], F32))
        ps_r = en(nc.psum_tensor("p0_psr", [128, 512], F32))
        ps_m = en(nc.psum_tensor("p0_psm", [128, 512], F32))
        pp = [en(nc.psum_tensor(f"p0_pp{i}", [128, 512], F32)) for i in range(2)]
        ps_a = en(nc.psum_tensor("p0_psa", [128, 512], F32))

        b.w("tensor", self.ctok, wqt, gw1t, gw2t, bt1, bt2, bt3)
        ROW = lambda i: rows[0:1, i * TB:(i + 1) * TB]

        ev_dma = [None, None, None]
        prev_xn_tok = None
        prev_gate_tok = None
        alpha_dma = None
        pp_free = [None, None]
        for blk in range(NB):
            t0 = blk * TB
            b.w("sync", prev_xn_tok)
            lt = [b.dma("sync", xt[i][:], x[i * 128:(i + 1) * 128, t0:t0 + TB])
                  for i in range(KC)]
            sq_free = None
            for i in range(KC):
                b.w("scalar", lt[i], sq_free)
                qt = b.op("scalar", "activation", sq[:], xt[i][:], AF.Square)
                b.w("tensor", qt)
                m2 = b.op("tensor", "matmul", ps_sx2[0:1, :], self.ones_col[:], sq[:],
                          start=(i == 0), stop=(i == KC - 1))
                sq_free = m2
            for i in range(KC):
                b.w("tensor", lt[i] if i == 0 else None)
                mt = b.op("tensor", "matmul", ps_sx[0:1, :], self.ones_col[:], xt[i][:],
                          start=(i == 0), stop=(i == KC - 1))
            sx_tok, sx2_tok = mt, m2
            b.w("vector", sx_tok)
            t = b.op("vector", "tensor_scalar", ROW(0), ps_sx[0:1, :], 1.0 / D, None, ALU.mult)
            b.w("vector", t, sx2_tok)
            t = b.op("vector", "tensor_scalar", ROW(1), ps_sx2[0:1, :], 1.0 / D, None, ALU.mult)
            b.w("vector", t)
            t = b.op("vector", "tensor_mul", ROW(2), ROW(0), ROW(0))
            b.w("vector", t)
            t = b.op("vector", "tensor_sub", ROW(2), ROW(1), ROW(2))
            b.w("scalar", t)
            t = b.op("scalar", "activation", ROW(3), ROW(2), AF.Sqrt, bias=self.epsc[0:1, 0:1])
            b.w("vector", t)
            t = b.op("vector", "reciprocal", ROW(4), ROW(3))
            b.w("vector", t)
            rstd_tok = t
            t = b.op("vector", "tensor_mul", ROW(5), ROW(0), ROW(4))
            mur_tok = t
            b.w("tensor", rstd_tok, prev_xn_tok)
            rep_r = b.op("tensor", "matmul", ps_r[:, :], self.ones_rowf[:], ROW(4),
                         start=True, stop=True)
            b.w("tensor", mur_tok)
            rep_m = b.op("tensor", "matmul", ps_m[:, :], self.ones_rowf[:], ROW(5),
                         start=True, stop=True)
            xn_toks = []
            for i in range(KC):
                b.w("vector", rep_r if i == 0 else None)
                t = b.op("vector", "tensor_mul", sq[:], xt[i][:], ps_r[:, :])
                b.w("vector", t, rep_m if i == 0 else None)
                t = b.op("vector", "tensor_sub", xnb[i][:], sq[:], ps_m[:, :])
                b.w("vector", t)
                xn_toks.append(t)
                b.w("gpsimd", t)
                b.dma("gpsimd", self.scr["xn0"][i, :, t0:t0 + TB], xnb[i][:])
            prev_xn_tok = xn_toks[-1]

            for m in range(24):
                pidx = m % 2
                b.w("tensor", pp_free[pidx])
                for k in range(KC):
                    b.w("tensor", xn_toks[k] if m == 0 else None)
                    b.op("tensor", "matmul", pp[pidx][:, :],
                         wq[:, (m * 8 + k) * 128:(m * 8 + k) * 128 + 128],
                         xnb[k][:], start=(k == 0), stop=(k == KC - 1))
                mm_tok = b.tok("tensor")
                j = m % 3
                bias = bq[:, m:m + 1]
                b.w("vector", mm_tok, ev_dma[j])
                if 8 <= m < 16:
                    t = b.op("vector", "tensor_scalar", evb[j][:], pp[pidx][:, :],
                             bias, 0.125, ALU.add, ALU.mult)
                else:
                    t = b.op("vector", "tensor_scalar", evb[j][:], pp[pidx][:, :],
                             bias, None, ALU.add)
                pp_free[pidx] = t
                b.w("gpsimd", t)
                dst = (self.scr["q0"] if m < 8 else self.scr["k0"] if m < 16 else self.scr["v0"])
                ev_dma[j] = b.dma("gpsimd", dst[m % 8, :, t0:t0 + TB], evb[j][:])

            for m in range(4):
                pidx = m % 2
                b.w("tensor", pp_free[pidx])
                for k in range(KC):
                    b.op("tensor", "matmul", pp[pidx][:, :],
                         gw1[:, (m * 8 + k) * 128:(m * 8 + k) * 128 + 128],
                         xnb[k][:], start=(k == 0), stop=(k == KC - 1))
                mm_tok = b.tok("tensor")
                b.w("scalar", mm_tok, prev_gate_tok if m < 2 else None)
                t = b.op("scalar", "activation", gt[m][:], pp[pidx][:, :], AF.Gelu,
                         bias=gb1[:, m:m + 1])
                pp_free[pidx] = t
            g_toks = b.tok("scalar")
            b.w("tensor", g_toks)
            for k in range(4):
                b.op("tensor", "matmul", ps_a[0:16, :], gw2[:, k * 16:(k + 1) * 16],
                     gt[k][:], start=(k == 0), stop=(k == 3))
            amm = b.tok("tensor")
            b.w("scalar", amm, alpha_dma)
            t = b.op("scalar", "activation", arow[:], ps_a[0:16, :], AF.Sigmoid,
                     bias=gb2[:])
            prev_gate_tok = t
            b.w("scalar", t)
            alpha_dma = b.dma("scalar", self.scr["alpha"][:, t0:t0 + TB], arow[:])

    def p0b(self, st):
        nc, b = self.nc, self.b
        en = st.enter_context
        wfc, t1 = self.load_weight_bf16(st, "wfc", 32 * 8 * 128, "wfc")
        wpj, t2 = self.load_weight_bf16(st, "wproj", 8 * 32 * 128, "wproj")
        xnb = [en(nc.sbuf_tensor(f"pb_xn{i}", [128, TB], BF16)) for i in range(KC)]
        ht = [en(nc.sbuf_tensor(f"pb_h{i}", [128, TB], BF16)) for i in range(32)]
        rt = en(nc.sbuf_tensor("pb_r", [128, TB], BF16))
        mem = en(nc.sbuf_tensor("pb_mem", [128, TB], F32))
        spk = en(nc.sbuf_tensor("pb_spk", [128, TB], F32))
        acc = en(nc.sbuf_tensor("pb_acc", [128, TB], F32))
        wv = en(nc.sbuf_tensor("pb_w", [128, TB], F32))
        sb = en(nc.sbuf_tensor("pb_sb", [128, TB], BF16))
        pp = [en(nc.psum_tensor(f"pb_pp{i}", [128, 512], F32)) for i in range(2)]

        b.w("tensor", t1, t2)
        prev_blk_tok = None
        sb_dma = None
        for blk in range(NB):
            t0 = blk * TB
            b.w("sync", prev_blk_tok)
            lt = [b.dma("sync", xnb[i][:], self.scr["xn0"][i, :, t0:t0 + TB])
                  for i in range(KC)]
            pp_free = [None, None]
            h_toks = []
            for m in range(32):
                pidx = m % 2
                b.w("tensor", pp_free[pidx])
                for k in range(KC):
                    b.w("tensor", lt[k] if m == 0 else None)
                    b.op("tensor", "matmul", pp[pidx][:, :],
                         wfc[:, (m * 8 + k) * 128:(m * 8 + k) * 128 + 128],
                         xnb[k][:], start=(k == 0), stop=(k == KC - 1))
                mm = b.tok("tensor")
                b.w("scalar", mm)
                tr = b.op("scalar", "activation", rt[:], pp[pidx][:, :], AF.Relu)
                pp_free[pidx] = tr
                b.w("vector", tr)
                th = b.op("vector", "tensor_mul", ht[m][:], rt[:], rt[:])
                b.w("scalar", th)
                h_toks.append(th)
            pp_free = [None, None]
            snn_last = None
            for m in range(KC):
                pidx = m % 2
                b.w("tensor", pp_free[pidx], h_toks[-1] if m == 0 else None)
                for k in range(32):
                    b.op("tensor", "matmul", pp[pidx][:, :],
                         wpj[:, (m * 32 + k) * 128:(m * 32 + k) * 128 + 128],
                         ht[k][:], start=(k == 0), stop=(k == 31))
                mm = b.tok("tensor")
                ps = pp[pidx]
                BKS = (0.24419428096993973, 0.46396913384288541, 0.68618592952553059,
                       0.90596078239847633, 1.0839784132255623)
                b.w("vector", mm, snn_last)
                t = b.op("vector", "tensor_scalar", acc[:], ps[:, :], BKS[0], None, ALU.is_gt)
                for bk in BKS[1:]:
                    b.w("vector", t)
                    t = b.op("vector", "scalar_tensor_tensor", acc[:], ps[:, :], bk,
                             acc[:], ALU.is_gt, ALU.add)
                pp_free[pidx] = t
                b.w("vector", sb_dma, t)
                t = b.op("vector", "tensor_copy", sb[:], acc[:])
                snn_last = t
                b.w("gpsimd", t)
                sb_dma = b.dma("gpsimd", self.scr["ffn0"][m, :, t0:t0 + TB], sb[:])
            prev_blk_tok = snn_last

    def p1_attn(self, st, qsrc, ksrc, vsrc, ydst, tag="a"):
        nc, b = self.nc, self.b
        en = st.enter_context
        q65 = en(nc.sbuf_tensor(f"p1{tag}_q65", [65, T], BF16))
        k65 = en(nc.sbuf_tensor(f"p1{tag}_k65", [65, T], BF16))
        vh = en(nc.sbuf_tensor(f"p1{tag}_vh", [64, T], BF16))
        sqt = en(nc.sbuf_tensor(f"p1{tag}_sqt", [64, T], F32))
        nrow = en(nc.sbuf_tensor(f"p1{tag}_nrow", [1, 2 * T + 8], F32))
        mrow = en(nc.sbuf_tensor(f"p1{tag}_mrow", [1, T], BF16))
        v65 = en(nc.sbuf_tensor(f"p1{tag}_v65", [128, 65 * NT], BF16))
        att = [en(nc.sbuf_tensor(f"p1{tag}_at{i}", [128, TB], BF16)) for i in range(2)]
        rcol = en(nc.sbuf_tensor(f"p1{tag}_rcol", [128, 1], F32))
        ytm = en(nc.sbuf_tensor(f"p1{tag}_ytm", [128, 64], BF16))
        ysm = en(nc.sbuf_tensor(f"p1{tag}_ysm", [64, T], BF16))
        psc = [en(nc.psum_tensor(f"p1{tag}_psc{i}", [128, 512], F32)) for i in range(2)]
        pv = [en(nc.psum_tensor(f"p1{tag}_pv{i}", [128, 512], F32)) for i in range(4)]
        ptb = en(nc.psum_tensor(f"p1{tag}_ptb", [128, 1024], BF16))
        psn = en(nc.psum_tensor(f"p1{tag}_psn", [128, 512], F32))

        QN, KN, MK = 0, T, 2 * T
        b.w("tensor", self.ctok)
        ysm_dma = None
        prev_head_tensor = None
        psc_free = [None, None]
        att_free = [None, None]
        for h in range(H):
            pair, half = h // 2, h % 2
            r0 = half * 64
            b.w("sync", prev_head_tensor)
            lq = b.dma("sync", q65[0:64, :], qsrc[pair, r0:r0 + 64, :])
            lk = b.dma("sync", k65[0:64, :], ksrc[pair, r0:r0 + 64, :])
            lv = b.dma("sync", vh[:], vsrc[pair, r0:r0 + 64, :])
            b.w("gpsimd", prev_head_tensor)
            v65r = v65.ap().rearrange("p (n c) -> p n c", c=65)
            ones_tok = b.op("gpsimd", "memset", v65r[:, :, 64:65], 1.0)
            kone = b.op("gpsimd", "memset", k65[64:65, :], 1.0)
            def normrow(src_ap, base, ld):
                b.w("scalar", ld)
                tq = b.op("scalar", "activation", sqt[:], src_ap, AF.Square)
                last = None
                for s in range(NB):
                    b.w("tensor", tq if s == 0 else None, last)
                    mm = b.op("tensor", "matmul", psn[0:1, 0:512], self.ones_col[0:64, :],
                              sqt[:, s * TB:(s + 1) * TB], start=True, stop=True)
                    b.w("scalar", mm)
                    last = b.op("scalar", "activation",
                                nrow[0:1, base + s * TB:base + (s + 1) * TB],
                                psn[0:1, 0:512], AF.Sqrt)
                return last
            t = normrow(q65[0:64, :], QN, lq)
            t = normrow(k65[0:64, :], KN, lk)
            b.w("vector", t)
            t = b.op("vector", "reduce_max", nrow[0:1, MK:MK + 1], nrow[0:1, KN:KN + T],
                     axis=mybir.AxisListType.X)
            b.w("vector", t)
            t = b.op("vector", "tensor_scalar", mrow[0:1, :], nrow[0:1, QN:QN + T],
                     nrow[0:1, MK:MK + 1], -1.0, ALU.mult, ALU.mult)
            b.w("gpsimd", t, lq)
            mdma = b.dma("gpsimd", q65[64:65, :], mrow[0:1, :])
            b.w("tensor", lv, ones_tok)
            ev_prev = None
            for kt in range(NT):
                tt = b.op("tensor", "transpose", ptb[0:128, 0:64],
                          vh[:, kt * 128:(kt + 1) * 128], self.idb[0:64, 0:64])
                b.w("vector", tt, ones_tok if kt == 0 else None)
                ev = b.op("vector", "tensor_copy", v65[:, kt * 65:kt * 65 + 64],
                          ptb[0:128, 0:64])
                b.w("tensor", ev)
            vtm_tok = b.tok("vector")
            b.w("tensor", vtm_tok, mdma, kone)
            b.w("vector", ysm_dma)
            for qb in range(NB):
                nk = 4 * qb + 4
                q_ap = q65[:, qb * TB:(qb + 1) * TB]

                def scores(kt):
                    pi = kt % 2
                    diag = kt >= 4 * qb
                    b.w("tensor", psc_free[pi])
                    b.op("tensor", "matmul", psc[pi][:, :], k65[:, kt * 128:(kt + 1) * 128],
                         q_ap, start=True, stop=not diag)
                    if diag:
                        r = kt - 4 * qb
                        b.op("tensor", "matmul", psc[pi][:, :], self.idb[:],
                             self.masks[:, r * TB:(r + 1) * TB], start=False, stop=True)
                    return b.tok("tensor")

                sc_tok = scores(0)
                for kt in range(nk):
                    pi = kt % 2
                    b.w("scalar", sc_tok, att_free[pi])
                    ex = b.op("scalar", "activation", att[pi][:], psc[pi][:, :], AF.Exp)
                    psc_free[pi] = ex
                    if kt + 1 < nk:
                        sc_tok = scores(kt + 1)
                    b.w("tensor", ex)
                    for m in range(4):
                        b.op("tensor", "matmul", pv[m][0:128, 0:65],
                             att[pi][:, m * 128:(m + 1) * 128],
                             v65[:, kt * 65:(kt + 1) * 65],
                             start=(kt == 0), stop=(kt == nk - 1))
                    att_free[pi] = b.tok("tensor")
                for m in range(4):
                    pv_tok = b.tok("tensor")
                    b.w("vector", pv_tok)
                    t = b.op("vector", "reciprocal", rcol[:], pv[m][0:128, 64:65])
                    b.w("vector", t)
                    t = b.op("vector", "tensor_scalar", ytm[:], pv[m][0:128, 0:64],
                             rcol[:], None, ALU.mult)
                    b.w("tensor", t)
                    tt = b.op("tensor", "transpose", ptb[0:64, 128:256], ytm[:], self.idb[:])
                    b.w("vector", tt)
                    t = b.op("vector", "tensor_copy",
                             ysm[:, qb * TB + m * 128:qb * TB + (m + 1) * 128],
                             ptb[0:64, 128:256])
                    b.w("tensor", t)
            fin = b.tok("vector")
            b.w("gpsimd", fin, ysm_dma)
            ysm_dma = b.dma("gpsimd", ydst[pair, r0:r0 + 64, :], ysm[:])
            prev_head_tensor = b.tok("tensor")

    def p2(self, st):
        nc, b = self.nc, self.b
        en = st.enter_context
        NEAR_T = CC
        dt = en(nc.sbuf_tensor("p2_dt", [128, 64], F32))
        gmt = en(nc.sbuf_tensor("p2_gmt", [128, 64], F32))
        mrun = en(nc.sbuf_tensor("p2_mrun", [128, 64], F32))
        mblk = en(nc.sbuf_tensor("p2_mblk", [128, CC * 128], BF16))
        wblk = en(nc.sbuf_tensor("p2_wblk", [128, 128], BF16))
        wsf = en(nc.sbuf_tensor("p2_wsf", [128, 64], F32))
        drow = en(nc.sbuf_tensor("p2_drow", [1, 8192], F32))
        ebf = en(nc.sbuf_tensor("p2_ebf", [32, 8192], BF16))
        dct = en(nc.sbuf_tensor("p2_dct", [128, 64 * 32], BF16))
        qpad = en(nc.sbuf_tensor("p2_qpad", [128, T + CC], BF16))
        kpr = en(nc.sbuf_tensor("p2_kpr", [128, T], BF16))
        vpr = en(nc.sbuf_tensor("p2_vpr", [128, T], BF16))
        y3 = en(nc.sbuf_tensor("p2_y3", [128, T], F32))
        utile = en(nc.sbuf_tensor("p2_u", [128, 8192], BF16))
        psit = utile
        lnd = drow
        gbf = utile
        rept2 = [en(nc.sbuf_tensor(f"p2_rep{i}", [128, 8192], BF16)) for i in range(2)]
        gacc = [en(nc.sbuf_tensor(f"p2_ga{i}", [128, 8192], BF16)) for i in range(2)]
        vre = [en(nc.sbuf_tensor(f"p2_vre{i}", [128, 2048], BF16)) for i in range(2)]
        kre = [en(nc.sbuf_tensor(f"p2_kre{i}", [128, 2048], BF16)) for i in range(2)]
        gtb = en(nc.sbuf_tensor("p2_gtb", [128, 256], BF16))
        phb = en(nc.sbuf_tensor("p2_phb", [128, 256], BF16))
        ptmp = en(nc.sbuf_tensor("p2_ptmp", [128, 512], BF16))
        y3f = en(nc.sbuf_tensor("p2_y3f", [128, 128], F32))
        ybf = kpr
        bps = [en(nc.psum_tensor(f"p2_bps{i}", [128, 512], F32)) for i in range(2)]
        ptb = en(nc.psum_tensor("p2_ptb", [128, 1024], BF16))
        psE = en(nc.psum_tensor("p2_psE", [128, 512], F32))
        ptf = en(nc.psum_tensor("p2_ptf", [128, 512], F32))

        lamT, gamT, wstT = self.inp["lamT"], self.inp["gamT"], self.inp["wstT"]
        lamN, gamN = self.inp["lamN"], self.inp["gamN"]
        reprows = self.scr["reprows"]
        mz = b.op("vector", "memset", mblk[:], 0.0)
        wz = b.op("vector", "memset", wblk[:], 0.0)
        ez = b.op("vector", "memset", ebf[:], 0.0)
        b.w("vector", mz, wz, ez)
        b.w("tensor", self.ctok)

        for p in range(NP):
            b.barrier()
            l1 = b.dma("sync", dt[:], lamT[2 * p:2 * p + 2].rearrange("h j i -> (h j) i"))
            l2 = b.dma("sync", gmt[:], gamT[2 * p:2 * p + 2].rearrange("h j i -> (h j) i"))
            l3 = b.dma("sync", wsf[:], wstT[2 * p:2 * p + 2].rearrange("h j i -> (h j) i"))
            l4 = b.dma("sync", drow[:], lamN[2 * p:2 * p + 2].rearrange("h i j -> () (h i j)"))
            lq = b.dma("sync", qpad[:, 0:T], self.scr["q0"][p])
            lk = b.dma("sync", kpr[:], self.scr["k0"][p])
            lv = b.dma("sync", vpr[:], self.scr["v0"][p])
            qz = b.op("vector", "memset", qpad[:, T:T + CC], 0.0)
            b.w("scalar", l1)
            t = b.op("scalar", "activation", dt[:], dt[:], AF.Sigmoid, scale=-1.0)
            dt_tok = t
            b.w("vector", l2, dt_tok)
            t = b.op("vector", "tensor_copy", mrun[:], gmt[:])
            for tau in range(CC):
                b.w("vector", t)
                c1 = b.op("vector", "tensor_copy", mblk[0:64, tau * 128:tau * 128 + 64], mrun[0:64, :])
                b.w("vector", c1)
                c2 = b.op("vector", "tensor_copy", mblk[64:128, tau * 128 + 64:tau * 128 + 128], mrun[64:128, :])
                b.w("vector", c2)
                t = b.op("vector", "tensor_mul", mrun[:], mrun[:], dt[:])
            mblk_tok = t
            b.w("vector", l3)
            c1 = b.op("vector", "tensor_copy", wblk[0:64, 0:64], wsf[0:64, :])
            b.w("vector", c1)
            c2 = b.op("vector", "tensor_copy", wblk[64:128, 64:128], wsf[64:128, :])
            wblk_tok = c2
            b.w("scalar", l4, dt_tok)
            t = b.op("scalar", "activation", drow[:], drow[:], AF.Sigmoid, scale=-1.0)
            b.w("scalar", t)
            t = b.op("scalar", "activation", lnd[:], drow[:], AF.Ln)
            lnd_tok = t
            elast = None
            for s in range(16):
                b.w("tensor", lnd_tok if s == 0 else None, elast)
                mm = b.op("tensor", "matmul", psE[0:17, 0:512], self.erow17[:],
                          lnd[0:1, s * 512:(s + 1) * 512], start=True, stop=True)
                b.w("scalar", mm)
                elast = b.op("scalar", "activation", ebf[0:17, s * 512:(s + 1) * 512],
                             psE[0:17, 0:512], AF.Exp)
            ebf_tok = elast
            b.w("gpsimd", ebf_tok)
            re1 = b.dma("gpsimd", reprows[0], ebf[0:17, :])
            b.w("scalar", ebf_tok)
            for s in range(64):
                b.dmat("scalar", dct[:, s * 32:(s + 1) * 32], ebf[0:32, s * 128:(s + 1) * 128])
            dct_tok = b.dtok("scalar")
            b.w("sync", b.tok("scalar"))
            l5 = b.dma("sync", drow[:], gamN[2 * p:2 * p + 2].rearrange("h i j -> () (h i j)"))
            b.w("vector", l5)
            t = b.op("vector", "tensor_copy", gbf[0:1, 0:8192], drow[:])
            b.w("gpsimd", t)
            rg = b.dma("gpsimd", reprows[1, 0:1, :], gbf[0:1, 0:8192])
            b.w("sync", rg)
            l6 = b.dma("sync", gbf[0:17, 0:8192], reprows[1, 0:1, :].to_broadcast([17, 8192]))
            b.w("vector", l6, re1, dct_tok)
            mtl_tok = b.op("vector", "tensor_mul", ebf[0:17, :], ebf[0:17, :], gbf[0:17, 0:8192])
            b.w("gpsimd", mtl_tok)
            re2 = b.dma("gpsimd", reprows[1], ebf[0:17, :])
            b.w("tensor", wblk_tok, lq, qz, lk, lv)
            bfree = [None, None]
            for s in range(NB):
                pi = s % 2
                b.w("tensor", bfree[pi])
                mm = b.op("tensor", "matmul", bps[pi][:, :], wblk[:],
                          qpad[:, s * TB:(s + 1) * TB], start=True, stop=True)
                b.w("vector", mm)
                ev = b.op("vector", "tensor_copy", y3[:, s * TB:(s + 1) * TB], bps[pi][:, :])
                bfree[pi] = ev
            qk = utile[:, 0:T]
            qk3 = qk.rearrange("p (c s) -> p c s", s=CC)
            y33 = y3.ap().rearrange("p (c s) -> p c s", s=CC)
            pt3 = ptmp.ap().rearrange("p (c s) -> p c s", s=CC)
            qk_tok = None
            b.w("tensor", mblk_tok)
            b.w("vector", re2)
            for tau in range(NEAR_T):
                b.w("vector", qk_tok)
                t = b.op("vector", "tensor_mul", qk, kpr[:], qpad[:, tau:tau + T])
                if tau > 0:
                    b.w("vector", t)
                    t = b.op("vector", "memset", qk3[:, :, CC - tau:CC], 0.0)
                qk_ready = t
                for s in range(NB):
                    pi = s % 2
                    b.w("tensor", qk_ready if s == 0 else None, bfree[pi])
                    mm = b.op("tensor", "matmul", bps[pi][:, :],
                              mblk[:, tau * 128:(tau + 1) * 128],
                              qk[:, s * TB:(s + 1) * TB], start=True, stop=True)
                    b.w("vector", mm, qk_tok)
                    t = b.op("vector", "tensor_mul", ptmp[:], vpr[:, s * TB:(s + 1) * TB],
                             bps[pi][:, :])
                    bfree[pi] = t
                    b.w("vector", t)
                    csl = slice(s * 32, (s + 1) * 32)
                    t = b.op("vector", "tensor_add",
                             y33[:, csl, tau:CC],
                             y33[:, csl, tau:CC],
                             pt3[:, :, 0:CC - tau])
                    qk_tok = t
            near_tok = qk_tok
            rr_tok = None
            for ct in range(2):
                for lam in range(CC):
                    vsl = vpr.ap()[:, ct * 2048 + lam: ct * 2048 + lam + 16 * 127 + 1:16]
                    tt = b.op("tensor", "transpose", ptb[0:128, 0:128], vsl, self.idb[:])
                    b.w("vector", tt)
                    rr_tok = b.op("vector", "tensor_copy", vre[ct][:, lam * 128:(lam + 1) * 128],
                                  ptb[0:128, 0:128])
                    b.w("tensor", rr_tok)
                    ksl = kpr.ap()[:, ct * 2048 + lam: ct * 2048 + lam + 16 * 127 + 1:16]
                    tt = b.op("tensor", "transpose", ptb[0:128, 0:128], ksl, self.idb[:])
                    b.w("vector", tt)
                    rr_tok = b.op("vector", "tensor_copy", kre[ct][:, lam * 128:(lam + 1) * 128],
                                  ptb[0:128, 0:128])
                    b.w("tensor", rr_tok)
            gtok = [None, None]
            rep_rd = [None, None]
            rep_user = [None, None]
            b.w("gpsimd", re1)
            rep_rd[0] = b.dma("gpsimd", rept2[0][:], reprows[0, CC:CC + 1, :].to_broadcast([128, 8192]))
            for lam in range(CC):
                buf = lam % 2
                if lam + 1 < CC:
                    e2 = CC - (lam + 1)
                    b.w("gpsimd", rep_user[1 - buf])
                    rep_rd[1 - buf] = b.dma("gpsimd", rept2[1 - buf][:],
                                            reprows[0, e2:e2 + 1, :].to_broadcast([128, 8192]))
                for ct in range(2):
                    vbc = vre[ct][:, lam * 128:(lam + 1) * 128].rearrange(
                        "p (h i) -> p h i", h=2).unsqueeze(3).to_broadcast([128, 2, 64, 64])
                    kbc = kre[ct][:, lam * 128:(lam + 1) * 128].rearrange(
                        "p (h j) -> p h j", h=2).unsqueeze(2).to_broadcast([128, 2, 64, 64])
                    u4 = utile.ap().rearrange("p (h i j) -> p h i j", h=2, i=64)
                    b.w("vector", rr_tok if lam == 0 and ct == 0 else None, gtok[1 - ct])
                    t = b.op("vector", "tensor_mul", u4, vbc, kbc)
                    b.w("vector", t, rep_rd[buf] if ct == 0 else None)
                    t = b.op("vector", "tensor_mul", utile[:], utile[:], rept2[buf][:])
                    b.w("vector", t)
                    if lam == 0:
                        t = b.op("vector", "tensor_copy", gacc[ct][:], utile[:])
                    else:
                        t = b.op("vector", "tensor_add", gacc[ct][:], gacc[ct][:], utile[:])
                    gtok[ct] = t
                rep_user[buf] = gtok[1]
            scan_tok = None
            wb_tok = None
            b.w("tensor", gtok[1], dct_tok)
            for s in range(64):
                tt1 = b.op("tensor", "transpose", ptb[0:128, 0:128],
                           gacc[0][:, s * 128:(s + 1) * 128], self.idb[:])
                b.w("vector", tt1, wb_tok)
                ev1 = b.op("vector", "tensor_copy", gtb[:, 0:128], ptb[0:128, 0:128])
                b.w("tensor", ev1)
                tt2 = b.op("tensor", "transpose", ptb[0:128, 0:128],
                           gacc[1][:, s * 128:(s + 1) * 128], self.idb[:])
                b.w("vector", tt2)
                ev2 = b.op("vector", "tensor_copy", gtb[:, 128:256], ptb[0:128, 0:128])
                b.w("vector", ev2)
                t = b.op("vector", "memset", phb[:, 0:1], 0.0)
                b.w("vector", t)
                dc = dcol.ap()[:, s:s + 1].to_broadcast([128, 255])
                t = b.op("vector", "tensor_tensor_scan", phb[:, 1:256], dc,
                         gtb[:, 0:255], 0.0, ALU.mult, ALU.add)
                scan_tok = t
                b.w("tensor", t, ev2)
                tt3 = b.op("tensor", "transpose", ptb[0:128, 0:128], phb[:, 0:128], self.idb[:])
                b.w("vector", tt3)
                wb1 = b.op("vector", "tensor_copy", gacc[0][:, s * 128:(s + 1) * 128],
                           ptb[0:128, 0:128])
                b.w("tensor", wb1)
                tt4 = b.op("tensor", "transpose", ptb[0:128, 0:128], phb[:, 128:256], self.idb[:])
                b.w("vector", tt4)
                wb2 = b.op("vector", "tensor_copy", gacc[1][:, s * 128:(s + 1) * 128],
                           ptb[0:128, 0:128])
                b.w("tensor", wb2)
                wb_tok = wb2
            phic = gacc
            b.w("tensor", wb_tok)
            qre_tok = None
            for ct in range(2):
                for lam in range(CC):
                    qsl = qpad.ap()[:, ct * 2048 + lam: ct * 2048 + lam + 16 * 127 + 1:16]
                    tt = b.op("tensor", "transpose", ptb[0:128, 0:128], qsl, self.idb[:])
                    b.w("vector", tt)
                    qre_tok = b.op("vector", "tensor_copy", kre[ct][:, lam * 128:(lam + 1) * 128],
                                   ptb[0:128, 0:128])
                    b.w("tensor", qre_tok)
            cons_tok = near_tok
            rep_user = [gtok[1], gtok[1]]
            b.w("gpsimd", re2, rep_user[0])
            rep_rd[0] = b.dma("gpsimd", rept2[0][:], reprows[1, 0:1, :].to_broadcast([128, 8192]))
            for lam in range(CC):
                buf = lam % 2
                if lam + 1 < CC:
                    b.w("gpsimd", rep_user[1 - buf])
                    rep_rd[1 - buf] = b.dma("gpsimd", rept2[1 - buf][:],
                                            reprows[1, lam + 1:lam + 2, :].to_broadcast([128, 8192]))
                for ct in range(2):
                    b.w("vector", rep_rd[buf] if ct == 0 else None,
                        qre_tok if lam == 0 and ct == 0 else None,
                        wb_tok if lam == 0 and ct == 0 else None)
                    t = b.op("vector", "tensor_mul", psit[:], phic[ct][:], rept2[buf][:])
                    qbc = kre[ct][:, lam * 128:(lam + 1) * 128].rearrange(
                        "p (h j) -> p h j", h=2).unsqueeze(2).to_broadcast([128, 2, 64, 64])
                    p4 = psit.ap().rearrange("p (h i j) -> p h i j", h=2, i=64)
                    b.w("vector", t)
                    t = b.op("vector", "tensor_mul", p4, p4, qbc)
                    p3v = psit.ap().rearrange("p (g j) -> p g j", j=64)
                    wdt = 32
                    while wdt >= 1:
                        b.w("vector", t)
                        if wdt == 1:
                            t = b.op("vector", "tensor_add",
                                     y3f.ap().unsqueeze(2),
                                     p3v[:, :, 0:1], p3v[:, :, 1:2])
                        else:
                            t = b.op("vector", "tensor_add", p3v[:, :, 0:wdt],
                                     p3v[:, :, 0:wdt], p3v[:, :, wdt:2 * wdt])
                        wdt //= 2
                    rep_user[buf] = t
                    b.w("tensor", t)
                    tt = b.op("tensor", "transpose", ptf[0:128, 0:128], y3f[:], self.idf[:])
                    b.w("vector", tt, cons_tok)
                    yap = y33[:, ct * 128:(ct + 1) * 128, lam:lam + 1]
                    t = b.op("vector", "tensor_add", yap, yap,
                             ptf[0:128, 0:128].unsqueeze(2))
                    cons_tok = t
                    b.w("tensor", t)
            b.w("vector", cons_tok)
            t = b.op("vector", "tensor_copy", ybf[:], y3[:])
            b.w("gpsimd", t)
            b.dma("gpsimd", self.scr["ystp"][p], ybf[:])
            b.w("vector", b.dtok("gpsimd"))

    def p1(self, st):
        self.p1_attn(st, self.scr["q0"], self.scr["k0"], self.scr["v0"], self.scr["ysm0"])

    def p1i_alloc(self, st, tag="i1", natt=2, npsc=2):
        nc, b = self.nc, self.b
        en = st.enter_context
        t = {}
        t["q65"] = en(nc.sbuf_tensor(f"{tag}_q65", [65, T], BF16))
        t["k65"] = en(nc.sbuf_tensor(f"{tag}_k65", [65, T], BF16))
        t["vh"] = en(nc.sbuf_tensor(f"{tag}_vh", [64, T], BF16))
        t["sqt"] = en(nc.sbuf_tensor(f"{tag}_sqt", [64, T], BF16))
        t["nrow"] = None
        t["mrow"] = None
        t["nrow_guard"] = None
        t["rmax"] = en(nc.sbuf_tensor(f"{tag}_rmax", [1, 1], F32))
        t["v65"] = en(nc.sbuf_tensor(f"{tag}_v65", [128, 65 * NT], BF16))
        t["att"] = [en(nc.sbuf_tensor(f"{tag}_at{i}", [128, TB], BF16)) for i in range(natt)]
        t["yraw"] = en(nc.sbuf_tensor(f"{tag}_yr", [65, T], BF16))
        t["rrow"] = en(nc.sbuf_tensor(f"{tag}_rr", [1, T], BF16))
        t["ones1"] = en(nc.sbuf_tensor(f"{tag}_o1", [1, 64], BF16))
        t["psc"] = [en(nc.psum_tensor(f"{tag}_psc{i}", [128, 512], F32)) for i in range(npsc)]
        t["pv"] = en(nc.psum_tensor(f"{tag}_pv", [128, 512], F32))
        t["ptb"] = en(nc.psum_tensor(f"{tag}_ptb", [128, 1024], BF16))
        v65r = t["v65"].ap().rearrange("p (n c) -> p n c", c=65)
        o1 = b.op("gpsimd", "memset", v65r[:, :, 64:65], 1.0)
        o2 = b.op("gpsimd", "memset", t["k65"][64:65, :], 1.0)
        o3 = b.op("gpsimd", "memset", t["ones1"][:], 1.0)
        b.w("gpsimd", o1, o2, o3)
        t["ones1_tok"] = o3
        t["kone"] = o2
        t["ones"] = o1
        t["ysm_dma"] = None
        t["vtm"] = None
        t["sqt_mm"] = None
        t["prev_tensor"] = None
        t["psc_free"] = [None] * npsc
        t["att_free"] = [None] * natt
        t["prep_b_done"] = None
        t["mdma"] = None
        t["evac_last"] = None
        t["fin_vec"] = None
        t["fin_pe"] = None
        t["fin_mul"] = [None] * npsc
        return t

    def p1i_prep_a(self, t, h, qsrc, ksrc, vsrc):
        nc, b = self.nc, self.b
        pair, half = h // 2, h % 2
        r0 = half * 64
        par = h % 2
        q65, k65, sqt = t["q65"], t["k65"], t["sqt"]
        vh = t["vh"]
        nrow, mrow, v65 = t["nrow"], t["mrow"], t["v65"]
        psc, ptb = t["psc"], t["ptb"]
        psc_free = t["psc_free"]
        b.w("scalar", t.pop("load_guard", t["prev_tensor"]), t["ysm_dma"],
            t["prep_b_done"], t["mdma"], t["vtm"])
        lq = b.dma("scalar", q65[0:64, :], qsrc[pair, r0:r0 + 64, :])
        lk = b.dma("scalar", k65[0:64, :], ksrc[pair, r0:r0 + 64, :])
        lv = b.dma("scalar", vh[:], vsrc[pair, r0:r0 + 64, :])
        psn = psc[0]

        def normrow(src_ap, dstrow, ld):
            b.w("scalar", ld)
            tq = b.op("scalar", "activation", sqt[:], src_ap, AF.Square)
            last = None
            for s in range(NB):
                b.w("tensor", tq if s == 0 else None, last,
                    psc_free[0] if s == 0 else None)
                mm = b.op("tensor", "matmul", psn[0:1, 0:512], self.ones_colb[0:64, :],
                          sqt[:, s * TB:(s + 1) * TB], start=True, stop=True)
                b.w("scalar", mm)
                last = b.op("scalar", "activation",
                            dstrow[0:1, s * TB:(s + 1) * TB],
                            psn[0:1, 0:512], AF.Sqrt)
            return last
        b.w("scalar", t["nrow_guard"])
        tn = normrow(q65[0:64, :], nrow, lq)
        tn = normrow(k65[0:64, :], mrow, lk)
        psc_free[0] = tn
        t["tn"] = tn
        t["sqt_mm"] = b.tok("tensor")
        b.w("tensor", lv, t["ones"] if h == 0 else None)
        for kt in range(NT):
            tt = b.op("tensor", "transpose", ptb[0:128, 0:64],
                      vh[:, kt * 128:(kt + 1) * 128], self.idb[0:64, 0:64])
            b.w("scalar", tt)
            ev = b.op("scalar", "copy", v65[:, kt * 65:kt * 65 + 64],
                      ptb[0:128, 0:64])
            b.w("tensor", ev)
        t["vtm"] = b.tok("scalar")

    def p1i_prep_b(self, t, h):
        b = self.b
        b.w("vector", t["tn"])
        rx = b.op("vector", "reduce_max", t["rmax"][0:1, 0:1], t["mrow"][0:1, :],
                  axis=mybir.AxisListType.X)
        b.w("vector", rx)
        ts = b.op("vector", "tensor_scalar", t["mrow"][0:1, :], t["nrow"][0:1, :],
                  t["rmax"][0:1, 0:1], -1.0, ALU.mult, ALU.mult)
        b.w("gpsimd", ts, t["prev_tensor"])
        t["mdma"] = b.dma("gpsimd", t["q65"][64:65, :], t["mrow"][0:1, :])
        t["prep_b_done"] = ts

    def p1i_main(self, t, h):
        nc, b = self.nc, self.b
        q65, k65, v65 = t["q65"], t["k65"], t["v65"]
        att = t["att"]
        yraw = t["yraw"]
        psc, pvP = t["psc"], t["pv"]
        psc_free, att_free = t["psc_free"], t["att_free"]
        b.w("tensor", t["vtm"], t["mdma"], t["kone"] if h == 0 else None)
        b.w("scalar", t["fin_vec"], t["fin_pe"])
        t["pv_guard"] = t.get("pv_guard_next")
        cp = None
        for qb in range(NB):
            nk = 4 * qb + 4
            q_ap = q65[:, qb * TB:(qb + 1) * TB]

            def scores(kt):
                pi = kt % len(psc)
                diag = kt >= 4 * qb
                b.w("tensor", psc_free[pi])
                b.op("tensor", "matmul", psc[pi][:, :], k65[:, kt * 128:(kt + 1) * 128],
                     q_ap, start=True, stop=not diag)
                if diag:
                    r = kt - 4 * qb
                    b.op("tensor", "matmul", psc[pi][:, :], self.idb[:],
                         self.masks[:, r * TB:(r + 1) * TB], start=False, stop=True)
                return b.tok("tensor")

            exs = [None] * nk

            def pv(kt, w=True):
                ai = kt % len(att)
                b.w("tensor", exs[kt] if w else None,
                    t.pop("pv_guard", None) if kt == 0 and qb == 0 else None,
                    t.pop("cp_guard", None) if kt == 0 else None)
                b.op("tensor", "matmul", pvP[0:65, 0:512],
                     v65[:, kt * 65:(kt + 1) * 65], att[ai][:, :],
                     start=(kt == 0), stop=(kt == nk - 1))
                att_free[ai] = b.tok("tensor")
            if len(psc) >= 4:
                sc = {0: scores(0)}
                if nk > 1:
                    sc[1] = scores(1)
                for kt in range(nk):
                    pi = kt % len(psc)
                    ai = kt % len(att)
                    if kt % 2 == 0:
                        w2 = min(kt + 1, nk - 1)
                        b.w("scalar", sc[w2], att_free[ai],
                            att_free[(kt + 1) % len(att)] if kt + 1 < nk else None)
                    ex = b.op("scalar", "activation", att[ai][:], psc[pi][:, :], AF.Exp)
                    psc_free[pi] = ex
                    exs[kt] = ex
                    if kt + 2 < nk:
                        sc[kt + 2] = scores(kt + 2)
                    if kt % 2 == 1 and kt >= 3:
                        b.w("tensor", exs[kt - 2])
                        pv(kt - 3, w=False)
                        pv(kt - 2, w=False)
                b.w("tensor", exs[nk - 1])
                pv(nk - 2, w=False)
                pv(nk - 1, w=False)
            else:
                sc_tok = scores(0)
                for kt in range(nk):
                    pi = kt % len(psc)
                    ai = kt % len(att)
                    b.w("scalar", sc_tok, att_free[ai])
                    ex = b.op("scalar", "activation", att[ai][:], psc[pi][:, :], AF.Exp)
                    psc_free[pi] = ex
                    exs[kt] = ex
                    if kt + 1 < nk:
                        sc_tok = scores(kt + 1)
                    if kt >= 1:
                        pv(kt - 1)
                pv(nk - 1)
            pv_tok = b.tok("tensor")
            ee = t.get("evac_eng", "scalar")
            b.w(ee, pv_tok)
            cp = b.op(ee, "tensor_copy" if ee == "vector" else "copy",
                      yraw[:, qb * TB:(qb + 1) * TB], pvP[0:65, 0:512])
            t["cp_guard"] = cp
        t["evac_last"] = cp
        t["prev_tensor"] = b.tok("tensor")

    def p1i_fin(self, t, h, ydst):
        nc, b = self.nc, self.b
        pair, half = h // 2, h % 2
        r0 = half * 64
        yraw, rrow = t["yraw"], t["rrow"]
        ysm = t["sqt"]
        pvP = t["pv"]
        b.w("vector", t["evac_last"], t["fin_vec"])
        mul = None
        for s in range(NB):
            pi = s % 2
            hp = pvP[pi * 64:pi * 64 + 64, 0:512]
            rc = b.op("vector", "reciprocal", rrow[0:1, s * TB:(s + 1) * TB],
                      yraw[64:65, s * TB:(s + 1) * TB],
                      _lp="bf16 softmax denominator reciprocal (within tolerance)")
            b.w("tensor", rc, t["fin_mul"][pi],
                t["evac_last"] if s == 0 else None,
                t["sqt_mm"] if s == 0 else None,
                t["ones1_tok"] if s == 0 and h == 0 else None)
            bc = b.op("tensor", "matmul", hp, t["ones1"][:],
                      rrow[0:1, s * TB:(s + 1) * TB], start=True, stop=True)
            b.w("vector", bc, t["ysm_dma"] if s == 0 else None)
            mul = b.op("vector", "tensor_mul", ysm[:, s * TB:(s + 1) * TB],
                       yraw[0:64, s * TB:(s + 1) * TB], hp)
            t["fin_mul"][pi] = mul
        t["fin_vec"] = mul
        t["pv_guard_next"] = mul
        t["fin_pe"] = b.tok("tensor")
        b.w("scalar", mul)
        t["ysm_dma"] = b.dma("scalar", ydst[pair, r0:r0 + 64, :], ysm[:])

    def p2i_alloc(self, st):
        nc, b = self.nc, self.b
        en = st.enter_context
        t = {}
        t["dt"] = en(nc.sbuf_tensor("i2_dt", [128, 64], F32))
        t["gmt"] = en(nc.sbuf_tensor("i2_gmt", [128, 64], F32))
        t["mrun"] = en(nc.sbuf_tensor("i2_mrun", [128, 64], F32))
        t["mblk"] = en(nc.sbuf_tensor("i2_mblk", [128, CC * 128], BF16))
        t["wblk"] = en(nc.sbuf_tensor("i2_wblk", [128, 128], BF16))
        t["wsf"] = en(nc.sbuf_tensor("i2_wsf", [128, 64], F32))
        t["drow"] = en(nc.sbuf_tensor("i2_drow", [1, 2048], F32))
        t["ebf"] = en(nc.sbuf_tensor("i2_ebf", [17, 4096], BF16))
        t["dcol"] = [en(nc.sbuf_tensor(f"i2_dc{i}", [128, 32], BF16)) for i in range(2)]
        t["est"] = en(nc.sbuf_tensor("i2_est", [32, 128], BF16))
        t["qpad"] = en(nc.sbuf_tensor("i2_qpad", [128, T + CC], BF16))
        t["kpr"] = en(nc.sbuf_tensor("i2_kpr", [128, T], BF16))
        t["vpr"] = en(nc.sbuf_tensor("i2_vpr", [128, T], BF16))
        t["y3"] = en(nc.sbuf_tensor("i2_y3", [128, T], F32))
        t["u"] = en(nc.sbuf_tensor("i2_u", [128, 4096], BF16))
        t["rep"] = [en(nc.sbuf_tensor(f"i2_rep{i}", [128, 4096], BF16)) for i in range(2)]
        t["gacc"] = [en(nc.sbuf_tensor(f"i2_ga{i}", [128, 4096], BF16)) for i in range(2)]
        t["vre"] = [en(nc.sbuf_tensor(f"i2_vre{i}", [128, 2048], BF16)) for i in range(2)]
        t["kre"] = [en(nc.sbuf_tensor(f"i2_kre{i}", [128, 2048], BF16)) for i in range(2)]
        t["qre"] = [en(nc.sbuf_tensor(f"i2_qre{i}", [128, 2048], BF16)) for i in range(2)]
        t["gtb"] = [en(nc.sbuf_tensor(f"i2_gtb{i}", [128, 256], BF16)) for i in range(2)]
        t["phb"] = [en(nc.sbuf_tensor(f"i2_phb{i}", [128, 256], BF16)) for i in range(2)]
        t["ptmp"] = en(nc.sbuf_tensor("i2_ptmp", [128, 512], BF16))
        t["y3fb"] = [en(nc.sbuf_tensor(f"i2_y3fb{i}", [128, 128], BF16)) for i in range(2)]
        t["y3ft"] = [en(nc.sbuf_tensor(f"i2_y3ft{i}", [128, 128], BF16)) for i in range(2)]
        t["y3h1"] = en(nc.sbuf_tensor("i2_y3h1", [64, T], F32))
        t["bps"] = en(nc.psum_tensor("i2_bps", [128, 512], F32))
        t["ptb"] = en(nc.psum_tensor("i2_ptb", [128, 1024], BF16))
        t["pts"] = en(nc.psum_tensor("i2_pts", [128, 1024], BF16))
        mz = b.op("vector", "memset", t["mblk"][:], 0.0)
        wz = b.op("vector", "memset", t["wblk"][:], 0.0)
        ez = b.op("vector", "memset", t["est"][:], 0.0)
        qz = b.op("vector", "memset", t["qpad"][:, T:T + CC], 0.0)
        t["qz"] = qz
        p0 = b.op("vector", "memset", t["phb"][0][:, 0:1], 0.0)
        p1z = b.op("vector", "memset", t["phb"][1][:, 0:1], 0.0)
        t["phz"] = p1z
        f0 = b.op("vector", "memset", t["y3fb"][0][:], 0.0)
        f1 = b.op("vector", "memset", t["y3fb"][1][:], 0.0)
        b.w("vector", mz, wz, ez, qz, p0, p1z, f0, f1)
        return t

    def p2i_pair_loads_params(self, t, p):
        b = self.b
        lamT, gamT, wstT = self.inp["lamT"], self.inp["gamT"], self.inp["wstT"]
        b.w("scalar", t.get("mblk_tok"), t.get("wblk_tok"))
        t["pl1"] = b.dma("scalar", t["dt"][:], lamT[2 * p:2 * p + 2].rearrange("h j i -> (h j) i"))
        t["pl2"] = b.dma("scalar", t["gmt"][:], gamT[2 * p:2 * p + 2].rearrange("h j i -> (h j) i"))
        t["pl3"] = b.dma("scalar", t["wsf"][:], wstT[2 * p:2 * p + 2].rearrange("h j i -> (h j) i"))

    def p2i_pair_loads(self, t, p, t1):
        b = self.b
        b.w("scalar", t.get("near_tok"), t.get("rr_tok"), t1.get("prep_b_done"),
            t1.get("mdma"))
        t["plq"] = b.dma("scalar", t["qpad"][:, 0:T], self.scr["q0"][p])
        t["plk"] = b.dma("scalar", t["kpr"][:], self.scr["k0"][p])
        t["plv"] = b.dma("scalar", t["vpr"][:], self.scr["v0"][p])
        t["pre_pair"] = p

    def p2i_pair_start(self, t, p):
        nc, b = self.nc, self.b
        lamT, gamT, wstT = self.inp["lamT"], self.inp["gamT"], self.inp["wstT"]
        dt, gmt, mrun, mblk, wblk, wsf = (t["dt"], t["gmt"], t["mrun"], t["mblk"],
                                          t["wblk"], t["wsf"])
        qpad, kpr, vpr, y3 = t["qpad"], t["kpr"], t["vpr"], t["y3"]
        bps, ptmp = t["bps"], t["ptmp"]
        utile = t["u"]
        if t.get("pre_pair") == p:
            l1, l2, l3 = t["pl1"], t["pl2"], t["pl3"]
            lq, lk, lv = t["plq"], t["plk"], t["plv"]
        else:
            l1 = b.dma("sync", dt[:], lamT[2 * p:2 * p + 2].rearrange("h j i -> (h j) i"))
            l2 = b.dma("sync", gmt[:], gamT[2 * p:2 * p + 2].rearrange("h j i -> (h j) i"))
            l3 = b.dma("sync", wsf[:], wstT[2 * p:2 * p + 2].rearrange("h j i -> (h j) i"))
            lq = b.dma("sync", qpad[:, 0:T], self.scr["q0"][p])
            lk = b.dma("sync", kpr[:], self.scr["k0"][p])
            lv = b.dma("sync", vpr[:], self.scr["v0"][p])
        b.w("scalar", l1)
        dt_tok = b.op("scalar", "activation", dt[:], dt[:], AF.Sigmoid, scale=-1.0)
        b.w("vector", l2, dt_tok)
        tv = b.op("vector", "tensor_copy", mrun[:], gmt[:])
        for tau in range(CC):
            b.w("vector", tv)
            c1 = b.op("vector", "tensor_copy", mblk[0:64, tau * 128:tau * 128 + 64], mrun[0:64, :])
            b.w("vector", c1)
            c2 = b.op("vector", "tensor_copy", mblk[64:128, tau * 128 + 64:tau * 128 + 128],
                      mrun[64:128, :])
            b.w("vector", c2)
            tv = b.op("vector", "tensor_mul", mrun[:], mrun[:], dt[:])
        mblk_tok = tv
        b.w("vector", l3)
        c1 = b.op("vector", "tensor_copy", wblk[0:64, 0:64], wsf[0:64, :])
        b.w("vector", c1)
        wblk_tok = b.op("vector", "tensor_copy", wblk[64:128, 64:128], wsf[64:128, :])
        b.w("vector", t.get("y3h1_dma"))
        zh = b.op("vector", "memset", t["y3h1"][:], 0.0)
        t["y3h1_zero"] = zh
        t["mblk_tok"] = mblk_tok
        t["wblk_tok"] = wblk_tok
        b.w("tensor", wblk_tok, lq, t.pop("qz", None), lk, lv)
        bfree = None
        for s in range(NB):
            b.w("tensor", bfree)
            mm = b.op("tensor", "matmul", bps[:, :], wblk[:],
                      qpad[:, s * TB:(s + 1) * TB], start=True, stop=True)
            b.w("vector", mm, t.get("ystp_dma") if s == 0 else None)
            bfree = b.op("vector", "tensor_copy", y3[:, s * TB:(s + 1) * TB], bps[:, :])
        qk = utile[:, 0:T]
        qk3 = qk.rearrange("p (c s) -> p c s", s=CC)
        y33 = y3.ap().rearrange("p (c s) -> p c s", s=CC)
        pt3 = ptmp.ap().rearrange("p (c s) -> p c s", s=CC)
        t["y33"] = y33
        qk_tok = None
        b.w("tensor", mblk_tok)
        for tau in range(CC):
            b.w("vector", qk_tok)
            tv = b.op("vector", "tensor_mul", qk, kpr[:], qpad[:, tau:tau + T])
            if tau > 0:
                b.w("vector", tv)
                tv = b.op("vector", "memset", qk3[:, :, CC - tau:CC], 0.0)
            qk_ready = tv
            for s in range(NB):
                b.w("tensor", qk_ready if s == 0 else None, bfree)
                mm = b.op("tensor", "matmul", bps[:, :],
                          mblk[:, tau * 128:(tau + 1) * 128],
                          qk[:, s * TB:(s + 1) * TB], start=True, stop=True)
                b.w("vector", mm, qk_tok)
                tv = b.op("vector", "tensor_mul", ptmp[:], vpr[:, s * TB:(s + 1) * TB],
                          bps[:, :])
                bfree = tv
                b.w("vector", tv)
                csl = slice(s * 32, (s + 1) * 32)
                tv = b.op("vector", "tensor_add", y33[:, csl, tau:CC],
                          y33[:, csl, tau:CC], pt3[:, :, 0:CC - tau])
                qk_tok = tv
        t["near_tok"] = qk_tok
        t["bfree"] = bfree
        vre, kre, qre, ptb = t["vre"], t["kre"], t["qre"], t["ptb"]
        rr_tok = None
        b.w("tensor", t.get("scanptb_free"))
        for ct in range(2):
            for lam in range(CC):
                for (srct, dstt) in ((vpr, vre), (kpr, kre), (qpad, qre)):
                    sl = srct.ap()[:, ct * 2048 + lam: ct * 2048 + lam + 16 * 127 + 1:16]
                    tt = b.op("tensor", "transpose", ptb[0:128, 0:128], sl, self.idb[:])
                    b.w("vector", tt)
                    rr_tok = b.op("vector", "tensor_copy",
                                  dstt[ct][:, lam * 128:(lam + 1) * 128], ptb[0:128, 0:128])
                    b.w("tensor", rr_tok)
        t["rr_tok"] = rr_tok

    def p2i_far_prep(self, t, p, hh):
        nc, b = self.nc, self.b
        lamN, gamN = self.inp["lamN"], self.inp["gamN"]
        reprows = self.scr["reprows"]
        h = 2 * p + hh
        drow, ebf = t["drow"], t["ebf"]
        dcol, est = t["dcol"][hh], t["est"]
        gbf = t["u"]
        bps = t["bps"]
        elast = t.get("exps_last")
        ebf_free = t.get("ebf_free") or []
        lamr = lamN[h].rearrange("i j -> () (i j)")
        for hf in range(2):
            b.w("scalar", t.get("drow_free") if hf == 0 else None)
            l4 = b.dma("scalar", drow[:], lamr[0:1, hf * 2048:(hf + 1) * 2048])
            b.w("scalar", l4)
            tv = b.op("scalar", "activation", drow[:], drow[:], AF.Sigmoid, scale=-1.0)
            b.w("scalar", tv)
            lnd_tok = b.op("scalar", "activation", drow[:], drow[:], AF.Ln)
            for s in range(4):
                b.w("tensor", lnd_tok if s == 0 else None, elast,
                    t.get("bfree") if s == 0 and hf == 0 else None,
                    *(ebf_free if s == 0 and hf == 0 else []))
                mm = b.op("tensor", "matmul", bps[0:17, 0:512], self.erow17[:],
                          drow[0:1, s * 512:(s + 1) * 512], start=True, stop=True)
                b.w("scalar", mm)
                elast = b.op("scalar", "activation",
                             ebf[0:17, hf * 2048 + s * 512:hf * 2048 + (s + 1) * 512],
                             bps[0:17, 0:512], AF.Exp)
        t["exps_last"] = elast
        drow_mm = b.tok("tensor")
        ebf_tok = elast
        b.w("gpsimd", ebf_tok, *t.get("rep_rds", []))
        re1 = b.dma("gpsimd", reprows[0, :, 0:4096], ebf[0:17, :])
        b.w("scalar", re1, t.get("est_free"))
        ed = b.dma("scalar", est[:, :],
                   reprows[0, 16:17, 0:4096].rearrange("() (a c) -> a c", c=128))
        b.w("scalar", ed)
        dct_tok = b.dmat("scalar", dcol[:, :], est[:, :])
        t["est_free"] = dct_tok
        t["dct_tok"] = dct_tok
        gamr = gamN[h].rearrange("i j -> () (i j)")
        tv = None
        for hf in range(2):
            b.w("scalar", drow_mm if hf == 0 else tv)
            l5 = b.dma("scalar", drow[:], gamr[0:1, hf * 2048:(hf + 1) * 2048])
            b.w("vector", l5, t.get("far_busy") if hf == 0 else None)
            tv = b.op("vector", "tensor_copy", gbf[0:1, hf * 2048:(hf + 1) * 2048], drow[:])
            b.w("vector", tv)
        t["drow_free"] = tv
        b.w("gpsimd", tv, t.get("far_busy"))
        rg = b.dma("gpsimd", reprows[1, 0:1, 0:4096], gbf[0:1, 0:4096])
        b.w("gpsimd", rg)
        l6 = b.dma("gpsimd", gbf[0:17, 0:4096], reprows[1, 0:1, 0:4096].to_broadcast([17, 4096]))
        b.w("vector", l6, re1, dct_tok)
        mtl_tok = b.op("vector", "tensor_mul", ebf[0:17, :], ebf[0:17, :], gbf[0:17, 0:4096])
        b.w("gpsimd", mtl_tok)
        re2 = b.dma("gpsimd", reprows[1, :, 0:4096], ebf[0:17, :])
        t["mtl_tok"] = mtl_tok
        t["re1"] = re1
        t["re2"] = re2
        t["ebf_free"] = [mtl_tok, dct_tok, re1, re2]

    def p2i_far_gscan(self, t, p, hh):
        nc, b = self.nc, self.b
        reprows = self.scr["reprows"]
        r0 = hh * 64
        utile, rept2, gacc = t["u"], t["rep"], t["gacc"]
        vre, kre = t["vre"], t["kre"]
        gtb2, phb2 = t["gtb"], t["phb"]
        dcol = t["dcol"][hh]
        rr_tok, mtl_tok, dct_tok = t["rr_tok"], t["mtl_tok"], t["dct_tok"]
        gtok = [None, None]
        rep_rd = [None, None]
        rep_user = [None, None]
        b.w("gpsimd", t["re1"])
        rep_rd[0] = b.dma("gpsimd", rept2[0][:], reprows[0, CC:CC + 1, 0:4096].to_broadcast([128, 4096]))
        for lam in range(CC):
            buf = lam % 2
            if lam + 1 < CC:
                e2 = CC - (lam + 1)
                b.w("gpsimd", rep_user[1 - buf])
                rep_rd[1 - buf] = b.dma("gpsimd", rept2[1 - buf][:],
                                        reprows[0, e2:e2 + 1, 0:4096].to_broadcast([128, 4096]))
            for ct in range(2):
                vbc = vre[ct][:, lam * 128 + r0:lam * 128 + r0 + 64].unsqueeze(2).to_broadcast([128, 64, 64])
                kbc = kre[ct][:, lam * 128 + r0:lam * 128 + r0 + 64].unsqueeze(1).to_broadcast([128, 64, 64])
                u4 = utile.ap().rearrange("p (i j) -> p i j", i=64)
                b.w("vector", rr_tok if lam == 0 and ct == 0 else None, gtok[1 - ct],
                    mtl_tok if lam == 0 and ct == 0 else None,
                    t.get("far_busy") if lam == 0 and ct == 0 else None)
                tv = b.op("vector", "tensor_mul", u4, vbc, kbc)
                b.w("vector", tv, rep_rd[buf] if ct == 0 else None)
                tv = b.op("vector", "tensor_mul", utile[:], utile[:], rept2[buf][:])
                b.w("vector", tv)
                if lam == 0:
                    tv = b.op("vector", "tensor_copy", gacc[ct][:], utile[:])
                else:
                    tv = b.op("vector", "tensor_add", gacc[ct][:], gacc[ct][:], utile[:])
                gtok[ct] = tv
            rep_user[buf] = gtok[1]
        t["gtok1"] = gtok[1]
        t["rep_rds"] = [x for x in rep_rd if x is not None]
        t["rep_user"] = rep_user
        banks = [t["pts"], t["ptb"]]
        sct = [t.get("sc_f0"), t.get("sc_f1")]
        cpfree = [t.get("cpf_0"), t.get("cpf_1")]
        otfree = [t.get("otf_0"), t.get("otf_1")]
        intok = [None, None]
        wbtok = [None, None]
        incp = [None, None]
        for s in range(34):
            bb = s % 2
            bk = banks[bb]
            if s < 32:
                b.w("tensor", cpfree[bb], incp[bb], gtok[1] if s == 0 else None,
                    rr_tok if s == 0 and bb == 1 else None)
                b.op("tensor", "transpose", bk[0:128, 0:128],
                     gacc[0][:, s * 128:(s + 1) * 128], self.idb[:])
                b.op("tensor", "transpose", bk[0:128, 128:256],
                     gacc[1][:, s * 128:(s + 1) * 128], self.idb[:])
                intok[bb] = b.tok("tensor")
            if s >= 2:
                b.w("tensor", sct[bb])
                b.op("tensor", "transpose", bk[0:128, 256:384],
                     phb2[bb][:, 0:128], self.idb[:])
                b.op("tensor", "transpose", bk[0:128, 384:512],
                     phb2[bb][:, 128:256], self.idb[:])
                otfree[bb] = b.tok("tensor")
            if s < 32:
                b.w("vector", intok[bb])
                b.op("vector", "tensor_copy", gtb2[bb][:, 0:128], bk[0:128, 0:128])
                incp[bb] = b.op("vector", "tensor_copy", gtb2[bb][:, 128:256],
                                bk[0:128, 128:256])
            if s >= 2:
                b.w("vector", otfree[bb])
                b.op("vector", "tensor_copy", gacc[0][:, (s - 2) * 128:(s - 1) * 128],
                     bk[0:128, 256:384])
                wbtok[bb] = b.op("vector", "tensor_copy",
                                 gacc[1][:, (s - 2) * 128:(s - 1) * 128],
                                 bk[0:128, 384:512])
                cpfree[bb] = wbtok[bb]
            if s < 32:
                b.w("vector", t.pop("phz", None) if s == 0 else None,
                    otfree[bb], incp[bb],
                    dct_tok if s == 0 else None)
                dc = dcol.ap()[:, s:s + 1].to_broadcast([128, 255])
                sv = b.op("vector", "tensor_tensor_scan", phb2[bb][:, 1:256], dc,
                          gtb2[bb][:, 0:255], 0.0, ALU.mult, ALU.add)
                sct[bb] = sv
        t["sc_f0"], t["sc_f1"] = sct
        t["cpf_0"], t["cpf_1"] = cpfree
        t["otf_0"], t["otf_1"] = otfree
        t["wb_last"] = wbtok[1]
        t["scanptb_free"] = otfree[1]

    def p2i_far_consume(self, t, p, hh):
        nc, b = self.nc, self.b
        reprows = self.scr["reprows"]
        r0 = hh * 64
        utile, rept2, gacc = t["u"], t["rep"], t["gacc"]
        qre = t["qre"]
        y3fb2, y3ft2 = t["y3fb"], t["y3ft"]
        y33 = t["y33"]
        phic = gacc
        wb_last = t["wb_last"]
        wb_last2 = None
        re2 = t["re2"]
        cons_tok = t["near_tok"]
        addt = [t.get("add_f0"), t.get("add_f1")]
        dmt = [t.get("dm_f0"), t.get("dm_f1")]
        rep_rd = [None, None]
        rep_user = list(t["rep_user"])
        b.w("gpsimd", re2, t["gtok1"])
        rep_rd[0] = b.dma("gpsimd", rept2[0][:], reprows[1, 0:1, 0:4096].to_broadcast([128, 4096]))
        pend = None
        prev_tree = None
        step = 0
        for lam in range(CC):
            buf = lam % 2
            if lam + 1 < CC:
                b.w("gpsimd", rep_user[1 - buf])
                rep_rd[1 - buf] = b.dma("gpsimd", rept2[1 - buf][:],
                                        reprows[1, lam + 1:lam + 2, 0:4096].to_broadcast([128, 4096]))
            for ct in range(2):
                bb = step % 2
                b.w("vector", rep_rd[buf] if ct == 0 else None,
                    wb_last if lam == 0 and ct == 0 else None,
                    wb_last2 if lam == 0 and ct == 0 else None, prev_tree)
                tv = b.op("vector", "tensor_mul", utile[:], phic[ct][:], rept2[buf][:])
                qbc = qre[ct][:, lam * 128 + r0:lam * 128 + r0 + 64].unsqueeze(1).to_broadcast([128, 64, 64])
                p4 = utile.ap().rearrange("p (i j) -> p i j", i=64)
                b.w("vector", tv)
                tv = b.op("vector", "tensor_mul", p4, p4, qbc)
                p3v = p4
                wdt = 32
                while wdt >= 1:
                    b.w("vector", tv, dmt[bb] if wdt == 1 else None)
                    if wdt == 1:
                        tv = b.op("vector", "tensor_add",
                                  y3fb2[bb][:, 0:64].unsqueeze(2),
                                  p3v[:, :, 0:1], p3v[:, :, 1:2])
                    else:
                        tv = b.op("vector", "tensor_add", p3v[:, :, 0:wdt],
                                  p3v[:, :, 0:wdt], p3v[:, :, wdt:2 * wdt])
                    wdt //= 2
                rep_user[buf] = tv
                prev_tree = tv
                if pend is not None:
                    ptok, pyap, pbb, pz = pend
                    b.w("vector", ptok, pz)
                    av = b.op("vector", "tensor_add", pyap, pyap,
                              y3ft2[pbb][0:64, 0:128].unsqueeze(2))
                    addt[pbb] = av
                    cons_tok = av
                b.w("sync", tv, addt[bb])
                dm = b.dmat("sync", y3ft2[bb][:, :], y3fb2[bb][:, :])
                dmt[bb] = dm
                if hh == 0:
                    yap = y33[0:64, ct * 128:(ct + 1) * 128, lam:lam + 1]
                else:
                    yap = t["y3h1"].ap().rearrange("p (c s) -> p c s", s=CC)[
                        :, ct * 128:(ct + 1) * 128, lam:lam + 1]
                pend = (dm, yap, bb,
                        t["y3h1_zero"] if hh == 1 and step == 0 else None)
                step += 1
        ptok, pyap, pbb, pz = pend
        b.w("vector", ptok, pz)
        av = b.op("vector", "tensor_add", pyap, pyap,
                  y3ft2[pbb][0:64, 0:128].unsqueeze(2))
        addt[pbb] = av
        cons_tok = av
        t["add_f0"], t["add_f1"] = addt
        t["dm_f0"], t["dm_f1"] = dmt
        t["near_tok"] = cons_tok
        t["far_busy"] = cons_tok

    def p2i_pair_end(self, t, p):
        nc, b = self.nc, self.b
        b.w("gpsimd", t["near_tok"])
        d1 = b.dma("gpsimd", self.scr["ystp"][p], t["y3"][:])
        b.w("gpsimd", d1)
        for cc in range(NB):
            d1 = b.dma("gpsimd", self.scr["ystp"][p, 64:128, cc * TB:(cc + 1) * TB],
                       t["y3h1"][:, cc * TB:(cc + 1) * TB], accum_op=ALU.add)
        t["prev_pair"] = t["near_tok"]
        t["ystp_dma"] = d1
        t["y3h1_dma"] = d1

    def p_mid(self, st):
        import os
        sk = set(os.environ.get("PM_SKIP", "").split(","))
        b = self.b
        b.w("tensor", self.ctok)
        t1 = self.p1i_alloc(st)
        t2 = self.p2i_alloc(st)
        t1["nrow"] = t2["qpad"][0:1, 0:T]
        t1["mrow"] = t2["kpr"][0:1, 0:T]
        q0, k0, v0 = self.scr["q0"], self.scr["k0"], self.scr["v0"]
        y0 = self.scr["ysm0"]
        p1on, faron = "p1" not in sk, "far" not in sk
        prepon = "prepa" not in sk
        if p1on and prepon:
            self.p1i_prep_a(t1, 0, q0, k0, v0)
            self.p1i_prep_b(t1, 0)
        for p in range(NP):
            b.w("sync", t2.get("prev_pair"), t2.get("ystp_dma"),
                t1["prep_b_done"], t1["mdma"])
            self.p2i_pair_start(t2, p)
            if faron and "prep" not in sk:
                self.p2i_far_prep(t2, p, 0)
            if p1on:
                if "main" not in sk:
                    self.p1i_main(t1, 2 * p)
                if prepon:
                    t1["nrow_guard"] = t2["near_tok"]
                    self.p1i_prep_a(t1, 2 * p + 1, q0, k0, v0)
            if faron and "gscan" not in sk:
                self.p2i_far_gscan(t2, p, 0)
            if p1on and prepon:
                self.p1i_prep_b(t1, 2 * p + 1)
                if "fin" not in sk:
                    self.p1i_fin(t1, 2 * p, y0)
            if faron and "consume" not in sk:
                self.p2i_far_consume(t2, p, 0)
            if faron and "prep" not in sk:
                self.p2i_far_prep(t2, p, 1)
            if p1on and "main" not in sk:
                self.p1i_main(t1, 2 * p + 1)
            if p1on and p + 1 < NP and prepon:
                t1["nrow_guard"] = t2["rr_tok"]
                self.p1i_prep_a(t1, 2 * p + 2, q0, k0, v0)
            if p + 1 < NP:
                self.p2i_pair_loads_params(t2, p + 1)
            if faron and "gscan" not in sk:
                self.p2i_far_gscan(t2, p, 1)
            if p1on:
                if p + 1 < NP and prepon:
                    self.p1i_prep_b(t1, 2 * p + 2)
                if "fin" not in sk:
                    self.p1i_fin(t1, 2 * p + 1, y0)
            if p + 1 < NP and os.environ.get("PM_NOPRE", "") != "1":
                self.p2i_pair_loads(t2, p + 1, t1)
            if faron and "consume" not in sk:
                self.p2i_far_consume(t2, p, 1)
            self.p2i_pair_end(t2, p)

    def p3(self, st):
        nc, b = self.nc, self.b
        en = st.enter_context
        wo, wot = self.load_weight_bf16(st, "outw", 8 * 8 * 128, "outw")
        ob = en(nc.sbuf_tensor("p3_ob", [128, 8], F32))
        obt = b.dma("sync", ob[:], self.inp["outb"])
        NBF = 3
        ar = [en(nc.sbuf_tensor(f"p3_ar{i}", [1, 2 * TB], F32)) for i in range(NBF)]
        ysmt = [en(nc.sbuf_tensor(f"p3_ysm{i}", [128, TB], BF16)) for i in range(NBF)]
        ystt = [en(nc.sbuf_tensor(f"p3_yst{i}", [128, TB], F32)) for i in range(NBF)]
        tb1 = [en(nc.sbuf_tensor(f"p3_tb1{i}", [128, TB], BF16)) for i in range(NBF)]
        yb = [[en(nc.sbuf_tensor(f"p3_yb{j}_{i}", [128, TB], BF16)) for i in range(NP)]
              for j in range(2)]
        xt = [en(nc.sbuf_tensor(f"p3_x{i}", [128, TB], F32)) for i in range(2)]
        ft = [en(nc.sbuf_tensor(f"p3_f{i}", [128, TB], BF16)) for i in range(2)]
        tf = [en(nc.sbuf_tensor(f"p3_tf{i}", [128, TB], F32)) for i in range(2)]
        x1t = [en(nc.sbuf_tensor(f"p3_x1{i}", [128, TB], F32)) for i in range(2)]
        par = [en(nc.psum_tensor(f"p3_par{i}", [128, 512], F32)) for i in range(2)]
        pp = [en(nc.psum_tensor(f"p3_pp{i}", [128, 512], F32)) for i in range(2)]
        b.w("tensor", wot, obt, self.ctok)
        x = self.inp["x"]
        bl_done = [None] * NBF
        par_free = [None, None]
        ybfree = [None, None]
        xfree = [None, None]
        x1d = [None, None]
        i = 0
        for blk in range(NB):
            t0 = blk * TB
            jb = blk % 2
            for p in range(NP):
                bi = i % NBF
                b.w("sync", bl_done[bi])
                la = b.dma("sync", ar[bi][0:1, 0:TB],
                           self.scr["alpha"][2 * p:2 * p + 1, t0:t0 + TB])
                lb = b.dma("sync", ar[bi][0:1, TB:2 * TB],
                           self.scr["alpha"][2 * p + 1:2 * p + 2, t0:t0 + TB])
                ls = b.dma("sync", ysmt[bi][:], self.scr["ysm0"][p, :, t0:t0 + TB])
                lt = b.dma("sync", ystt[bi][:], self.scr["ystp"][p, :, t0:t0 + TB])
                pi = i % 2
                b.w("tensor", la, lb, par_free[pi],
                    ybfree[jb] if p == 0 else None)
                b.op("tensor", "matmul", par[pi][0:64, :], self.ones_rowf[0:1, 0:64],
                     ar[bi][0:1, 0:TB], start=True, stop=True)
                am = b.op("tensor", "matmul", par[pi][64:128, :],
                          self.ones_rowf[0:1, 0:64],
                          ar[bi][0:1, TB:2 * TB], start=True, stop=True)
                b.w("vector", ls, lt, am, ybfree[jb] if p == 0 else None)
                t = b.op("vector", "tensor_sub", tb1[bi][:], ysmt[bi][:], ystt[bi][:])
                b.w("vector", t)
                t = b.op("vector", "tensor_mul", tb1[bi][:], tb1[bi][:], par[pi][:, :])
                par_free[pi] = t
                b.w("vector", t)
                t = b.op("vector", "tensor_add", yb[jb][p][:], ystt[bi][:], tb1[bi][:])
                bl_done[bi] = t
                i += 1
            yb_tok = b.tok("vector")
            ppf = [None, None]
            for m in range(KC):
                pi = m % 2
                b.w("sync", xfree[pi])
                lx = b.dma("sync", xt[pi][:], x[m * 128:(m + 1) * 128, t0:t0 + TB])
                lf = b.dma("sync", ft[pi][:], self.scr["ffn0"][m, :, t0:t0 + TB])
                b.w("tensor", ppf[pi], yb_tok if m == 0 else None)
                for k in range(KC):
                    b.op("tensor", "matmul", pp[pi][:, :],
                         wo[:, (m * 8 + k) * 128:(m * 8 + k) * 128 + 128],
                         yb[jb][k][:], start=(k == 0), stop=(k == KC - 1))
                mm = b.tok("tensor")
                if m == KC - 1:
                    ybfree[jb] = mm
                b.w("vector", mm, lf)
                t = b.op("vector", "scalar_tensor_tensor", tf[pi][:], ft[pi][:], 0.2,
                         pp[pi][:, :], ALU.mult, ALU.add)
                ppf[pi] = t
                b.w("vector", t, lx, x1d[pi])
                t = b.op("vector", "scalar_tensor_tensor", x1t[pi][:], tf[pi][:],
                         ob[:, m:m + 1], xt[pi][:], ALU.add, ALU.add)
                xfree[pi] = t
                b.w("gpsimd", t)
                x1d[pi] = b.dma("gpsimd", self.scr["x1"][m, :, t0:t0 + TB], x1t[pi][:])

    def p4a(self, st):
        nc, b = self.nc, self.b
        en = st.enter_context
        wq, t1 = self.load_weight_bf16(st, "w1q", 8 * 8 * 128, "w1q")
        wk, t2 = self.load_weight_bf16(st, "w1k", 8 * 8 * 128, "w1k")
        wv, t3 = self.load_weight_bf16(st, "w1v", 8 * 8 * 128, "w1v")
        x1 = self.scr["x1"]
        xt = [en(nc.sbuf_tensor(f"p4_x{i}", [128, TB], F32)) for i in range(KC)]
        sq = en(nc.sbuf_tensor("p4_sq", [128, TB], F32))
        rows = en(nc.sbuf_tensor("p4_rows", [1, 6 * TB], F32))
        xnb = [en(nc.sbuf_tensor(f"p4_xn{i}", [128, TB], BF16)) for i in range(KC)]
        cosT = en(nc.sbuf_tensor("p4_cos", [128, 32 * NT], F32))
        sinT = en(nc.sbuf_tensor("p4_sin", [128, 32 * NT], F32))
        ivf = en(nc.sbuf_tensor("p4_ivf", [1, 32], F32))
        trow = en(nc.sbuf_tensor("p4_trow", [1, 128], F32))
        ang = en(nc.sbuf_tensor("p4_ang", [128, 32], F32))
        angi = en(nc.sbuf_tensor("p4_angi", [128, 32], I32))
        angf = en(nc.sbuf_tensor("p4_angf", [128, 32], F32))
        qtm = en(nc.sbuf_tensor("p4_qtm", [128, 1024], F32))
        rt = [en(nc.sbuf_tensor(f"p4_rt{i}", [128, 512], F32)) for i in range(4)]
        rout = en(nc.sbuf_tensor("p4_rout", [128, 1024], BF16))
        qfmb = en(nc.sbuf_tensor("p4_qfmb", [128, NP * 128], BF16))
        kfmb = en(nc.sbuf_tensor("p4_kfmb", [128, NP * 128], BF16))
        ev = en(nc.sbuf_tensor("p4_ev", [128, TB], BF16))
        ps_sx = en(nc.psum_tensor("p4_psx", [128, 512], F32))
        ps_sx2 = en(nc.psum_tensor("p4_psx2", [128, 512], F32))
        ps_r = en(nc.psum_tensor("p4_psr", [128, 512], F32))
        ps_m = en(nc.psum_tensor("p4_psm", [128, 512], F32))
        pp = [en(nc.psum_tensor(f"p4_pp{i}", [128, 512], F32)) for i in range(2)]
        ptb = en(nc.psum_tensor("p4_ptb", [128, 1024], BF16))
        PI = 3.141592653589793
        b.w("tensor", t1, t2, t3, self.ctok)
        ROW = lambda i: rows[0:1, i * TB:(i + 1) * TB]
        t = b.op("scalar", "activation", ivf[:], self.irow32[:], AF.Exp,
                 scale=-np.log(10000.0) / 32.0)
        ivt = t
        tab_tok = None
        prev_mm = None
        for tt in range(NT):
            b.w("gpsimd", prev_mm)
            g = b.op("gpsimd", "iota", trow[:], pattern=[[1, 128]], base=tt * 128,
                     channel_multiplier=0, allow_small_or_imprecise_dtypes=True)
            b.w("tensor", g, ivt, tab_tok)
            mm = b.op("tensor", "matmul", ps_sx[0:128, 0:32],
                      trow[:], ivf[:], start=True, stop=True)
            prev_mm = mm
            MAGIC = 12582912.0
            b.w("vector", mm)
            t = b.op("vector", "tensor_scalar", ang[:], ps_sx[0:128, 0:32],
                     1.0 / (2 * PI), None, ALU.mult)
            b.w("vector", t)
            t = b.op("vector", "tensor_scalar", angf[:], ang[:], MAGIC, MAGIC, ALU.add, ALU.subtract)
            b.w("vector", t)
            t = b.op("vector", "tensor_sub", angf[:], ang[:], angf[:])
            b.w("scalar", t)
            s1 = b.op("scalar", "activation", sinT[:, tt * 32:(tt + 1) * 32], angf[:],
                      AF.Sin, scale=2 * PI)
            b.w("vector", s1)
            t = b.op("vector", "tensor_scalar", ang[:], ang[:], -1.0, 0.25, ALU.mult, ALU.add)
            b.w("vector", t)
            t = b.op("vector", "tensor_scalar", angf[:], ang[:], MAGIC, MAGIC, ALU.add, ALU.subtract)
            b.w("vector", t)
            t = b.op("vector", "tensor_sub", angf[:], ang[:], angf[:])
            b.w("scalar", t)
            s2 = b.op("scalar", "activation", cosT[:, tt * 32:(tt + 1) * 32], angf[:],
                      AF.Sin, scale=2 * PI)
            tab_tok = s2
            b.w("tensor", s2)
        prev_xn = None
        prev_v = None
        for blk in range(NB):
            t0 = blk * TB
            b.w("sync", prev_xn)
            lt = [b.dma("sync", xt[i][:], x1[i, :, t0:t0 + TB]) for i in range(KC)]
            sq_free = None
            for i in range(KC):
                b.w("scalar", lt[i], sq_free)
                qt = b.op("scalar", "activation", sq[:], xt[i][:], AF.Square)
                b.w("tensor", qt)
                m2 = b.op("tensor", "matmul", ps_sx2[0:1, :], self.ones_col[:], sq[:],
                          start=(i == 0), stop=(i == KC - 1))
                sq_free = m2
            for i in range(KC):
                b.w("tensor", lt[i] if i == 0 else None)
                mt = b.op("tensor", "matmul", ps_sx[0:1, :], self.ones_col[:], xt[i][:],
                          start=(i == 0), stop=(i == KC - 1))
            b.w("vector", mt)
            t = b.op("vector", "tensor_scalar", ROW(0), ps_sx[0:1, :], 1.0 / D, None, ALU.mult)
            b.w("vector", t, m2)
            t = b.op("vector", "tensor_scalar", ROW(1), ps_sx2[0:1, :], 1.0 / D, None, ALU.mult)
            b.w("vector", t)
            t = b.op("vector", "tensor_mul", ROW(2), ROW(0), ROW(0))
            b.w("vector", t)
            t = b.op("vector", "tensor_sub", ROW(2), ROW(1), ROW(2))
            b.w("scalar", t)
            t = b.op("scalar", "activation", ROW(3), ROW(2), AF.Sqrt, bias=self.epsc[0:1, 0:1])
            b.w("vector", t)
            t = b.op("vector", "reciprocal", ROW(4), ROW(3))
            b.w("vector", t)
            rstd_tok = t
            t = b.op("vector", "tensor_mul", ROW(5), ROW(0), ROW(4))
            mur_tok = t
            b.w("tensor", rstd_tok, prev_xn)
            rep_r = b.op("tensor", "matmul", ps_r[:, :], self.ones_rowf[:], ROW(4),
                         start=True, stop=True)
            b.w("tensor", mur_tok)
            rep_m = b.op("tensor", "matmul", ps_m[:, :], self.ones_rowf[:], ROW(5),
                         start=True, stop=True)
            xn_toks = []
            for i in range(KC):
                b.w("vector", rep_r if i == 0 else None)
                t = b.op("vector", "tensor_mul", sq[:], xt[i][:], ps_r[:, :])
                b.w("vector", t, rep_m if i == 0 else None)
                t = b.op("vector", "tensor_sub", xnb[i][:], sq[:], ps_m[:, :])
                b.w("vector", t)
                xn_toks.append(t)
                b.w("gpsimd", t)
                b.dma("gpsimd", self.scr["xn1"][i, :, t0:t0 + TB], xnb[i][:])
            prev_xn = xn_toks[-1]
            ppf = [None, None]
            ev_dma = None
            for m in range(KC):
                pi = m % 2
                b.w("tensor", ppf[pi], xn_toks[-1] if m == 0 else None)
                for k in range(KC):
                    b.op("tensor", "matmul", pp[pi][:, :],
                         wv[:, (m * 8 + k) * 128:(m * 8 + k) * 128 + 128],
                         xnb[k][:], start=(k == 0), stop=(k == KC - 1))
                mm = b.tok("tensor")
                b.w("vector", mm, ev_dma, prev_v)
                t = b.op("vector", "tensor_copy", ev[:], pp[pi][:, :])
                ppf[pi] = t
                b.w("gpsimd", t)
                ev_dma = b.dma("gpsimd", self.scr["v1"][m, :, t0:t0 + TB], ev[:])
                prev_v = None
            for sub in range(4):
                ttile = blk * 4 + sub
                for (wmat, dst, buf, scaled) in ((wq, self.scr["q1"], qfmb, False),
                                                 (wk, self.scr["k1"], kfmb, True)):
                    for nb2 in range(2):
                        pi = nb2 % 2
                        b.w("tensor", ppf[pi])
                        for k in range(KC):
                            b.op("tensor", "matmul", pp[pi][:, :],
                                 xnb[k][:, sub * 128:(sub + 1) * 128],
                                 wmat[:, k * 1024 + nb2 * 512:k * 1024 + nb2 * 512 + 512],
                                 start=(k == 0), stop=(k == KC - 1))
                        mm = b.tok("tensor")
                        b.w("vector", mm)
                        t = b.op("vector", "tensor_copy", qtm[:, nb2 * 512:(nb2 + 1) * 512], pp[pi][:, :])
                        ppf[pi] = t
                    qv = qtm.ap().rearrange("p (h e r) -> p h e r", h=16, e=2)
                    ev_ = qv[:, :, 0, :]
                    ov_ = qv[:, :, 1, :]
                    cb = cosT[:, ttile * 32:(ttile + 1) * 32].unsqueeze(1).to_broadcast([128, 16, 32])
                    sb = sinT[:, ttile * 32:(ttile + 1) * 32].unsqueeze(1).to_broadcast([128, 16, 32])
                    r3 = [rt[j].ap().rearrange("p (h r) -> p h r", h=16) for j in range(4)]
                    b.w("vector", b.tok("vector"))
                    t = b.op("vector", "tensor_mul", r3[0], ev_, cb)
                    ta = b.op("vector", "tensor_mul", r3[1], ov_, sb)
                    tbb = b.op("vector", "tensor_mul", r3[2], ev_, sb)
                    tc = b.op("vector", "tensor_mul", r3[3], ov_, cb)
                    rv = rout.ap().rearrange("p (h e r) -> p h e r", h=16, e=2)
                    b.w("vector", ta)
                    t = b.op("vector", "tensor_sub", rv[:, :, 0, :], r3[0], r3[1])
                    b.w("vector", tc)
                    t = b.op("vector", "tensor_add", rv[:, :, 1, :], r3[2], r3[3])
                    rope_tok = t
                    b.w("tensor", rope_tok)
                    for pr in range(NP):
                        tt2 = b.op("tensor", "transpose", ptb[0:128, 0:128],
                                   rout[:, pr * 128:(pr + 1) * 128], self.idb[:])
                        b.w("vector", tt2)
                        if scaled:
                            cpt = b.op("vector", "tensor_scalar", buf[:, pr * 128:(pr + 1) * 128],
                                       ptb[0:128, 0:128], 0.125, None, ALU.mult)
                        else:
                            cpt = b.op("vector", "tensor_copy", buf[:, pr * 128:(pr + 1) * 128],
                                       ptb[0:128, 0:128])
                        b.w("tensor", cpt)
                    done = b.tok("vector")
                    b.w("gpsimd", done)
                    for pr in range(NP):
                        b.dma("gpsimd", dst[pr, :, ttile * 128:(ttile + 1) * 128],
                              buf[:, pr * 128:(pr + 1) * 128])
                    b.w("vector", b.dtok("gpsimd"))

    def p4b(self, st):
        nc, b = self.nc, self.b
        en = st.enter_context
        b.w("tensor", self.ctok)
        t = self.p1i_alloc(st, tag="b1", natt=4, npsc=4)
        t["evac_eng"] = "vector"
        t["q65s"] = [t["q65"], en(nc.sbuf_tensor("b_q65b", [65, T], BF16))]
        t["k65s"] = [t["k65"], en(nc.sbuf_tensor("b_k65b", [65, T], BF16))]
        t["vhs"] = [t["vh"], en(nc.sbuf_tensor("b_vhb", [64, T], BF16))]
        t["nrow"] = en(nc.sbuf_tensor("b_nrow", [1, T], BF16))
        t["mrow"] = en(nc.sbuf_tensor("b_mrow", [1, T], BF16))
        ko = b.op("gpsimd", "memset", t["k65s"][1][64:65, :], 1.0)
        b.w("gpsimd", ko)
        q1, k1, v1 = self.scr["q1"], self.scr["k1"], self.scr["v1"]
        y1 = self.scr["ysm1"]

        def bank(h):
            par = h % 2
            t["q65"], t["k65"], t["vh"] = (t["q65s"][par], t["k65s"][par],
                                           t["vhs"][par])
        prevs = [None, None]
        bank(0)
        self.p1i_prep_a(t, 0, q1, k1, v1)
        self.p1i_prep_b(t, 0)
        for h in range(H):
            bank(h)
            self.p1i_main(t, h)
            prevs[h % 2] = t["prev_tensor"]
            if h + 1 < H:
                bank(h + 1)
                t["load_guard"] = prevs[(h + 1) % 2]
                self.p1i_prep_a(t, h + 1, q1, k1, v1)
                self.p1i_prep_b(t, h + 1)
            bank(h)
            self.p1i_fin(t, h, y1)

    def p4c(self, st):
        nc, b = self.nc, self.b
        en = st.enter_context
        wo, wot = self.load_weight_bf16(st, "w1o", 8 * 8 * 128, "w1o")
        yt = [en(nc.sbuf_tensor(f"p4c_y{i}", [128, TB], BF16)) for i in range(KC)]
        xt = en(nc.sbuf_tensor("p4c_x", [128, TB], F32))
        ot = en(nc.sbuf_tensor("p4c_o", [128, TB], F32))
        pp = [en(nc.psum_tensor(f"p4c_pp{i}", [128, 512], F32)) for i in range(2)]
        b.w("tensor", wot, self.ctok)
        for blk in range(NB):
            t0 = blk * TB
            b.barrier()
            lt = [b.dma("sync", yt[i][:], self.scr["ysm1"][i, :, t0:t0 + TB]) for i in range(KC)]
            ppf = [None, None]
            for m in range(KC):
                pi = m % 2
                lx = b.dma("sync", xt[:], self.scr["x1"][m, :, t0:t0 + TB])
                b.w("tensor", ppf[pi], lt[-1] if m == 0 else None)
                for k in range(KC):
                    b.op("tensor", "matmul", pp[pi][:, :],
                         wo[:, (m * 8 + k) * 128:(m * 8 + k) * 128 + 128],
                         yt[k][:], start=(k == 0), stop=(k == KC - 1))
                mm = b.tok("tensor")
                b.w("vector", mm, lx)
                t = b.op("vector", "tensor_add", ot[:], xt[:], pp[pi][:, :])
                ppf[pi] = t
                b.w("gpsimd", t)
                od = b.dma("gpsimd", self.scr["x2p"][m, :, t0:t0 + TB], ot[:])
                b.w("vector", od)
                b.w("sync", t)

    def p5(self, st):
        nc, b = self.nc, self.b
        en = st.enter_context
        HW = T // 2
        NBH = HW // 512
        scl = en(nc.sbuf_tensor("p5_scl", [128, 32 * 4 + 8], F32))
        S1, S2, SH1, SH2, S3 = 0, 32, 64, 96, 128
        lsc = [b.dma("sync", scl[:, S1:S1 + 32], self.inp["f1s1"]),
               b.dma("sync", scl[:, S2:S2 + 32], self.inp["f1s2"]),
               b.dma("sync", scl[:, SH1:SH1 + 32], self.inp["f1sh1"]),
               b.dma("sync", scl[:, SH2:SH2 + 32], self.inp["f1sh2"]),
               b.dma("sync", scl[:, S3:S3 + 8], self.inp["f1s3"])]
        ht = [en(nc.sbuf_tensor(f"p5_h{i}", [128, HW], BF16)) for i in range(32)]
        at = en(nc.sbuf_tensor("p5_a", [128, HW], BF16))
        bt = en(nc.sbuf_tensor("p5_b", [128, HW], BF16))
        wst = en(nc.sbuf_tensor("p5_wst", [128, 2048], F32))
        whb2 = [en(nc.sbuf_tensor(f"p5_whb{i}", [128, 4096], BF16)) for i in range(2)]
        pp = [en(nc.psum_tensor(f"p5_pp{i}", [128, 512], F32)) for i in range(2)]
        b.w("tensor", lsc[-1], self.ctok)

        last_cast = [None]

        def stream_w(name, m, ncols, wf_tok, buf):
            whb = whb2[buf]
            cw = last_cast[0]
            for off in range(0, ncols, 2048):
                wdt = min(2048, ncols - off)
                b.w("sync", cw)
                lw = b.dma("sync", wst[:, 0:wdt],
                           self.inp[name][:, m * ncols + off:m * ncols + off + wdt])
                b.w("gpsimd", lw, wf_tok if off == 0 else None)
                cw = b.op("gpsimd", "tensor_copy", whb[:, off:off + wdt], wst[:, 0:wdt])
                b.w("gpsimd", cw)
            last_cast[0] = cw
            return cw

        for half in range(2):
            t0 = half * HW
            b.barrier()
            hstk = ExitStack()
            xnb = [hstk.enter_context(nc.sbuf_tensor(f"p5_xn{i}_{half}", [128, HW], BF16))
                   for i in range(KC)]
            lt = [b.dma("sync", xnb[i][:], self.scr["xn1"][i, :, t0:t0 + HW]) for i in range(KC)]
            ppf = [None, None]
            ab_tok = None
            sched = []
            for m in range(32):
                sched += [("f1w1", m), ("f1w2", m)]
            wtoks = [None, None]
            casts = [None, None]
            casts[0] = stream_w(sched[0][0], sched[0][1], 8 * 128, None, 0)
            for si, (wnm, m) in enumerate(sched):
                buf = si % 2
                if si + 1 < len(sched):
                    n_nm, n_m = sched[si + 1]
                    casts[1 - buf] = stream_w(n_nm, n_m, 8 * 128, wtoks[1 - buf], 1 - buf)
                whb = whb2[buf]
                for nb2 in range(NBH):
                    pi = nb2 % 2
                    csl = slice(nb2 * 512, (nb2 + 1) * 512)
                    b.w("tensor", ppf[pi], casts[buf] if nb2 == 0 else None,
                        lt[-1] if si == 0 and nb2 == 0 else None)
                    for k in range(KC):
                        b.op("tensor", "matmul", pp[pi][:, :], whb[:, k * 128:(k + 1) * 128],
                             xnb[k][:, csl], start=(k == 0), stop=(k == KC - 1))
                    mm = b.tok("tensor")
                    if wnm == "f1w1":
                        b.w("scalar", mm, ab_tok)
                        t = b.op("scalar", "activation", at[:, csl], pp[pi][:, :], AF.Silu,
                                 scale=scl[:, S1 + m:S1 + m + 1])
                        ppf[pi] = t
                        ab_tok = t
                    else:
                        b.w("vector", mm, ab_tok)
                        t = b.op("vector", "tensor_scalar", bt[:, csl], pp[pi][:, :],
                                 scl[:, S2 + m:S2 + m + 1], None, ALU.mult)
                        ppf[pi] = t
                        b.w("vector", t)
                        t = b.op("vector", "tensor_mul", ht[m][:, csl], at[:, csl], bt[:, csl])
                        ab_tok = t
                wtoks[buf] = b.tok("tensor")
            b.barrier()
            hstk.close()
            hstk2 = ExitStack()
            xt = hstk2.enter_context(nc.sbuf_tensor(f"p5_x_{half}", [128, HW], F32))
            ot = hstk2.enter_context(nc.sbuf_tensor(f"p5_o_{half}", [128, HW], F32))
            for (wname, sbase, dst_scr) in (("f1h1", SH1, self.scr["h1a"]),
                                            ("f1h2", SH2, self.scr["h1b"])):
                gel_tok = ab_tok
                st_dma = None
                wtoks = [None, None]
                casts = [None, None]
                casts[0] = stream_w(wname, 0, 32 * 128, wtoks[0], 0)
                for m in range(32):
                    buf = m % 2
                    if m + 1 < 32:
                        casts[1 - buf] = stream_w(wname, m + 1, 32 * 128, wtoks[1 - buf], 1 - buf)
                    whb = whb2[buf]
                    for nb2 in range(NBH):
                        pi = nb2 % 2
                        csl = slice(nb2 * 512, (nb2 + 1) * 512)
                        b.w("tensor", ppf[pi], casts[buf] if nb2 == 0 else None,
                            gel_tok if m == 0 and nb2 == 0 else None)
                        for k in range(32):
                            b.op("tensor", "matmul", pp[pi][:, :], whb[:, k * 128:(k + 1) * 128],
                                 ht[k][:, csl], start=(k == 0), stop=(k == 31))
                        mm = b.tok("tensor")
                        b.w("scalar", mm, st_dma if nb2 == 0 else None)
                        t = b.op("scalar", "activation", at[:, csl], pp[pi][:, :], AF.Gelu,
                                 scale=scl[:, sbase + m:sbase + m + 1])
                        ppf[pi] = t
                    b.w("gpsimd", b.tok("scalar"))
                    st_dma = b.dma("gpsimd", dst_scr[m, :, t0:t0 + HW], at[:])
                    wtoks[buf] = b.tok("tensor")
                b.w("sync", b.dtok("gpsimd"))
                rl = [b.dma("sync", ht[i][:], dst_scr[i, :, t0:t0 + HW]) for i in range(32)]
                b.w("tensor", rl[-1])
                b.w("vector", rl[-1])
                b.w("scalar", rl[-1])
                ab_tok = None
            wtoks = [None, None]
            casts = [None, None]
            casts[0] = stream_w("f1w3", 0, 32 * 128, wtoks[0], 0)
            for m in range(KC):
                buf = m % 2
                lx = b.dma("sync", xt[:], self.scr["x2p"][m, :, t0:t0 + HW])
                if m + 1 < KC:
                    casts[1 - buf] = stream_w("f1w3", m + 1, 32 * 128, wtoks[1 - buf], 1 - buf)
                whb = whb2[buf]
                for nb2 in range(NBH):
                    pi = nb2 % 2
                    csl = slice(nb2 * 512, (nb2 + 1) * 512)
                    b.w("tensor", ppf[pi], casts[buf] if nb2 == 0 else None)
                    for k in range(32):
                        b.op("tensor", "matmul", pp[pi][:, :], whb[:, k * 128:(k + 1) * 128],
                             ht[k][:, csl], start=(k == 0), stop=(k == 31))
                    mm = b.tok("tensor")
                    b.w("vector", mm, lx if nb2 == 0 else None)
                    t = b.op("vector", "scalar_tensor_tensor", ot[:, csl], pp[pi][:, :],
                             scl[:, S3 + m:S3 + m + 1], xt[:, csl], ALU.mult, ALU.add)
                    ppf[pi] = t
                vtok = b.tok("vector")
                b.w("gpsimd", vtok)
                od = b.dma("gpsimd", self.outx[m * 128:(m + 1) * 128, t0:t0 + HW], ot[:])
                b.w("vector", od)
                b.w("sync", vtok)
                wtoks[buf] = b.tok("tensor")
            b.barrier()
            hstk2.close()

    def finish(self):
        b = self.b
        for eng in ("sync", "scalar", "gpsimd"):
            if b.semc[b.dma_sem[eng]]:
                b.w(eng, b.dtok(eng))



def run(inputs, phases, dbg_out=(), cores=None, sim=False, dbg_in=None):
    xin = np.asarray(inputs["L0_attn_x"])
    w = prep_weights(inputs)
    core_ids = list(range(8)) if cores is None else cores
    dbg_in = dbg_in or {}
    kk = Kern(phases=phases, dbg_out=dbg_out, dbg_in=tuple(dbg_in.keys()))
    nc = kk.build()
    in_maps = []
    for c in core_ids:
        m = dict(w)
        m["x"] = np.ascontiguousarray(xin[c].T, dtype=np.float32)
        for k2, v2 in dbg_in.items():
            m[k2] = v2
        in_maps.append(m)
    if sim:
        from concourse.bass_interp import CoreSim
        nc.finalize()
        s = CoreSim(nc)
        for k2, v2 in in_maps[0].items():
            s.tensor(k2)[:] = v2
        s.simulate()
        names = ["xout"] + list(dbg_out)
        res = [{nm: np.array(s.tensor(nm)) for nm in names}]
    else:
        res = run_bass_kernel_spmd(nc, in_maps, core_ids=core_ids).results
    kk.close()
    return res


def kernel(**inputs):
    res = run(inputs, phases=PHASES_ALL)
    out = np.stack([r["xout"].T for r in res], axis=0)
    return np.ascontiguousarray(out, dtype=np.float32)


# version 27
# speedup vs baseline: 1.0377x; 1.0377x over previous
import numpy as np
import ml_dtypes
from contextlib import ExitStack
import concourse.bass as bass
import concourse.mybir as mybir
from concourse.bass_utils import run_bass_kernel_spmd

F32 = mybir.dt.float32
BF16 = mybir.dt.bfloat16
I32 = mybir.dt.int32
ALU = mybir.AluOpType
AF = mybir.ActivationFunctionType
BF = ml_dtypes.bfloat16

T, D, H, DH, DFF = 4096, 1024, 16, 64, 4096
NT = T // 128
KC = D // 128
NP = H // 2
CC = 16
NCH = T // CC
TB = 512
NB = T // TB
EPS = 1e-5
NEG = -1e30
BETA = 0.9

ENGINES = ("sync", "scalar", "vector", "tensor", "gpsimd")


class Tok:
    __slots__ = ("s", "v")

    def __init__(self, s, v):
        self.s = s
        self.v = v


class BB:

    def __init__(self, nc, stack):
        self.nc = nc
        self.q = {e: [] for e in ENGINES}
        self.semh = {}
        self.semc = {}
        for name in ("dS", "dA", "dG", "oT", "oV", "oA", "oP", "oS"):
            self.semh[name] = stack.enter_context(nc.semaphore(name))
            self.semc[name] = 0
        self.dma_sem = {"sync": "dS", "scalar": "dA", "gpsimd": "dG"}
        self.chain = {"tensor": "oT", "vector": "oV", "scalar": "oA",
                      "gpsimd": "oP", "sync": "oS"}
        self._batch = {e: [] for e in ENGINES}

    def _close_batch(self, eng):
        bl = self._batch[eng]
        if bl:
            end = bl[-1].v
            for t in bl:
                t.v = end
            self._batch[eng] = []

    def w(self, eng, *toks):
        self._close_batch(eng)
        for t in toks:
            if t is None:
                continue
            h = self.semh[t.s]
            self.q[eng].append(lambda e, h=h, t=t: e.wait_ge(h, t.v))

    def _dma_common(self, eng, fn):
        s = self.dma_sem[eng]
        if not self._batch[eng] and self.semc[s] > 0:
            h = self.semh[s]
            v = self.semc[s]
            self.q[eng].append(lambda e, h=h, v=v: e.wait_ge(h, v))
        self.semc[s] += 16
        t = Tok(s, self.semc[s])
        self._batch[eng].append(t)
        self.q[eng].append(fn)
        return t

    def dma(self, eng, out, in_, **kw):
        h = self.semh[self.dma_sem[eng]]
        return self._dma_common(eng, lambda e, out=out, in_=in_, kw=kw, h=h:
                                e.dma_start(out=out, in_=in_, **kw).then_inc(h, 16))

    def dmat(self, eng, out, in_):
        h = self.semh[self.dma_sem[eng]]
        return self._dma_common(eng, lambda e, out=out, in_=in_, h=h:
                                e.dma_start_transpose(out, in_).then_inc(h, 16))

    def op(self, eng, meth, *args, **kw):
        self._close_batch(eng)
        s = self.chain[eng]
        self.semc[s] += 1
        t = Tok(s, self.semc[s])
        h = self.semh[s]
        lp = kw.pop("_lp", None)

        def fn(e, meth=meth, args=args, kw=kw, h=h, lp=lp):
            if lp:
                with self.nc.allow_low_precision(reason=lp):
                    getattr(e, meth)(*args, **kw).then_inc(h, 1)
            else:
                getattr(e, meth)(*args, **kw).then_inc(h, 1)
        self.q[eng].append(fn)
        return t

    def tok(self, eng):
        s = self.chain[eng]
        return Tok(s, self.semc[s])

    def barrier(self):
        for e in ENGINES:
            self._close_batch(e)
        toks = [Tok(s, c) for s, c in self.semc.items() if c > 0]
        for e in ENGINES:
            self.w(e, *toks)

    def dtok(self, eng):
        s = self.dma_sem[eng]
        return Tok(s, self.semc[s])

    def build_block(self, stack):
        block = stack.enter_context(self.nc.Block())
        q = self.q

        @block.sync
        def _(e):
            for fn in q["sync"]:
                fn(e)

        @block.scalar
        def _(e):
            for fn in q["scalar"]:
                fn(e)

        @block.vector
        def _(e):
            for fn in q["vector"]:
                fn(e)

        @block.tensor
        def _(e):
            for fn in q["tensor"]:
                fn(e)

        @block.gpsimd
        def _(e):
            for fn in q["gpsimd"]:
                fn(e)



def sbuf_image(w):
    K, M = w.shape
    t = w.reshape(K // 128, 128, M // 128, 128).transpose(1, 2, 0, 3)
    return np.ascontiguousarray(t.reshape(128, (M // 128) * (K // 128) * 128), dtype=np.float32)


def sbuf_image_tm(w):
    K, M = w.shape
    t = w.reshape(K // 128, 128, M // 128, 128).transpose(1, 0, 2, 3)
    return np.ascontiguousarray(t.reshape(128, (M // 128) * (K // 128) * 128), dtype=np.float32)


def col_tiles(bv):
    return np.ascontiguousarray(bv.reshape(-1, 128).T, dtype=np.float32)


def rope_perm():
    return np.arange(D).reshape(H, 32, 2).transpose(0, 2, 1).reshape(-1)


def prep_weights(inputs):
    g = {}
    f32 = lambda a: np.ascontiguousarray(np.asarray(a), dtype=np.float32)
    g["wqkv"] = sbuf_image(f32(inputs["L0_attn_Wqkv_w"]))
    g["bqkv"] = col_tiles(f32(inputs["L0_attn_Wqkv_b"]))
    g["gw1"] = sbuf_image(f32(inputs["L0_attn_gate_w1"]))
    g["gb1"] = col_tiles(f32(inputs["L0_attn_gate_b1"]))
    gw2 = f32(inputs["L0_attn_gate_w2"])
    g["gw2"] = np.ascontiguousarray(gw2.reshape(4, 128, 16).transpose(1, 0, 2).reshape(128, 64))
    g["gb2"] = f32(inputs["L0_attn_gate_b2"]).reshape(16, 1)
    g["outw"] = sbuf_image(f32(inputs["L0_attn_out_w"]))
    g["outb"] = col_tiles(f32(inputs["L0_attn_out_b"]))
    g["lamT"] = f32(np.transpose(inputs["L0_attn_Lambda_raw"], (0, 2, 1)))
    g["gamT"] = f32(np.transpose(inputs["L0_attn_Gamma"], (0, 2, 1)))
    g["lamN"] = f32(inputs["L0_attn_Lambda_raw"])
    g["gamN"] = f32(inputs["L0_attn_Gamma"])
    g["wstT"] = f32(np.transpose(inputs["L0_attn_W_static"], (0, 2, 1)))
    g["wfc"] = sbuf_image(f32(inputs["L0_ffn_W_fc"]).T)
    g["wproj"] = sbuf_image(f32(inputs["L0_ffn_W_proj"]).T)
    perm = rope_perm()
    g["w1q"] = sbuf_image_tm(f32(inputs["L1_attn_Wq"]).T[:, perm])
    g["w1k"] = sbuf_image_tm(f32(inputs["L1_attn_Wk"]).T[:, perm])
    g["w1v"] = sbuf_image(f32(inputs["L1_attn_Wv"]).T)
    g["w1o"] = sbuf_image(f32(inputs["L1_attn_Wo"]).T)
    g["f1w1"] = sbuf_image(f32(inputs["L1_ffn_w1_w"]).T)
    g["f1s1"] = col_tiles(f32(inputs["L1_ffn_w1_s"]))
    g["f1w2"] = sbuf_image(f32(inputs["L1_ffn_w2_w"]).T)
    g["f1s2"] = col_tiles(f32(inputs["L1_ffn_w2_s"]))
    g["f1h1"] = sbuf_image(f32(inputs["L1_ffn_h1_w"]).T)
    g["f1sh1"] = col_tiles(f32(inputs["L1_ffn_h1_s"]))
    g["f1h2"] = sbuf_image(f32(inputs["L1_ffn_h2_w"]).T)
    g["f1sh2"] = col_tiles(f32(inputs["L1_ffn_h2_s"]))
    g["f1w3"] = sbuf_image(f32(inputs["L1_ffn_w3_w"]).T)
    g["f1s3"] = col_tiles(f32(inputs["L1_ffn_w3_s"]))
    return g


WSHAPE = {
    "wqkv": [128, 24 * 8 * 128], "bqkv": [128, 24],
    "gw1": [128, 4 * 8 * 128], "gb1": [128, 4],
    "gw2": [128, 64], "gb2": [16, 1],
    "outw": [128, 8 * 8 * 128], "outb": [128, 8],
    "lamT": [H, 64, 64], "gamT": [H, 64, 64],
    "lamN": [H, 64, 64], "gamN": [H, 64, 64], "wstT": [H, 64, 64],
    "wfc": [128, 32 * 8 * 128], "wproj": [128, 8 * 32 * 128],
    "w1q": [128, 8 * 8 * 128], "w1k": [128, 8 * 8 * 128],
    "w1v": [128, 8 * 8 * 128], "w1o": [128, 8 * 8 * 128],
    "f1w1": [128, 32 * 8 * 128], "f1s1": [128, 32],
    "f1w2": [128, 32 * 8 * 128], "f1s2": [128, 32],
    "f1h1": [128, 32 * 32 * 128], "f1sh1": [128, 32],
    "f1h2": [128, 32 * 32 * 128], "f1sh2": [128, 32],
    "f1w3": [128, 8 * 32 * 128], "f1s3": [128, 8],
}

SCRATCH = {
    "xn0": ([KC, 128, T], BF16),
    "q0": ([NP, 128, T], BF16),
    "k0": ([NP, 128, T], BF16),
    "v0": ([NP, 128, T], BF16),
    "alpha": ([16, T], F32),
    "ffn0": ([KC, 128, T], BF16),
    "ysm0": ([NP, 128, T], BF16),
    "ystp": ([NP, 128, T], F32),
    "x1": ([KC, 128, T], F32),
    "xn1": ([KC, 128, T], BF16),
    "q1": ([NP, 128, T], BF16),
    "k1": ([NP, 128, T], BF16),
    "v1": ([NP, 128, T], BF16),
    "ysm1": ([NP, 128, T], BF16),
    "x2p": ([KC, 128, T], F32),
    "h1a": ([32, 128, T], BF16),
    "h1b": ([32, 128, T], BF16),
    "reprows": ([2, 17, 8192], BF16),
}


PHASES_ALL = ("p0a", "p0b", "p_mid", "p3", "p4a", "p4b", "p4c", "p5")


class Kern:
    def __init__(self, phases=PHASES_ALL, dbg_out=(), dbg_in=()):
        self.phases = phases
        self.dbg_out = set(dbg_out)
        self.dbg_in = set(dbg_in)
        self.nc = bass.Bass("TRN2", target_bir_lowering=False)

    def build(self):
        nc = self.nc
        self.stack = ExitStack()
        st = self.stack
        self.inp = {"x": nc.dram_tensor("x", [D, T], F32, kind="ExternalInput").ap()}
        for name, shape in WSHAPE.items():
            self.inp[name] = nc.dram_tensor(name, shape, F32, kind="ExternalInput").ap()
        self.outx = nc.dram_tensor("xout", [D, T], F32, kind="ExternalOutput").ap()
        self.scr = {}
        for name, (shape, dt) in SCRATCH.items():
            if name in self.dbg_in:
                h = nc.dram_tensor(name, shape, dt, kind="ExternalInput")
            elif name in self.dbg_out:
                h = nc.dram_tensor(name, shape, dt, kind="ExternalOutput")
            else:
                h = nc.dram_tensor(name, shape, dt)
            self.scr[name] = h.ap()

        b = BB(nc, st)
        self.b = b
        self.consts(st)
        for ph in self.phases:
            b.barrier()
            with ExitStack() as pst:
                getattr(self, ph)(pst)
        self.finish()
        b.build_block(st)
        self.stack.close()
        return nc

    def close(self):
        pass

    def consts(self, st):
        nc, b = self.nc, self.b
        en = st.enter_context
        self.idf = en(nc.sbuf_tensor("c_idf", [128, 128], F32))
        self.idb = en(nc.sbuf_tensor("c_idb", [128, 128], BF16))
        self.ones_col = en(nc.sbuf_tensor("c_onesc", [128, 1], F32))
        self.ones_rowf = en(nc.sbuf_tensor("c_onesrf", [1, 128], F32))
        self.ones_colb = en(nc.sbuf_tensor("c_onescb", [128, 1], BF16))
        self.masks = en(nc.sbuf_tensor("c_masks", [128, 4 * TB], BF16))
        self.epsc = en(nc.sbuf_tensor("c_eps", [1, 1], F32))
        self.erow17 = en(nc.sbuf_tensor("c_e17", [1, 17], F32))
        self.irow32 = en(nc.sbuf_tensor("c_i32", [1, 32], F32))
        self.idb8 = en(nc.sbuf_tensor("c_idb8", [128, 128], BF16))
        g = "gpsimd"
        t = b.op(g, "memset", self.idf[:], 1.0); b.w(g, t)
        t = b.op(g, "affine_select", out=self.idf[:], in_=self.idf[:],
                 compare_op=ALU.is_equal, fill=0.0, base=0,
                 pattern=[[-1, 128]], channel_multiplier=1); b.w(g, t)
        t = b.op(g, "memset", self.idb[:], 1.0); b.w(g, t)
        t = b.op(g, "affine_select", out=self.idb[:], in_=self.idb[:],
                 compare_op=ALU.is_equal, fill=0.0, base=0,
                 pattern=[[-1, 128]], channel_multiplier=1); b.w(g, t)
        t = b.op(g, "memset", self.ones_col[:], 1.0); b.w(g, t)
        t = b.op(g, "memset", self.ones_colb[:], 1.0); b.w(g, t)
        t = b.op(g, "memset", self.ones_rowf[:], 1.0); b.w(g, t)
        t = b.op(g, "memset", self.epsc[:], EPS); b.w(g, t)
        t = b.op(g, "iota", self.erow17[:], pattern=[[1, 17]], base=0,
                 channel_multiplier=0, allow_small_or_imprecise_dtypes=True); b.w(g, t)
        t = b.op(g, "iota", self.irow32[:], pattern=[[1, 32]], base=0,
                 channel_multiplier=0, allow_small_or_imprecise_dtypes=True); b.w(g, t)
        t = b.op(g, "memset", self.idb8[:], 0.125); b.w(g, t)
        t = b.op(g, "affine_select", out=self.idb8[:], in_=self.idb8[:],
                 compare_op=ALU.is_equal, fill=0.0, base=0,
                 pattern=[[-1, 128]], channel_multiplier=1); b.w(g, t)
        for r in range(4):
            sl = self.masks[:, r * TB:(r + 1) * TB]
            t = b.op(g, "memset", sl, 0.0); b.w(g, t)
            t = b.op(g, "affine_select", out=sl, in_=sl, compare_op=ALU.is_ge,
                     fill=NEG, base=-128 * r, pattern=[[1, TB]],
                     channel_multiplier=-1); b.w(g, t)
        self.ctok = b.tok(g)

    def load_weight_bf16(self, st, name, cols, tag, stage_cols=8192):
        nc, b = self.nc, self.b
        dst = st.enter_context(nc.sbuf_tensor(f"w_{tag}", [128, cols], BF16))
        rtoks = []
        with ExitStack() as tmp_st:
            stg = tmp_st.enter_context(nc.sbuf_tensor(f"wst_{tag}", [128, min(stage_cols, cols)], F32))
            off = 0
            prev_cast = None
            while off < cols:
                w = min(stage_cols, cols - off)
                lt = b.dma("sync", stg[:, 0:w], self.inp[name][:, off:off + w])
                b.w("vector", lt)
                if prev_cast is not None:
                    pass
                ct = b.op("vector", "tensor_copy", dst[:, off:off + w], stg[:, 0:w])
                b.w("vector", ct)
                b.w("sync", ct)
                prev_cast = ct
                off += w
            rtoks.append(prev_cast)
        return dst, rtoks[-1]

    def p0a(self, st):
        nc, b = self.nc, self.b
        en = st.enter_context
        x = self.inp["x"]
        wq, wqt = self.load_weight_bf16(st, "wqkv", 24 * 8 * 128, "qkv")
        gw1, gw1t = self.load_weight_bf16(st, "gw1", 4 * 8 * 128, "gw1")
        gw2, gw2t = self.load_weight_bf16(st, "gw2", 64, "gw2")
        bq = en(nc.sbuf_tensor("p0_bq", [128, 24], F32))
        gb1 = en(nc.sbuf_tensor("p0_gb1", [128, 4], F32))
        gb2 = en(nc.sbuf_tensor("p0_gb2", [16, 1], F32))
        bt1 = b.dma("sync", bq[:], self.inp["bqkv"])
        bt2 = b.dma("sync", gb1[:], self.inp["gb1"])
        bt3 = b.dma("sync", gb2[:], self.inp["gb2"])

        xt = [en(nc.sbuf_tensor(f"p0_x{i}", [128, TB], F32)) for i in range(KC)]
        sq = en(nc.sbuf_tensor("p0_sq", [128, TB], F32))
        rows = en(nc.sbuf_tensor("p0_rows", [1, 6 * TB], F32))
        xnb = [en(nc.sbuf_tensor(f"p0_xn{i}", [128, TB], BF16)) for i in range(KC)]
        gt = [en(nc.sbuf_tensor(f"p0_g{i}", [128, TB], BF16)) for i in range(4)]
        evb = [en(nc.sbuf_tensor(f"p0_ev{i}", [128, TB], BF16)) for i in range(3)]
        arow = en(nc.sbuf_tensor("p0_arow", [16, TB], F32))
        ps_sx = en(nc.psum_tensor("p0_psx", [128, 512], F32))
        ps_sx2 = en(nc.psum_tensor("p0_psx2", [128, 512], F32))
        ps_r = en(nc.psum_tensor("p0_psr", [128, 512], F32))
        ps_m = en(nc.psum_tensor("p0_psm", [128, 512], F32))
        pp = [en(nc.psum_tensor(f"p0_pp{i}", [128, 512], F32)) for i in range(2)]
        ps_a = en(nc.psum_tensor("p0_psa", [128, 512], F32))

        b.w("tensor", self.ctok, wqt, gw1t, gw2t, bt1, bt2, bt3)
        ROW = lambda i: rows[0:1, i * TB:(i + 1) * TB]

        ev_dma = [None, None, None]
        prev_xn_tok = None
        prev_gate_tok = None
        alpha_dma = None
        pp_free = [None, None]
        for blk in range(NB):
            t0 = blk * TB
            b.w("sync", prev_xn_tok)
            lt = [b.dma("sync", xt[i][:], x[i * 128:(i + 1) * 128, t0:t0 + TB])
                  for i in range(KC)]
            sq_free = None
            for i in range(KC):
                b.w("scalar", lt[i], sq_free)
                qt = b.op("scalar", "activation", sq[:], xt[i][:], AF.Square)
                b.w("tensor", qt)
                m2 = b.op("tensor", "matmul", ps_sx2[0:1, :], self.ones_col[:], sq[:],
                          start=(i == 0), stop=(i == KC - 1))
                sq_free = m2
            for i in range(KC):
                b.w("tensor", lt[i] if i == 0 else None)
                mt = b.op("tensor", "matmul", ps_sx[0:1, :], self.ones_col[:], xt[i][:],
                          start=(i == 0), stop=(i == KC - 1))
            sx_tok, sx2_tok = mt, m2
            b.w("vector", sx_tok)
            t = b.op("vector", "tensor_scalar", ROW(0), ps_sx[0:1, :], 1.0 / D, None, ALU.mult)
            b.w("vector", t, sx2_tok)
            t = b.op("vector", "tensor_scalar", ROW(1), ps_sx2[0:1, :], 1.0 / D, None, ALU.mult)
            b.w("vector", t)
            t = b.op("vector", "tensor_mul", ROW(2), ROW(0), ROW(0))
            b.w("vector", t)
            t = b.op("vector", "tensor_sub", ROW(2), ROW(1), ROW(2))
            b.w("scalar", t)
            t = b.op("scalar", "activation", ROW(3), ROW(2), AF.Sqrt, bias=self.epsc[0:1, 0:1])
            b.w("vector", t)
            t = b.op("vector", "reciprocal", ROW(4), ROW(3))
            b.w("vector", t)
            rstd_tok = t
            t = b.op("vector", "tensor_mul", ROW(5), ROW(0), ROW(4))
            mur_tok = t
            b.w("tensor", rstd_tok, prev_xn_tok)
            rep_r = b.op("tensor", "matmul", ps_r[:, :], self.ones_rowf[:], ROW(4),
                         start=True, stop=True)
            b.w("tensor", mur_tok)
            rep_m = b.op("tensor", "matmul", ps_m[:, :], self.ones_rowf[:], ROW(5),
                         start=True, stop=True)
            xn_toks = []
            for i in range(KC):
                b.w("vector", rep_r if i == 0 else None)
                t = b.op("vector", "tensor_mul", sq[:], xt[i][:], ps_r[:, :])
                b.w("vector", t, rep_m if i == 0 else None)
                t = b.op("vector", "tensor_sub", xnb[i][:], sq[:], ps_m[:, :])
                b.w("vector", t)
                xn_toks.append(t)
                b.w("gpsimd", t)
                b.dma("gpsimd", self.scr["xn0"][i, :, t0:t0 + TB], xnb[i][:])
            prev_xn_tok = xn_toks[-1]

            for m in range(24):
                pidx = m % 2
                b.w("tensor", pp_free[pidx])
                for k in range(KC):
                    b.w("tensor", xn_toks[k] if m == 0 else None)
                    b.op("tensor", "matmul", pp[pidx][:, :],
                         wq[:, (m * 8 + k) * 128:(m * 8 + k) * 128 + 128],
                         xnb[k][:], start=(k == 0), stop=(k == KC - 1))
                mm_tok = b.tok("tensor")
                j = m % 3
                bias = bq[:, m:m + 1]
                b.w("vector", mm_tok, ev_dma[j])
                if 8 <= m < 16:
                    t = b.op("vector", "tensor_scalar", evb[j][:], pp[pidx][:, :],
                             bias, 0.125, ALU.add, ALU.mult)
                else:
                    t = b.op("vector", "tensor_scalar", evb[j][:], pp[pidx][:, :],
                             bias, None, ALU.add)
                pp_free[pidx] = t
                b.w("gpsimd", t)
                dst = (self.scr["q0"] if m < 8 else self.scr["k0"] if m < 16 else self.scr["v0"])
                ev_dma[j] = b.dma("gpsimd", dst[m % 8, :, t0:t0 + TB], evb[j][:])

            for m in range(4):
                pidx = m % 2
                b.w("tensor", pp_free[pidx])
                for k in range(KC):
                    b.op("tensor", "matmul", pp[pidx][:, :],
                         gw1[:, (m * 8 + k) * 128:(m * 8 + k) * 128 + 128],
                         xnb[k][:], start=(k == 0), stop=(k == KC - 1))
                mm_tok = b.tok("tensor")
                b.w("scalar", mm_tok, prev_gate_tok if m < 2 else None)
                t = b.op("scalar", "activation", gt[m][:], pp[pidx][:, :], AF.Gelu,
                         bias=gb1[:, m:m + 1])
                pp_free[pidx] = t
            g_toks = b.tok("scalar")
            b.w("tensor", g_toks)
            for k in range(4):
                b.op("tensor", "matmul", ps_a[0:16, :], gw2[:, k * 16:(k + 1) * 16],
                     gt[k][:], start=(k == 0), stop=(k == 3))
            amm = b.tok("tensor")
            b.w("scalar", amm, alpha_dma)
            t = b.op("scalar", "activation", arow[:], ps_a[0:16, :], AF.Sigmoid,
                     bias=gb2[:])
            prev_gate_tok = t
            b.w("scalar", t)
            alpha_dma = b.dma("scalar", self.scr["alpha"][:, t0:t0 + TB], arow[:])

    def p0b(self, st):
        nc, b = self.nc, self.b
        en = st.enter_context
        wfc, t1 = self.load_weight_bf16(st, "wfc", 32 * 8 * 128, "wfc")
        wpj, t2 = self.load_weight_bf16(st, "wproj", 8 * 32 * 128, "wproj")
        xnb = [en(nc.sbuf_tensor(f"pb_xn{i}", [128, TB], BF16)) for i in range(KC)]
        ht = [en(nc.sbuf_tensor(f"pb_h{i}", [128, TB], BF16)) for i in range(32)]
        rt = en(nc.sbuf_tensor("pb_r", [128, TB], BF16))
        mem = en(nc.sbuf_tensor("pb_mem", [128, TB], F32))
        spk = en(nc.sbuf_tensor("pb_spk", [128, TB], F32))
        acc = en(nc.sbuf_tensor("pb_acc", [128, TB], F32))
        wv = en(nc.sbuf_tensor("pb_w", [128, TB], F32))
        sb = en(nc.sbuf_tensor("pb_sb", [128, TB], BF16))
        pp = [en(nc.psum_tensor(f"pb_pp{i}", [128, 512], F32)) for i in range(2)]

        b.w("tensor", t1, t2)
        prev_blk_tok = None
        sb_dma = None
        for blk in range(NB):
            t0 = blk * TB
            b.w("sync", prev_blk_tok)
            lt = [b.dma("sync", xnb[i][:], self.scr["xn0"][i, :, t0:t0 + TB])
                  for i in range(KC)]
            pp_free = [None, None]
            h_toks = []
            for m in range(32):
                pidx = m % 2
                b.w("tensor", pp_free[pidx])
                for k in range(KC):
                    b.w("tensor", lt[k] if m == 0 else None)
                    b.op("tensor", "matmul", pp[pidx][:, :],
                         wfc[:, (m * 8 + k) * 128:(m * 8 + k) * 128 + 128],
                         xnb[k][:], start=(k == 0), stop=(k == KC - 1))
                mm = b.tok("tensor")
                b.w("scalar", mm)
                tr = b.op("scalar", "activation", rt[:], pp[pidx][:, :], AF.Relu)
                pp_free[pidx] = tr
                b.w("vector", tr)
                th = b.op("vector", "tensor_mul", ht[m][:], rt[:], rt[:])
                b.w("scalar", th)
                h_toks.append(th)
            pp_free = [None, None]
            snn_last = None
            for m in range(KC):
                pidx = m % 2
                b.w("tensor", pp_free[pidx], h_toks[-1] if m == 0 else None)
                for k in range(32):
                    b.op("tensor", "matmul", pp[pidx][:, :],
                         wpj[:, (m * 32 + k) * 128:(m * 32 + k) * 128 + 128],
                         ht[k][:], start=(k == 0), stop=(k == 31))
                mm = b.tok("tensor")
                ps = pp[pidx]
                BKS = (0.24419428096993973, 0.46396913384288541, 0.68618592952553059,
                       0.90596078239847633, 1.0839784132255623)
                b.w("vector", mm, snn_last)
                t = b.op("vector", "tensor_scalar", acc[:], ps[:, :], BKS[0], None, ALU.is_gt)
                for bk in BKS[1:]:
                    b.w("vector", t)
                    t = b.op("vector", "scalar_tensor_tensor", acc[:], ps[:, :], bk,
                             acc[:], ALU.is_gt, ALU.add)
                pp_free[pidx] = t
                b.w("vector", sb_dma, t)
                t = b.op("vector", "tensor_copy", sb[:], acc[:])
                snn_last = t
                b.w("gpsimd", t)
                sb_dma = b.dma("gpsimd", self.scr["ffn0"][m, :, t0:t0 + TB], sb[:])
            prev_blk_tok = snn_last

    def p1_attn(self, st, qsrc, ksrc, vsrc, ydst, tag="a"):
        nc, b = self.nc, self.b
        en = st.enter_context
        q65 = en(nc.sbuf_tensor(f"p1{tag}_q65", [65, T], BF16))
        k65 = en(nc.sbuf_tensor(f"p1{tag}_k65", [65, T], BF16))
        vh = en(nc.sbuf_tensor(f"p1{tag}_vh", [64, T], BF16))
        sqt = en(nc.sbuf_tensor(f"p1{tag}_sqt", [64, T], F32))
        nrow = en(nc.sbuf_tensor(f"p1{tag}_nrow", [1, 2 * T + 8], F32))
        mrow = en(nc.sbuf_tensor(f"p1{tag}_mrow", [1, T], BF16))
        v65 = en(nc.sbuf_tensor(f"p1{tag}_v65", [128, 65 * NT], BF16))
        att = [en(nc.sbuf_tensor(f"p1{tag}_at{i}", [128, TB], BF16)) for i in range(2)]
        rcol = en(nc.sbuf_tensor(f"p1{tag}_rcol", [128, 1], F32))
        ytm = en(nc.sbuf_tensor(f"p1{tag}_ytm", [128, 64], BF16))
        ysm = en(nc.sbuf_tensor(f"p1{tag}_ysm", [64, T], BF16))
        psc = [en(nc.psum_tensor(f"p1{tag}_psc{i}", [128, 512], F32)) for i in range(2)]
        pv = [en(nc.psum_tensor(f"p1{tag}_pv{i}", [128, 512], F32)) for i in range(4)]
        ptb = en(nc.psum_tensor(f"p1{tag}_ptb", [128, 1024], BF16))
        psn = en(nc.psum_tensor(f"p1{tag}_psn", [128, 512], F32))

        QN, KN, MK = 0, T, 2 * T
        b.w("tensor", self.ctok)
        ysm_dma = None
        prev_head_tensor = None
        psc_free = [None, None]
        att_free = [None, None]
        for h in range(H):
            pair, half = h // 2, h % 2
            r0 = half * 64
            b.w("sync", prev_head_tensor)
            lq = b.dma("sync", q65[0:64, :], qsrc[pair, r0:r0 + 64, :])
            lk = b.dma("sync", k65[0:64, :], ksrc[pair, r0:r0 + 64, :])
            lv = b.dma("sync", vh[:], vsrc[pair, r0:r0 + 64, :])
            b.w("gpsimd", prev_head_tensor)
            v65r = v65.ap().rearrange("p (n c) -> p n c", c=65)
            ones_tok = b.op("gpsimd", "memset", v65r[:, :, 64:65], 1.0)
            kone = b.op("gpsimd", "memset", k65[64:65, :], 1.0)
            def normrow(src_ap, base, ld):
                b.w("scalar", ld)
                tq = b.op("scalar", "activation", sqt[:], src_ap, AF.Square)
                last = None
                for s in range(NB):
                    b.w("tensor", tq if s == 0 else None, last)
                    mm = b.op("tensor", "matmul", psn[0:1, 0:512], self.ones_col[0:64, :],
                              sqt[:, s * TB:(s + 1) * TB], start=True, stop=True)
                    b.w("scalar", mm)
                    last = b.op("scalar", "activation",
                                nrow[0:1, base + s * TB:base + (s + 1) * TB],
                                psn[0:1, 0:512], AF.Sqrt)
                return last
            t = normrow(q65[0:64, :], QN, lq)
            t = normrow(k65[0:64, :], KN, lk)
            b.w("vector", t)
            t = b.op("vector", "reduce_max", nrow[0:1, MK:MK + 1], nrow[0:1, KN:KN + T],
                     axis=mybir.AxisListType.X)
            b.w("vector", t)
            t = b.op("vector", "tensor_scalar", mrow[0:1, :], nrow[0:1, QN:QN + T],
                     nrow[0:1, MK:MK + 1], -1.0, ALU.mult, ALU.mult)
            b.w("gpsimd", t, lq)
            mdma = b.dma("gpsimd", q65[64:65, :], mrow[0:1, :])
            b.w("tensor", lv, ones_tok)
            ev_prev = None
            for kt in range(NT):
                tt = b.op("tensor", "transpose", ptb[0:128, 0:64],
                          vh[:, kt * 128:(kt + 1) * 128], self.idb[0:64, 0:64])
                b.w("vector", tt, ones_tok if kt == 0 else None)
                ev = b.op("vector", "tensor_copy", v65[:, kt * 65:kt * 65 + 64],
                          ptb[0:128, 0:64])
                b.w("tensor", ev)
            vtm_tok = b.tok("vector")
            b.w("tensor", vtm_tok, mdma, kone)
            b.w("vector", ysm_dma)
            for qb in range(NB):
                nk = 4 * qb + 4
                q_ap = q65[:, qb * TB:(qb + 1) * TB]

                def scores(kt):
                    pi = kt % 2
                    diag = kt >= 4 * qb
                    b.w("tensor", psc_free[pi])
                    b.op("tensor", "matmul", psc[pi][:, :], k65[:, kt * 128:(kt + 1) * 128],
                         q_ap, start=True, stop=not diag)
                    if diag:
                        r = kt - 4 * qb
                        b.op("tensor", "matmul", psc[pi][:, :], self.idb[:],
                             self.masks[:, r * TB:(r + 1) * TB], start=False, stop=True)
                    return b.tok("tensor")

                sc_tok = scores(0)
                for kt in range(nk):
                    pi = kt % 2
                    b.w("scalar", sc_tok, att_free[pi])
                    ex = b.op("scalar", "activation", att[pi][:], psc[pi][:, :], AF.Exp)
                    psc_free[pi] = ex
                    if kt + 1 < nk:
                        sc_tok = scores(kt + 1)
                    b.w("tensor", ex)
                    for m in range(4):
                        b.op("tensor", "matmul", pv[m][0:128, 0:65],
                             att[pi][:, m * 128:(m + 1) * 128],
                             v65[:, kt * 65:(kt + 1) * 65],
                             start=(kt == 0), stop=(kt == nk - 1))
                    att_free[pi] = b.tok("tensor")
                for m in range(4):
                    pv_tok = b.tok("tensor")
                    b.w("vector", pv_tok)
                    t = b.op("vector", "reciprocal", rcol[:], pv[m][0:128, 64:65])
                    b.w("vector", t)
                    t = b.op("vector", "tensor_scalar", ytm[:], pv[m][0:128, 0:64],
                             rcol[:], None, ALU.mult)
                    b.w("tensor", t)
                    tt = b.op("tensor", "transpose", ptb[0:64, 128:256], ytm[:], self.idb[:])
                    b.w("vector", tt)
                    t = b.op("vector", "tensor_copy",
                             ysm[:, qb * TB + m * 128:qb * TB + (m + 1) * 128],
                             ptb[0:64, 128:256])
                    b.w("tensor", t)
            fin = b.tok("vector")
            b.w("gpsimd", fin, ysm_dma)
            ysm_dma = b.dma("gpsimd", ydst[pair, r0:r0 + 64, :], ysm[:])
            prev_head_tensor = b.tok("tensor")

    def p2(self, st):
        nc, b = self.nc, self.b
        en = st.enter_context
        NEAR_T = CC
        dt = en(nc.sbuf_tensor("p2_dt", [128, 64], F32))
        gmt = en(nc.sbuf_tensor("p2_gmt", [128, 64], F32))
        mrun = en(nc.sbuf_tensor("p2_mrun", [128, 64], F32))
        mblk = en(nc.sbuf_tensor("p2_mblk", [128, CC * 128], BF16))
        wblk = en(nc.sbuf_tensor("p2_wblk", [128, 128], BF16))
        wsf = en(nc.sbuf_tensor("p2_wsf", [128, 64], F32))
        drow = en(nc.sbuf_tensor("p2_drow", [1, 8192], F32))
        ebf = en(nc.sbuf_tensor("p2_ebf", [32, 8192], BF16))
        dct = en(nc.sbuf_tensor("p2_dct", [128, 64 * 32], BF16))
        qpad = en(nc.sbuf_tensor("p2_qpad", [128, T + CC], BF16))
        kpr = en(nc.sbuf_tensor("p2_kpr", [128, T], BF16))
        vpr = en(nc.sbuf_tensor("p2_vpr", [128, T], BF16))
        y3 = en(nc.sbuf_tensor("p2_y3", [128, T], F32))
        utile = en(nc.sbuf_tensor("p2_u", [128, 8192], BF16))
        psit = utile
        lnd = drow
        gbf = utile
        rept2 = [en(nc.sbuf_tensor(f"p2_rep{i}", [128, 8192], BF16)) for i in range(2)]
        gacc = [en(nc.sbuf_tensor(f"p2_ga{i}", [128, 8192], BF16)) for i in range(2)]
        vre = [en(nc.sbuf_tensor(f"p2_vre{i}", [128, 2048], BF16)) for i in range(2)]
        kre = [en(nc.sbuf_tensor(f"p2_kre{i}", [128, 2048], BF16)) for i in range(2)]
        gtb = en(nc.sbuf_tensor("p2_gtb", [128, 256], BF16))
        phb = en(nc.sbuf_tensor("p2_phb", [128, 256], BF16))
        ptmp = en(nc.sbuf_tensor("p2_ptmp", [128, 512], BF16))
        y3f = en(nc.sbuf_tensor("p2_y3f", [128, 128], F32))
        ybf = kpr
        bps = [en(nc.psum_tensor(f"p2_bps{i}", [128, 512], F32)) for i in range(2)]
        ptb = en(nc.psum_tensor("p2_ptb", [128, 1024], BF16))
        psE = en(nc.psum_tensor("p2_psE", [128, 512], F32))
        ptf = en(nc.psum_tensor("p2_ptf", [128, 512], F32))

        lamT, gamT, wstT = self.inp["lamT"], self.inp["gamT"], self.inp["wstT"]
        lamN, gamN = self.inp["lamN"], self.inp["gamN"]
        reprows = self.scr["reprows"]
        mz = b.op("vector", "memset", mblk[:], 0.0)
        wz = b.op("vector", "memset", wblk[:], 0.0)
        ez = b.op("vector", "memset", ebf[:], 0.0)
        b.w("vector", mz, wz, ez)
        b.w("tensor", self.ctok)

        for p in range(NP):
            b.barrier()
            l1 = b.dma("sync", dt[:], lamT[2 * p:2 * p + 2].rearrange("h j i -> (h j) i"))
            l2 = b.dma("sync", gmt[:], gamT[2 * p:2 * p + 2].rearrange("h j i -> (h j) i"))
            l3 = b.dma("sync", wsf[:], wstT[2 * p:2 * p + 2].rearrange("h j i -> (h j) i"))
            l4 = b.dma("sync", drow[:], lamN[2 * p:2 * p + 2].rearrange("h i j -> () (h i j)"))
            lq = b.dma("sync", qpad[:, 0:T], self.scr["q0"][p])
            lk = b.dma("sync", kpr[:], self.scr["k0"][p])
            lv = b.dma("sync", vpr[:], self.scr["v0"][p])
            qz = b.op("vector", "memset", qpad[:, T:T + CC], 0.0)
            b.w("scalar", l1)
            t = b.op("scalar", "activation", dt[:], dt[:], AF.Sigmoid, scale=-1.0)
            dt_tok = t
            b.w("vector", l2, dt_tok)
            t = b.op("vector", "tensor_copy", mrun[:], gmt[:])
            for tau in range(CC):
                b.w("vector", t)
                c1 = b.op("vector", "tensor_copy", mblk[0:64, tau * 128:tau * 128 + 64], mrun[0:64, :])
                b.w("vector", c1)
                c2 = b.op("vector", "tensor_copy", mblk[64:128, tau * 128 + 64:tau * 128 + 128], mrun[64:128, :])
                b.w("vector", c2)
                t = b.op("vector", "tensor_mul", mrun[:], mrun[:], dt[:])
            mblk_tok = t
            b.w("vector", l3)
            c1 = b.op("vector", "tensor_copy", wblk[0:64, 0:64], wsf[0:64, :])
            b.w("vector", c1)
            c2 = b.op("vector", "tensor_copy", wblk[64:128, 64:128], wsf[64:128, :])
            wblk_tok = c2
            b.w("scalar", l4, dt_tok)
            t = b.op("scalar", "activation", drow[:], drow[:], AF.Sigmoid, scale=-1.0)
            b.w("scalar", t)
            t = b.op("scalar", "activation", lnd[:], drow[:], AF.Ln)
            lnd_tok = t
            elast = None
            for s in range(16):
                b.w("tensor", lnd_tok if s == 0 else None, elast)
                mm = b.op("tensor", "matmul", psE[0:17, 0:512], self.erow17[:],
                          lnd[0:1, s * 512:(s + 1) * 512], start=True, stop=True)
                b.w("scalar", mm)
                elast = b.op("scalar", "activation", ebf[0:17, s * 512:(s + 1) * 512],
                             psE[0:17, 0:512], AF.Exp)
            ebf_tok = elast
            b.w("gpsimd", ebf_tok)
            re1 = b.dma("gpsimd", reprows[0], ebf[0:17, :])
            b.w("scalar", ebf_tok)
            for s in range(64):
                b.dmat("scalar", dct[:, s * 32:(s + 1) * 32], ebf[0:32, s * 128:(s + 1) * 128])
            dct_tok = b.dtok("scalar")
            b.w("sync", b.tok("scalar"))
            l5 = b.dma("sync", drow[:], gamN[2 * p:2 * p + 2].rearrange("h i j -> () (h i j)"))
            b.w("vector", l5)
            t = b.op("vector", "tensor_copy", gbf[0:1, 0:8192], drow[:])
            b.w("gpsimd", t)
            rg = b.dma("gpsimd", reprows[1, 0:1, :], gbf[0:1, 0:8192])
            b.w("sync", rg)
            l6 = b.dma("sync", gbf[0:17, 0:8192], reprows[1, 0:1, :].to_broadcast([17, 8192]))
            b.w("vector", l6, re1, dct_tok)
            mtl_tok = b.op("vector", "tensor_mul", ebf[0:17, :], ebf[0:17, :], gbf[0:17, 0:8192])
            b.w("gpsimd", mtl_tok)
            re2 = b.dma("gpsimd", reprows[1], ebf[0:17, :])
            b.w("tensor", wblk_tok, lq, qz, lk, lv)
            bfree = [None, None]
            for s in range(NB):
                pi = s % 2
                b.w("tensor", bfree[pi])
                mm = b.op("tensor", "matmul", bps[pi][:, :], wblk[:],
                          qpad[:, s * TB:(s + 1) * TB], start=True, stop=True)
                b.w("vector", mm)
                ev = b.op("vector", "tensor_copy", y3[:, s * TB:(s + 1) * TB], bps[pi][:, :])
                bfree[pi] = ev
            qk = utile[:, 0:T]
            qk3 = qk.rearrange("p (c s) -> p c s", s=CC)
            y33 = y3.ap().rearrange("p (c s) -> p c s", s=CC)
            pt3 = ptmp.ap().rearrange("p (c s) -> p c s", s=CC)
            qk_tok = None
            b.w("tensor", mblk_tok)
            b.w("vector", re2)
            for tau in range(NEAR_T):
                b.w("vector", qk_tok)
                t = b.op("vector", "tensor_mul", qk, kpr[:], qpad[:, tau:tau + T])
                if tau > 0:
                    b.w("vector", t)
                    t = b.op("vector", "memset", qk3[:, :, CC - tau:CC], 0.0)
                qk_ready = t
                for s in range(NB):
                    pi = s % 2
                    b.w("tensor", qk_ready if s == 0 else None, bfree[pi])
                    mm = b.op("tensor", "matmul", bps[pi][:, :],
                              mblk[:, tau * 128:(tau + 1) * 128],
                              qk[:, s * TB:(s + 1) * TB], start=True, stop=True)
                    b.w("vector", mm, qk_tok)
                    t = b.op("vector", "tensor_mul", ptmp[:], vpr[:, s * TB:(s + 1) * TB],
                             bps[pi][:, :])
                    bfree[pi] = t
                    b.w("vector", t)
                    csl = slice(s * 32, (s + 1) * 32)
                    t = b.op("vector", "tensor_add",
                             y33[:, csl, tau:CC],
                             y33[:, csl, tau:CC],
                             pt3[:, :, 0:CC - tau])
                    qk_tok = t
            near_tok = qk_tok
            rr_tok = None
            for ct in range(2):
                for lam in range(CC):
                    vsl = vpr.ap()[:, ct * 2048 + lam: ct * 2048 + lam + 16 * 127 + 1:16]
                    tt = b.op("tensor", "transpose", ptb[0:128, 0:128], vsl, self.idb[:])
                    b.w("vector", tt)
                    rr_tok = b.op("vector", "tensor_copy", vre[ct][:, lam * 128:(lam + 1) * 128],
                                  ptb[0:128, 0:128])
                    b.w("tensor", rr_tok)
                    ksl = kpr.ap()[:, ct * 2048 + lam: ct * 2048 + lam + 16 * 127 + 1:16]
                    tt = b.op("tensor", "transpose", ptb[0:128, 0:128], ksl, self.idb[:])
                    b.w("vector", tt)
                    rr_tok = b.op("vector", "tensor_copy", kre[ct][:, lam * 128:(lam + 1) * 128],
                                  ptb[0:128, 0:128])
                    b.w("tensor", rr_tok)
            gtok = [None, None]
            rep_rd = [None, None]
            rep_user = [None, None]
            b.w("gpsimd", re1)
            rep_rd[0] = b.dma("gpsimd", rept2[0][:], reprows[0, CC:CC + 1, :].to_broadcast([128, 8192]))
            for lam in range(CC):
                buf = lam % 2
                if lam + 1 < CC:
                    e2 = CC - (lam + 1)
                    b.w("gpsimd", rep_user[1 - buf])
                    rep_rd[1 - buf] = b.dma("gpsimd", rept2[1 - buf][:],
                                            reprows[0, e2:e2 + 1, :].to_broadcast([128, 8192]))
                for ct in range(2):
                    vbc = vre[ct][:, lam * 128:(lam + 1) * 128].rearrange(
                        "p (h i) -> p h i", h=2).unsqueeze(3).to_broadcast([128, 2, 64, 64])
                    kbc = kre[ct][:, lam * 128:(lam + 1) * 128].rearrange(
                        "p (h j) -> p h j", h=2).unsqueeze(2).to_broadcast([128, 2, 64, 64])
                    u4 = utile.ap().rearrange("p (h i j) -> p h i j", h=2, i=64)
                    b.w("vector", rr_tok if lam == 0 and ct == 0 else None, gtok[1 - ct])
                    t = b.op("vector", "tensor_mul", u4, vbc, kbc)
                    b.w("vector", t, rep_rd[buf] if ct == 0 else None)
                    t = b.op("vector", "tensor_mul", utile[:], utile[:], rept2[buf][:])
                    b.w("vector", t)
                    if lam == 0:
                        t = b.op("vector", "tensor_copy", gacc[ct][:], utile[:])
                    else:
                        t = b.op("vector", "tensor_add", gacc[ct][:], gacc[ct][:], utile[:])
                    gtok[ct] = t
                rep_user[buf] = gtok[1]
            scan_tok = None
            wb_tok = None
            b.w("tensor", gtok[1], dct_tok)
            for s in range(64):
                tt1 = b.op("tensor", "transpose", ptb[0:128, 0:128],
                           gacc[0][:, s * 128:(s + 1) * 128], self.idb[:])
                b.w("vector", tt1, wb_tok)
                ev1 = b.op("vector", "tensor_copy", gtb[:, 0:128], ptb[0:128, 0:128])
                b.w("tensor", ev1)
                tt2 = b.op("tensor", "transpose", ptb[0:128, 0:128],
                           gacc[1][:, s * 128:(s + 1) * 128], self.idb[:])
                b.w("vector", tt2)
                ev2 = b.op("vector", "tensor_copy", gtb[:, 128:256], ptb[0:128, 0:128])
                b.w("vector", ev2)
                t = b.op("vector", "memset", phb[:, 0:1], 0.0)
                b.w("vector", t)
                dc = dcol.ap()[:, s:s + 1].to_broadcast([128, 255])
                t = b.op("vector", "tensor_tensor_scan", phb[:, 1:256], dc,
                         gtb[:, 0:255], 0.0, ALU.mult, ALU.add)
                scan_tok = t
                b.w("tensor", t, ev2)
                tt3 = b.op("tensor", "transpose", ptb[0:128, 0:128], phb[:, 0:128], self.idb[:])
                b.w("vector", tt3)
                wb1 = b.op("vector", "tensor_copy", gacc[0][:, s * 128:(s + 1) * 128],
                           ptb[0:128, 0:128])
                b.w("tensor", wb1)
                tt4 = b.op("tensor", "transpose", ptb[0:128, 0:128], phb[:, 128:256], self.idb[:])
                b.w("vector", tt4)
                wb2 = b.op("vector", "tensor_copy", gacc[1][:, s * 128:(s + 1) * 128],
                           ptb[0:128, 0:128])
                b.w("tensor", wb2)
                wb_tok = wb2
            phic = gacc
            b.w("tensor", wb_tok)
            qre_tok = None
            for ct in range(2):
                for lam in range(CC):
                    qsl = qpad.ap()[:, ct * 2048 + lam: ct * 2048 + lam + 16 * 127 + 1:16]
                    tt = b.op("tensor", "transpose", ptb[0:128, 0:128], qsl, self.idb[:])
                    b.w("vector", tt)
                    qre_tok = b.op("vector", "tensor_copy", kre[ct][:, lam * 128:(lam + 1) * 128],
                                   ptb[0:128, 0:128])
                    b.w("tensor", qre_tok)
            cons_tok = near_tok
            rep_user = [gtok[1], gtok[1]]
            b.w("gpsimd", re2, rep_user[0])
            rep_rd[0] = b.dma("gpsimd", rept2[0][:], reprows[1, 0:1, :].to_broadcast([128, 8192]))
            for lam in range(CC):
                buf = lam % 2
                if lam + 1 < CC:
                    b.w("gpsimd", rep_user[1 - buf])
                    rep_rd[1 - buf] = b.dma("gpsimd", rept2[1 - buf][:],
                                            reprows[1, lam + 1:lam + 2, :].to_broadcast([128, 8192]))
                for ct in range(2):
                    b.w("vector", rep_rd[buf] if ct == 0 else None,
                        qre_tok if lam == 0 and ct == 0 else None,
                        wb_tok if lam == 0 and ct == 0 else None)
                    t = b.op("vector", "tensor_mul", psit[:], phic[ct][:], rept2[buf][:])
                    qbc = kre[ct][:, lam * 128:(lam + 1) * 128].rearrange(
                        "p (h j) -> p h j", h=2).unsqueeze(2).to_broadcast([128, 2, 64, 64])
                    p4 = psit.ap().rearrange("p (h i j) -> p h i j", h=2, i=64)
                    b.w("vector", t)
                    t = b.op("vector", "tensor_mul", p4, p4, qbc)
                    p3v = psit.ap().rearrange("p (g j) -> p g j", j=64)
                    wdt = 32
                    while wdt >= 1:
                        b.w("vector", t)
                        if wdt == 1:
                            t = b.op("vector", "tensor_add",
                                     y3f.ap().unsqueeze(2),
                                     p3v[:, :, 0:1], p3v[:, :, 1:2])
                        else:
                            t = b.op("vector", "tensor_add", p3v[:, :, 0:wdt],
                                     p3v[:, :, 0:wdt], p3v[:, :, wdt:2 * wdt])
                        wdt //= 2
                    rep_user[buf] = t
                    b.w("tensor", t)
                    tt = b.op("tensor", "transpose", ptf[0:128, 0:128], y3f[:], self.idf[:])
                    b.w("vector", tt, cons_tok)
                    yap = y33[:, ct * 128:(ct + 1) * 128, lam:lam + 1]
                    t = b.op("vector", "tensor_add", yap, yap,
                             ptf[0:128, 0:128].unsqueeze(2))
                    cons_tok = t
                    b.w("tensor", t)
            b.w("vector", cons_tok)
            t = b.op("vector", "tensor_copy", ybf[:], y3[:])
            b.w("gpsimd", t)
            b.dma("gpsimd", self.scr["ystp"][p], ybf[:])
            b.w("vector", b.dtok("gpsimd"))

    def p1(self, st):
        self.p1_attn(st, self.scr["q0"], self.scr["k0"], self.scr["v0"], self.scr["ysm0"])

    def p1i_alloc(self, st, tag="i1", natt=2, npsc=2):
        nc, b = self.nc, self.b
        en = st.enter_context
        t = {}
        t["q65"] = en(nc.sbuf_tensor(f"{tag}_q65", [65, T], BF16))
        t["k65"] = en(nc.sbuf_tensor(f"{tag}_k65", [65, T], BF16))
        t["vh"] = en(nc.sbuf_tensor(f"{tag}_vh", [64, T], BF16))
        t["sqt"] = en(nc.sbuf_tensor(f"{tag}_sqt", [64, T], BF16))
        t["nrow"] = None
        t["mrow"] = None
        t["nrow_guard"] = None
        t["rmax"] = en(nc.sbuf_tensor(f"{tag}_rmax", [1, 1], F32))
        t["v65"] = en(nc.sbuf_tensor(f"{tag}_v65", [128, 65 * NT], BF16))
        t["att"] = [en(nc.sbuf_tensor(f"{tag}_at{i}", [128, TB], BF16)) for i in range(natt)]
        t["yraw"] = en(nc.sbuf_tensor(f"{tag}_yr", [65, T], BF16))
        t["rrow"] = en(nc.sbuf_tensor(f"{tag}_rr", [1, T], BF16))
        t["ones1"] = en(nc.sbuf_tensor(f"{tag}_o1", [1, 64], BF16))
        t["psc"] = [en(nc.psum_tensor(f"{tag}_psc{i}", [128, 512], F32)) for i in range(npsc)]
        t["pv"] = en(nc.psum_tensor(f"{tag}_pv", [128, 512], F32))
        t["ptb"] = en(nc.psum_tensor(f"{tag}_ptb", [128, 1024], BF16))
        v65r = t["v65"].ap().rearrange("p (n c) -> p n c", c=65)
        o1 = b.op("gpsimd", "memset", v65r[:, :, 64:65], 1.0)
        o2 = b.op("gpsimd", "memset", t["k65"][64:65, :], 1.0)
        o3 = b.op("gpsimd", "memset", t["ones1"][:], 1.0)
        b.w("gpsimd", o1, o2, o3)
        t["ones1_tok"] = o3
        t["kone"] = o2
        t["ones"] = o1
        t["ysm_dma"] = None
        t["vtm"] = None
        t["sqt_mm"] = None
        t["prev_tensor"] = None
        t["psc_free"] = [None] * npsc
        t["att_free"] = [None] * natt
        t["prep_b_done"] = None
        t["mdma"] = None
        t["evac_last"] = None
        t["fin_vec"] = None
        t["fin_pe"] = None
        t["fin_mul"] = [None] * npsc
        return t

    def p1i_prep_a(self, t, h, qsrc, ksrc, vsrc):
        nc, b = self.nc, self.b
        pair, half = h // 2, h % 2
        r0 = half * 64
        par = h % 2
        q65, k65, sqt = t["q65"], t["k65"], t["sqt"]
        vh = t["vh"]
        nrow, mrow, v65 = t["nrow"], t["mrow"], t["v65"]
        psc, ptb = t["psc"], t["ptb"]
        psc_free = t["psc_free"]
        b.w("scalar", t.pop("load_guard", t["prev_tensor"]), t["ysm_dma"],
            t["prep_b_done"], t["mdma"], t["vtm"])
        lq = b.dma("scalar", q65[0:64, :], qsrc[pair, r0:r0 + 64, :])
        lk = b.dma("scalar", k65[0:64, :], ksrc[pair, r0:r0 + 64, :])
        lv = b.dma("scalar", vh[:], vsrc[pair, r0:r0 + 64, :])
        psn = psc[0]

        def normrow(src_ap, dstrow, ld):
            b.w("scalar", ld)
            tq = b.op("scalar", "activation", sqt[:], src_ap, AF.Square)
            last = None
            for s in range(NB):
                b.w("tensor", tq if s == 0 else None, last,
                    psc_free[0] if s == 0 else None)
                mm = b.op("tensor", "matmul", psn[0:1, 0:512], self.ones_colb[0:64, :],
                          sqt[:, s * TB:(s + 1) * TB], start=True, stop=True)
                b.w("scalar", mm)
                last = b.op("scalar", "activation",
                            dstrow[0:1, s * TB:(s + 1) * TB],
                            psn[0:1, 0:512], AF.Sqrt)
            return last
        b.w("scalar", t["nrow_guard"])
        tn = normrow(q65[0:64, :], nrow, lq)
        tn = normrow(k65[0:64, :], mrow, lk)
        psc_free[0] = tn
        t["tn"] = tn
        t["sqt_mm"] = b.tok("tensor")
        b.w("tensor", lv, t["ones"] if h == 0 else None)
        for kt in range(NT):
            tt = b.op("tensor", "transpose", ptb[0:128, 0:64],
                      vh[:, kt * 128:(kt + 1) * 128], self.idb[0:64, 0:64])
            b.w("scalar", tt)
            ev = b.op("scalar", "copy", v65[:, kt * 65:kt * 65 + 64],
                      ptb[0:128, 0:64])
            b.w("tensor", ev)
        t["vtm"] = b.tok("scalar")

    def p1i_prep_b(self, t, h):
        b = self.b
        b.w("vector", t["tn"])
        rx = b.op("vector", "reduce_max", t["rmax"][0:1, 0:1], t["mrow"][0:1, :],
                  axis=mybir.AxisListType.X)
        b.w("vector", rx)
        ts = b.op("vector", "tensor_scalar", t["mrow"][0:1, :], t["nrow"][0:1, :],
                  t["rmax"][0:1, 0:1], -1.0, ALU.mult, ALU.mult)
        b.w("gpsimd", ts, t["prev_tensor"])
        t["mdma"] = b.dma("gpsimd", t["q65"][64:65, :], t["mrow"][0:1, :])
        t["prep_b_done"] = ts

    def p1i_main(self, t, h):
        nc, b = self.nc, self.b
        q65, k65, v65 = t["q65"], t["k65"], t["v65"]
        att = t["att"]
        yraw = t["yraw"]
        psc, pvP = t["psc"], t["pv"]
        psc_free, att_free = t["psc_free"], t["att_free"]
        b.w("tensor", t["vtm"], t["mdma"], t["kone"] if h == 0 else None)
        b.w("scalar", t["fin_vec"], t["fin_pe"])
        t["pv_guard"] = t.get("pv_guard_next")
        cp = None
        for qb in range(NB):
            nk = 4 * qb + 4
            q_ap = q65[:, qb * TB:(qb + 1) * TB]

            def scores(kt):
                pi = kt % len(psc)
                diag = kt >= 4 * qb
                b.w("tensor", psc_free[pi])
                b.op("tensor", "matmul", psc[pi][:, :], k65[:, kt * 128:(kt + 1) * 128],
                     q_ap, start=True, stop=not diag)
                if diag:
                    r = kt - 4 * qb
                    b.op("tensor", "matmul", psc[pi][:, :], self.idb[:],
                         self.masks[:, r * TB:(r + 1) * TB], start=False, stop=True)
                return b.tok("tensor")

            exs = [None] * nk

            def pv(kt, w=True):
                ai = kt % len(att)
                b.w("tensor", exs[kt] if w else None,
                    t.pop("pv_guard", None) if kt == 0 and qb == 0 else None,
                    t.pop("cp_guard", None) if kt == 0 else None)
                b.op("tensor", "matmul", pvP[0:65, 0:512],
                     v65[:, kt * 65:(kt + 1) * 65], att[ai][:, :],
                     start=(kt == 0), stop=(kt == nk - 1))
                att_free[ai] = b.tok("tensor")
            if len(psc) >= 4:
                sc = {0: scores(0)}
                if nk > 1:
                    sc[1] = scores(1)
                for kt in range(nk):
                    pi = kt % len(psc)
                    ai = kt % len(att)
                    if kt % 2 == 0:
                        w2 = min(kt + 1, nk - 1)
                        b.w("scalar", sc[w2], att_free[ai],
                            att_free[(kt + 1) % len(att)] if kt + 1 < nk else None)
                    ex = b.op("scalar", "activation", att[ai][:], psc[pi][:, :], AF.Exp)
                    psc_free[pi] = ex
                    exs[kt] = ex
                    if kt + 2 < nk:
                        sc[kt + 2] = scores(kt + 2)
                    if kt % 2 == 1 and kt >= 3:
                        b.w("tensor", exs[kt - 2])
                        pv(kt - 3, w=False)
                        pv(kt - 2, w=False)
                b.w("tensor", exs[nk - 1])
                pv(nk - 2, w=False)
                pv(nk - 1, w=False)
            else:
                sc_tok = scores(0)
                for kt in range(nk):
                    pi = kt % len(psc)
                    ai = kt % len(att)
                    b.w("scalar", sc_tok, att_free[ai])
                    ex = b.op("scalar", "activation", att[ai][:], psc[pi][:, :], AF.Exp)
                    psc_free[pi] = ex
                    exs[kt] = ex
                    if kt + 1 < nk:
                        sc_tok = scores(kt + 1)
                    if kt >= 1:
                        pv(kt - 1)
                pv(nk - 1)
            pv_tok = b.tok("tensor")
            ee = t.get("evac_eng", "scalar")
            b.w(ee, pv_tok)
            cp = b.op(ee, "tensor_copy" if ee == "vector" else "copy",
                      yraw[:, qb * TB:(qb + 1) * TB], pvP[0:65, 0:512])
            t["cp_guard"] = cp
        t["evac_last"] = cp
        t["prev_tensor"] = b.tok("tensor")

    def p1i_fin(self, t, h, ydst):
        nc, b = self.nc, self.b
        pair, half = h // 2, h % 2
        r0 = half * 64
        yraw, rrow = t["yraw"], t["rrow"]
        ysm = t["sqt"]
        pvP = t["pv"]
        b.w("vector", t["evac_last"], t["fin_vec"])
        rc = b.op("vector", "reciprocal", rrow[0:1, :], yraw[64:65, :],
                  _lp="bf16 softmax denominator reciprocal (within tolerance)")
        mul = None
        for s in range(NB):
            pi = s % 2
            hp = pvP[pi * 64:pi * 64 + 64, 0:512]
            b.w("tensor", rc if s == 0 else None, t["fin_mul"][pi],
                t["evac_last"] if s == 0 else None,
                t["sqt_mm"] if s == 0 else None,
                t["ones1_tok"] if s == 0 and h == 0 else None)
            bc = b.op("tensor", "matmul", hp, t["ones1"][:],
                      rrow[0:1, s * TB:(s + 1) * TB], start=True, stop=True)
            b.w("vector", bc, t["ysm_dma"] if s == 0 else None)
            mul = b.op("vector", "tensor_mul", ysm[:, s * TB:(s + 1) * TB],
                       yraw[0:64, s * TB:(s + 1) * TB], hp)
            t["fin_mul"][pi] = mul
        t["fin_vec"] = mul
        t["pv_guard_next"] = mul
        t["fin_pe"] = b.tok("tensor")
        b.w("scalar", mul)
        t["ysm_dma"] = b.dma("scalar", ydst[pair, r0:r0 + 64, :], ysm[:])

    def p2i_alloc(self, st):
        nc, b = self.nc, self.b
        en = st.enter_context
        t = {}
        t["dt"] = en(nc.sbuf_tensor("i2_dt", [128, 64], F32))
        t["gmt"] = en(nc.sbuf_tensor("i2_gmt", [128, 64], F32))
        t["mrun"] = en(nc.sbuf_tensor("i2_mrun", [128, 64], F32))
        t["mblk"] = en(nc.sbuf_tensor("i2_mblk", [128, CC * 128], BF16))
        t["wblk"] = en(nc.sbuf_tensor("i2_wblk", [128, 128], BF16))
        t["wsf"] = en(nc.sbuf_tensor("i2_wsf", [128, 64], F32))
        t["drow"] = en(nc.sbuf_tensor("i2_drow", [1, 2048], F32))
        t["ebf"] = en(nc.sbuf_tensor("i2_ebf", [17, 4096], BF16))
        t["dcol"] = [en(nc.sbuf_tensor(f"i2_dc{i}", [128, 32], BF16)) for i in range(2)]
        t["est"] = en(nc.sbuf_tensor("i2_est", [32, 128], BF16))
        t["qpad"] = en(nc.sbuf_tensor("i2_qpad", [128, T + CC], BF16))
        t["kpr"] = en(nc.sbuf_tensor("i2_kpr", [128, T], BF16))
        t["vpr"] = en(nc.sbuf_tensor("i2_vpr", [128, T], BF16))
        t["y3"] = en(nc.sbuf_tensor("i2_y3", [128, T], F32))
        t["u"] = en(nc.sbuf_tensor("i2_u", [128, 4096], BF16))
        t["rep"] = [en(nc.sbuf_tensor(f"i2_rep{i}", [128, 4096], BF16)) for i in range(2)]
        t["gacc"] = [en(nc.sbuf_tensor(f"i2_ga{i}", [128, 4096], BF16)) for i in range(2)]
        t["vre"] = [en(nc.sbuf_tensor(f"i2_vre{i}", [128, 2048], BF16)) for i in range(2)]
        t["kre"] = [en(nc.sbuf_tensor(f"i2_kre{i}", [128, 2048], BF16)) for i in range(2)]
        t["qre"] = [en(nc.sbuf_tensor(f"i2_qre{i}", [128, 2048], BF16)) for i in range(2)]
        t["gtb"] = [en(nc.sbuf_tensor(f"i2_gtb{i}", [128, 256], BF16)) for i in range(2)]
        t["phb"] = [en(nc.sbuf_tensor(f"i2_phb{i}", [128, 256], BF16)) for i in range(2)]
        t["ptmp"] = en(nc.sbuf_tensor("i2_ptmp", [128, 512], BF16))
        t["y3fb"] = [en(nc.sbuf_tensor(f"i2_y3fb{i}", [128, 128], BF16)) for i in range(2)]
        t["y3ft"] = [en(nc.sbuf_tensor(f"i2_y3ft{i}", [128, 128], BF16)) for i in range(2)]
        t["y3h1"] = en(nc.sbuf_tensor("i2_y3h1", [64, T], F32))
        t["bps"] = en(nc.psum_tensor("i2_bps", [128, 512], F32))
        t["bps2"] = en(nc.psum_tensor("i2_bps2", [128, 512], F32))
        t["ptb"] = en(nc.psum_tensor("i2_ptb", [128, 1024], BF16))
        t["pts"] = en(nc.psum_tensor("i2_pts", [128, 1024], BF16))
        mz = b.op("vector", "memset", t["mblk"][:], 0.0)
        wz = b.op("vector", "memset", t["wblk"][:], 0.0)
        ez = b.op("vector", "memset", t["est"][:], 0.0)
        qz = b.op("vector", "memset", t["qpad"][:, T:T + CC], 0.0)
        t["qz"] = qz
        p0 = b.op("vector", "memset", t["phb"][0][:, 0:1], 0.0)
        p1z = b.op("vector", "memset", t["phb"][1][:, 0:1], 0.0)
        t["phz"] = p1z
        f0 = b.op("vector", "memset", t["y3fb"][0][:], 0.0)
        f1 = b.op("vector", "memset", t["y3fb"][1][:], 0.0)
        b.w("vector", mz, wz, ez, qz, p0, p1z, f0, f1)
        return t

    def p2i_pair_loads_params(self, t, p):
        b = self.b
        lamT, gamT, wstT = self.inp["lamT"], self.inp["gamT"], self.inp["wstT"]
        b.w("scalar", t.get("mblk_tok"), t.get("wblk_tok"))
        t["pl1"] = b.dma("scalar", t["dt"][:], lamT[2 * p:2 * p + 2].rearrange("h j i -> (h j) i"))
        t["pl2"] = b.dma("scalar", t["gmt"][:], gamT[2 * p:2 * p + 2].rearrange("h j i -> (h j) i"))
        t["pl3"] = b.dma("scalar", t["wsf"][:], wstT[2 * p:2 * p + 2].rearrange("h j i -> (h j) i"))

    def p2i_pair_loads(self, t, p, t1):
        b = self.b
        b.w("scalar", t.get("near_tok"), t.get("rr_tok"), t1.get("prep_b_done"),
            t1.get("mdma"))
        t["plq"] = b.dma("scalar", t["qpad"][:, 0:T], self.scr["q0"][p])
        t["plk"] = b.dma("scalar", t["kpr"][:], self.scr["k0"][p])
        t["plv"] = b.dma("scalar", t["vpr"][:], self.scr["v0"][p])
        t["pre_pair"] = p

    def p2i_pair_start(self, t, p):
        nc, b = self.nc, self.b
        lamT, gamT, wstT = self.inp["lamT"], self.inp["gamT"], self.inp["wstT"]
        dt, gmt, mrun, mblk, wblk, wsf = (t["dt"], t["gmt"], t["mrun"], t["mblk"],
                                          t["wblk"], t["wsf"])
        qpad, kpr, vpr, y3 = t["qpad"], t["kpr"], t["vpr"], t["y3"]
        bps, ptmp = t["bps"], t["ptmp"]
        utile = t["u"]
        if t.get("pre_pair") == p:
            l1, l2, l3 = t["pl1"], t["pl2"], t["pl3"]
            lq, lk, lv = t["plq"], t["plk"], t["plv"]
        else:
            l1 = b.dma("sync", dt[:], lamT[2 * p:2 * p + 2].rearrange("h j i -> (h j) i"))
            l2 = b.dma("sync", gmt[:], gamT[2 * p:2 * p + 2].rearrange("h j i -> (h j) i"))
            l3 = b.dma("sync", wsf[:], wstT[2 * p:2 * p + 2].rearrange("h j i -> (h j) i"))
            lq = b.dma("sync", qpad[:, 0:T], self.scr["q0"][p])
            lk = b.dma("sync", kpr[:], self.scr["k0"][p])
            lv = b.dma("sync", vpr[:], self.scr["v0"][p])
        b.w("scalar", l1)
        dt_tok = b.op("scalar", "activation", dt[:], dt[:], AF.Sigmoid, scale=-1.0)
        b.w("vector", l2, dt_tok)
        tv = b.op("vector", "tensor_copy", mrun[:], gmt[:])
        for tau in range(CC):
            b.w("vector", tv)
            c1 = b.op("vector", "tensor_copy", mblk[0:64, tau * 128:tau * 128 + 64], mrun[0:64, :])
            b.w("vector", c1)
            c2 = b.op("vector", "tensor_copy", mblk[64:128, tau * 128 + 64:tau * 128 + 128],
                      mrun[64:128, :])
            b.w("vector", c2)
            tv = b.op("vector", "tensor_mul", mrun[:], mrun[:], dt[:])
        mblk_tok = tv
        b.w("vector", l3)
        c1 = b.op("vector", "tensor_copy", wblk[0:64, 0:64], wsf[0:64, :])
        b.w("vector", c1)
        wblk_tok = b.op("vector", "tensor_copy", wblk[64:128, 64:128], wsf[64:128, :])
        b.w("vector", t.get("y3h1_dma"))
        zh = b.op("vector", "memset", t["y3h1"][:], 0.0)
        t["y3h1_zero"] = zh
        t["mblk_tok"] = mblk_tok
        t["wblk_tok"] = wblk_tok
        bpss = [bps, t["bps2"]]
        bfree_b = [None, None]
        b.w("tensor", wblk_tok, lq, t.pop("qz", None), lk, lv)
        for s in range(NB):
            bk = bpss[s % 2]
            b.w("tensor", bfree_b[s % 2])
            mm = b.op("tensor", "matmul", bk[:, :], wblk[:],
                      qpad[:, s * TB:(s + 1) * TB], start=True, stop=True)
            b.w("vector", mm, t.get("ystp_dma") if s == 0 else None)
            bfree_b[s % 2] = b.op("vector", "tensor_copy",
                                  y3[:, s * TB:(s + 1) * TB], bk[:, :])
        qk = utile[:, 0:T]
        qk3 = qk.rearrange("p (c s) -> p c s", s=CC)
        y33 = y3.ap().rearrange("p (c s) -> p c s", s=CC)
        pt3 = ptmp.ap().rearrange("p (c s) -> p c s", s=CC)
        t["y33"] = y33
        qk_tok = None
        b.w("tensor", mblk_tok)
        for tau in range(CC):
            b.w("vector", qk_tok)
            tv = b.op("vector", "tensor_mul", qk, kpr[:], qpad[:, tau:tau + T])
            if tau > 0:
                b.w("vector", tv)
                tv = b.op("vector", "memset", qk3[:, :, CC - tau:CC], 0.0)
            qk_ready = tv
            for s in range(NB):
                bk = bpss[s % 2]
                b.w("tensor", qk_ready if s == 0 else None, bfree_b[s % 2])
                mm = b.op("tensor", "matmul", bk[:, :],
                          mblk[:, tau * 128:(tau + 1) * 128],
                          qk[:, s * TB:(s + 1) * TB], start=True, stop=True)
                b.w("vector", mm, qk_tok)
                tv = b.op("vector", "tensor_mul", ptmp[:], vpr[:, s * TB:(s + 1) * TB],
                          bk[:, :])
                bfree_b[s % 2] = tv
                b.w("vector", tv)
                csl = slice(s * 32, (s + 1) * 32)
                tv = b.op("vector", "tensor_add", y33[:, csl, tau:CC],
                          y33[:, csl, tau:CC], pt3[:, :, 0:CC - tau])
                qk_tok = tv
        t["near_tok"] = qk_tok
        t["bfree"] = bfree_b[(NB - 1) % 2]
        vre, kre, qre, ptb = t["vre"], t["kre"], t["qre"], t["ptb"]
        rr_tok = None
        b.w("tensor", t.get("scanptb_free"))
        for ct in range(2):
            for lam in range(CC):
                for (srct, dstt) in ((vpr, vre), (kpr, kre), (qpad, qre)):
                    sl = srct.ap()[:, ct * 2048 + lam: ct * 2048 + lam + 16 * 127 + 1:16]
                    tt = b.op("tensor", "transpose", ptb[0:128, 0:128], sl, self.idb[:])
                    b.w("vector", tt)
                    rr_tok = b.op("vector", "tensor_copy",
                                  dstt[ct][:, lam * 128:(lam + 1) * 128], ptb[0:128, 0:128])
                    b.w("tensor", rr_tok)
        t["rr_tok"] = rr_tok

    def p2i_far_prep(self, t, p, hh):
        nc, b = self.nc, self.b
        lamN, gamN = self.inp["lamN"], self.inp["gamN"]
        reprows = self.scr["reprows"]
        h = 2 * p + hh
        drow, ebf = t["drow"], t["ebf"]
        dcol, est = t["dcol"][hh], t["est"]
        gbf = t["u"]
        bps = t["bps"]
        elast = t.get("exps_last")
        ebf_free = t.get("ebf_free") or []
        lamr = lamN[h].rearrange("i j -> () (i j)")
        for hf in range(2):
            b.w("scalar", t.get("drow_free") if hf == 0 else None)
            l4 = b.dma("scalar", drow[:], lamr[0:1, hf * 2048:(hf + 1) * 2048])
            b.w("scalar", l4)
            tv = b.op("scalar", "activation", drow[:], drow[:], AF.Sigmoid, scale=-1.0)
            b.w("scalar", tv)
            lnd_tok = b.op("scalar", "activation", drow[:], drow[:], AF.Ln)
            for s in range(4):
                b.w("tensor", lnd_tok if s == 0 else None, elast,
                    t.get("bfree") if s == 0 and hf == 0 else None,
                    *(ebf_free if s == 0 and hf == 0 else []))
                mm = b.op("tensor", "matmul", bps[0:17, 0:512], self.erow17[:],
                          drow[0:1, s * 512:(s + 1) * 512], start=True, stop=True)
                b.w("scalar", mm)
                elast = b.op("scalar", "activation",
                             ebf[0:17, hf * 2048 + s * 512:hf * 2048 + (s + 1) * 512],
                             bps[0:17, 0:512], AF.Exp)
        t["exps_last"] = elast
        drow_mm = b.tok("tensor")
        ebf_tok = elast
        b.w("gpsimd", ebf_tok, *t.get("rep_rds", []))
        re1 = b.dma("gpsimd", reprows[0, :, 0:4096], ebf[0:17, :])
        b.w("scalar", re1, t.get("est_free"))
        ed = b.dma("scalar", est[:, :],
                   reprows[0, 16:17, 0:4096].rearrange("() (a c) -> a c", c=128))
        b.w("scalar", ed)
        dct_tok = b.dmat("scalar", dcol[:, :], est[:, :])
        t["est_free"] = dct_tok
        t["dct_tok"] = dct_tok
        gamr = gamN[h].rearrange("i j -> () (i j)")
        tv = None
        for hf in range(2):
            b.w("scalar", drow_mm if hf == 0 else tv)
            l5 = b.dma("scalar", drow[:], gamr[0:1, hf * 2048:(hf + 1) * 2048])
            b.w("vector", l5, t.get("far_busy") if hf == 0 else None)
            tv = b.op("vector", "tensor_copy", gbf[0:1, hf * 2048:(hf + 1) * 2048], drow[:])
            b.w("vector", tv)
        t["drow_free"] = tv
        b.w("gpsimd", tv, t.get("far_busy"))
        rg = b.dma("gpsimd", reprows[1, 0:1, 0:4096], gbf[0:1, 0:4096])
        b.w("gpsimd", rg)
        l6 = b.dma("gpsimd", gbf[0:17, 0:4096], reprows[1, 0:1, 0:4096].to_broadcast([17, 4096]))
        b.w("vector", l6, re1, dct_tok)
        mtl_tok = b.op("vector", "tensor_mul", ebf[0:17, :], ebf[0:17, :], gbf[0:17, 0:4096])
        b.w("gpsimd", mtl_tok)
        re2 = b.dma("gpsimd", reprows[1, :, 0:4096], ebf[0:17, :])
        t["mtl_tok"] = mtl_tok
        t["re1"] = re1
        t["re2"] = re2
        t["ebf_free"] = [mtl_tok, dct_tok, re1, re2]

    def p2i_far_gscan(self, t, p, hh):
        nc, b = self.nc, self.b
        reprows = self.scr["reprows"]
        r0 = hh * 64
        utile, rept2, gacc = t["u"], t["rep"], t["gacc"]
        vre, kre = t["vre"], t["kre"]
        gtb2, phb2 = t["gtb"], t["phb"]
        dcol = t["dcol"][hh]
        rr_tok, mtl_tok, dct_tok = t["rr_tok"], t["mtl_tok"], t["dct_tok"]
        gtok = [None, None]
        rep_rd = [None, None]
        rep_user = [None, None]
        b.w("gpsimd", t["re1"])
        rep_rd[0] = b.dma("gpsimd", rept2[0][:], reprows[0, CC:CC + 1, 0:4096].to_broadcast([128, 4096]))
        for lam in range(CC):
            buf = lam % 2
            if lam + 1 < CC:
                e2 = CC - (lam + 1)
                b.w("gpsimd", rep_user[1 - buf])
                rep_rd[1 - buf] = b.dma("gpsimd", rept2[1 - buf][:],
                                        reprows[0, e2:e2 + 1, 0:4096].to_broadcast([128, 4096]))
            for ct in range(2):
                vbc = vre[ct][:, lam * 128 + r0:lam * 128 + r0 + 64].unsqueeze(2).to_broadcast([128, 64, 64])
                kbc = kre[ct][:, lam * 128 + r0:lam * 128 + r0 + 64].unsqueeze(1).to_broadcast([128, 64, 64])
                u4 = utile.ap().rearrange("p (i j) -> p i j", i=64)
                b.w("vector", rr_tok if lam == 0 and ct == 0 else None, gtok[1 - ct],
                    mtl_tok if lam == 0 and ct == 0 else None,
                    t.get("far_busy") if lam == 0 and ct == 0 else None)
                tv = b.op("vector", "tensor_mul", u4, vbc, kbc)
                b.w("vector", tv, rep_rd[buf] if ct == 0 else None)
                tv = b.op("vector", "tensor_mul", utile[:], utile[:], rept2[buf][:])
                b.w("vector", tv)
                if lam == 0:
                    tv = b.op("vector", "tensor_copy", gacc[ct][:], utile[:])
                else:
                    tv = b.op("vector", "tensor_add", gacc[ct][:], gacc[ct][:], utile[:])
                gtok[ct] = tv
            rep_user[buf] = gtok[1]
        t["gtok1"] = gtok[1]
        t["rep_rds"] = [x for x in rep_rd if x is not None]
        t["rep_user"] = rep_user
        banks = [t["pts"], t["ptb"]]
        sct = [t.get("sc_f0"), t.get("sc_f1")]
        cpfree = [t.get("cpf_0"), t.get("cpf_1")]
        otfree = [t.get("otf_0"), t.get("otf_1")]
        intok = [None, None]
        wbtok = [None, None]
        incp = [None, None]
        for s in range(34):
            bb = s % 2
            bk = banks[bb]
            if s < 32:
                b.w("tensor", cpfree[bb], incp[bb], gtok[1] if s == 0 else None,
                    rr_tok if s == 0 and bb == 1 else None)
                b.op("tensor", "transpose", bk[0:128, 0:128],
                     gacc[0][:, s * 128:(s + 1) * 128], self.idb[:])
                b.op("tensor", "transpose", bk[0:128, 128:256],
                     gacc[1][:, s * 128:(s + 1) * 128], self.idb[:])
                intok[bb] = b.tok("tensor")
            if s >= 2:
                b.w("tensor", sct[bb])
                b.op("tensor", "transpose", bk[0:128, 256:384],
                     phb2[bb][:, 0:128], self.idb[:])
                b.op("tensor", "transpose", bk[0:128, 384:512],
                     phb2[bb][:, 128:256], self.idb[:])
                otfree[bb] = b.tok("tensor")
            if s < 32:
                b.w("vector", intok[bb])
                b.op("vector", "tensor_copy", gtb2[bb][:, 0:128], bk[0:128, 0:128])
                incp[bb] = b.op("vector", "tensor_copy", gtb2[bb][:, 128:256],
                                bk[0:128, 128:256])
            if s >= 2:
                b.w("vector", otfree[bb])
                b.op("vector", "tensor_copy", gacc[0][:, (s - 2) * 128:(s - 1) * 128],
                     bk[0:128, 256:384])
                wbtok[bb] = b.op("vector", "tensor_copy",
                                 gacc[1][:, (s - 2) * 128:(s - 1) * 128],
                                 bk[0:128, 384:512])
                cpfree[bb] = wbtok[bb]
            if s < 32:
                b.w("vector", t.pop("phz", None) if s == 0 else None,
                    otfree[bb], incp[bb],
                    dct_tok if s == 0 else None)
                dc = dcol.ap()[:, s:s + 1].to_broadcast([128, 255])
                sv = b.op("vector", "tensor_tensor_scan", phb2[bb][:, 1:256], dc,
                          gtb2[bb][:, 0:255], 0.0, ALU.mult, ALU.add)
                sct[bb] = sv
        t["sc_f0"], t["sc_f1"] = sct
        t["cpf_0"], t["cpf_1"] = cpfree
        t["otf_0"], t["otf_1"] = otfree
        t["wb_last"] = wbtok[1]
        t["scanptb_free"] = otfree[1]

    def p2i_far_consume(self, t, p, hh):
        nc, b = self.nc, self.b
        reprows = self.scr["reprows"]
        r0 = hh * 64
        utile, rept2, gacc = t["u"], t["rep"], t["gacc"]
        qre = t["qre"]
        y3fb2, y3ft2 = t["y3fb"], t["y3ft"]
        y33 = t["y33"]
        phic = gacc
        wb_last = t["wb_last"]
        wb_last2 = None
        re2 = t["re2"]
        cons_tok = t["near_tok"]
        addt = [t.get("add_f0"), t.get("add_f1")]
        dmt = [t.get("dm_f0"), t.get("dm_f1")]
        rep_rd = [None, None]
        rep_user = list(t["rep_user"])
        b.w("gpsimd", re2, t["gtok1"])
        rep_rd[0] = b.dma("gpsimd", rept2[0][:], reprows[1, 0:1, 0:4096].to_broadcast([128, 4096]))
        pend = None
        prev_tree = None
        step = 0
        for lam in range(CC):
            buf = lam % 2
            if lam + 1 < CC:
                b.w("gpsimd", rep_user[1 - buf])
                rep_rd[1 - buf] = b.dma("gpsimd", rept2[1 - buf][:],
                                        reprows[1, lam + 1:lam + 2, 0:4096].to_broadcast([128, 4096]))
            for ct in range(2):
                bb = step % 2
                b.w("vector", rep_rd[buf] if ct == 0 else None,
                    wb_last if lam == 0 and ct == 0 else None,
                    wb_last2 if lam == 0 and ct == 0 else None, prev_tree)
                tv = b.op("vector", "tensor_mul", utile[:], phic[ct][:], rept2[buf][:])
                qbc = qre[ct][:, lam * 128 + r0:lam * 128 + r0 + 64].unsqueeze(1).to_broadcast([128, 64, 64])
                p4 = utile.ap().rearrange("p (i j) -> p i j", i=64)
                b.w("vector", tv)
                tv = b.op("vector", "tensor_mul", p4, p4, qbc)
                p3v = p4
                wdt = 32
                while wdt >= 1:
                    b.w("vector", tv, dmt[bb] if wdt == 1 else None)
                    if wdt == 1:
                        tv = b.op("vector", "tensor_add",
                                  y3fb2[bb][:, 0:64].unsqueeze(2),
                                  p3v[:, :, 0:1], p3v[:, :, 1:2])
                    else:
                        tv = b.op("vector", "tensor_add", p3v[:, :, 0:wdt],
                                  p3v[:, :, 0:wdt], p3v[:, :, wdt:2 * wdt])
                    wdt //= 2
                rep_user[buf] = tv
                prev_tree = tv
                if pend is not None:
                    ptok, pyap, pbb, pz = pend
                    b.w("vector", ptok, pz)
                    av = b.op("vector", "tensor_add", pyap, pyap,
                              y3ft2[pbb][0:64, 0:128].unsqueeze(2))
                    addt[pbb] = av
                    cons_tok = av
                b.w("sync", tv, addt[bb])
                dm = b.dmat("sync", y3ft2[bb][:, :], y3fb2[bb][:, :])
                dmt[bb] = dm
                if hh == 0:
                    yap = y33[0:64, ct * 128:(ct + 1) * 128, lam:lam + 1]
                else:
                    yap = t["y3h1"].ap().rearrange("p (c s) -> p c s", s=CC)[
                        :, ct * 128:(ct + 1) * 128, lam:lam + 1]
                pend = (dm, yap, bb,
                        t["y3h1_zero"] if hh == 1 and step == 0 else None)
                step += 1
        ptok, pyap, pbb, pz = pend
        b.w("vector", ptok, pz)
        av = b.op("vector", "tensor_add", pyap, pyap,
                  y3ft2[pbb][0:64, 0:128].unsqueeze(2))
        addt[pbb] = av
        cons_tok = av
        t["add_f0"], t["add_f1"] = addt
        t["dm_f0"], t["dm_f1"] = dmt
        t["near_tok"] = cons_tok
        t["far_busy"] = cons_tok

    def p2i_pair_end(self, t, p):
        nc, b = self.nc, self.b
        b.w("gpsimd", t["near_tok"])
        d1 = b.dma("gpsimd", self.scr["ystp"][p], t["y3"][:])
        b.w("gpsimd", d1)
        for cc in range(NB):
            d1 = b.dma("gpsimd", self.scr["ystp"][p, 64:128, cc * TB:(cc + 1) * TB],
                       t["y3h1"][:, cc * TB:(cc + 1) * TB], accum_op=ALU.add)
        t["prev_pair"] = t["near_tok"]
        t["ystp_dma"] = d1
        t["y3h1_dma"] = d1

    def p_mid(self, st):
        import os
        sk = set(os.environ.get("PM_SKIP", "").split(","))
        b = self.b
        b.w("tensor", self.ctok)
        t1 = self.p1i_alloc(st)
        t2 = self.p2i_alloc(st)
        t1["nrow"] = t2["qpad"][0:1, 0:T]
        t1["mrow"] = t2["kpr"][0:1, 0:T]
        q0, k0, v0 = self.scr["q0"], self.scr["k0"], self.scr["v0"]
        y0 = self.scr["ysm0"]
        p1on, faron = "p1" not in sk, "far" not in sk
        prepon = "prepa" not in sk
        if p1on and prepon:
            self.p1i_prep_a(t1, 0, q0, k0, v0)
            self.p1i_prep_b(t1, 0)
        for p in range(NP):
            b.w("sync", t2.get("prev_pair"), t2.get("ystp_dma"),
                t1["prep_b_done"], t1["mdma"])
            self.p2i_pair_start(t2, p)
            if faron and "prep" not in sk:
                self.p2i_far_prep(t2, p, 0)
            if p1on:
                if "main" not in sk:
                    self.p1i_main(t1, 2 * p)
                if prepon:
                    t1["nrow_guard"] = t2["near_tok"]
                    self.p1i_prep_a(t1, 2 * p + 1, q0, k0, v0)
            if faron and "gscan" not in sk:
                self.p2i_far_gscan(t2, p, 0)
            if p1on and prepon:
                self.p1i_prep_b(t1, 2 * p + 1)
                if "fin" not in sk:
                    self.p1i_fin(t1, 2 * p, y0)
            if faron and "consume" not in sk:
                self.p2i_far_consume(t2, p, 0)
            if faron and "prep" not in sk:
                self.p2i_far_prep(t2, p, 1)
            if p1on and "main" not in sk:
                self.p1i_main(t1, 2 * p + 1)
            if p1on and p + 1 < NP and prepon:
                t1["nrow_guard"] = t2["rr_tok"]
                self.p1i_prep_a(t1, 2 * p + 2, q0, k0, v0)
            if p + 1 < NP:
                self.p2i_pair_loads_params(t2, p + 1)
            if faron and "gscan" not in sk:
                self.p2i_far_gscan(t2, p, 1)
            if p1on:
                if p + 1 < NP and prepon:
                    self.p1i_prep_b(t1, 2 * p + 2)
                if "fin" not in sk:
                    self.p1i_fin(t1, 2 * p + 1, y0)
            if p + 1 < NP and os.environ.get("PM_NOPRE", "") != "1":
                self.p2i_pair_loads(t2, p + 1, t1)
            if faron and "consume" not in sk:
                self.p2i_far_consume(t2, p, 1)
            self.p2i_pair_end(t2, p)

    def p3(self, st):
        nc, b = self.nc, self.b
        en = st.enter_context
        wo, wot = self.load_weight_bf16(st, "outw", 8 * 8 * 128, "outw")
        ob = en(nc.sbuf_tensor("p3_ob", [128, 8], F32))
        obt = b.dma("sync", ob[:], self.inp["outb"])
        NBF = 3
        ar = [en(nc.sbuf_tensor(f"p3_ar{i}", [1, 2 * TB], F32)) for i in range(NBF)]
        ysmt = [en(nc.sbuf_tensor(f"p3_ysm{i}", [128, TB], BF16)) for i in range(NBF)]
        ystt = [en(nc.sbuf_tensor(f"p3_yst{i}", [128, TB], F32)) for i in range(NBF)]
        tb1 = [en(nc.sbuf_tensor(f"p3_tb1{i}", [128, TB], BF16)) for i in range(NBF)]
        yb = [[en(nc.sbuf_tensor(f"p3_yb{j}_{i}", [128, TB], BF16)) for i in range(NP)]
              for j in range(2)]
        xt = [en(nc.sbuf_tensor(f"p3_x{i}", [128, TB], F32)) for i in range(2)]
        ft = [en(nc.sbuf_tensor(f"p3_f{i}", [128, TB], BF16)) for i in range(2)]
        tf = [en(nc.sbuf_tensor(f"p3_tf{i}", [128, TB], F32)) for i in range(2)]
        x1t = [en(nc.sbuf_tensor(f"p3_x1{i}", [128, TB], F32)) for i in range(2)]
        par = [en(nc.psum_tensor(f"p3_par{i}", [128, 512], F32)) for i in range(2)]
        pp = [en(nc.psum_tensor(f"p3_pp{i}", [128, 512], F32)) for i in range(2)]
        b.w("tensor", wot, obt, self.ctok)
        x = self.inp["x"]
        bl_done = [None] * NBF
        par_free = [None, None]
        ybfree = [None, None]
        xfree = [None, None]
        x1d = [None, None]
        i = 0
        for blk in range(NB):
            t0 = blk * TB
            jb = blk % 2
            for p in range(NP):
                bi = i % NBF
                b.w("sync", bl_done[bi])
                la = b.dma("sync", ar[bi][0:1, 0:TB],
                           self.scr["alpha"][2 * p:2 * p + 1, t0:t0 + TB])
                lb = b.dma("sync", ar[bi][0:1, TB:2 * TB],
                           self.scr["alpha"][2 * p + 1:2 * p + 2, t0:t0 + TB])
                ls = b.dma("sync", ysmt[bi][:], self.scr["ysm0"][p, :, t0:t0 + TB])
                lt = b.dma("sync", ystt[bi][:], self.scr["ystp"][p, :, t0:t0 + TB])
                pi = i % 2
                b.w("tensor", la, lb, par_free[pi],
                    ybfree[jb] if p == 0 else None)
                b.op("tensor", "matmul", par[pi][0:64, :], self.ones_rowf[0:1, 0:64],
                     ar[bi][0:1, 0:TB], start=True, stop=True)
                am = b.op("tensor", "matmul", par[pi][64:128, :],
                          self.ones_rowf[0:1, 0:64],
                          ar[bi][0:1, TB:2 * TB], start=True, stop=True)
                b.w("vector", ls, lt, am, ybfree[jb] if p == 0 else None)
                t = b.op("vector", "tensor_sub", tb1[bi][:], ysmt[bi][:], ystt[bi][:])
                b.w("vector", t)
                t = b.op("vector", "tensor_mul", tb1[bi][:], tb1[bi][:], par[pi][:, :])
                par_free[pi] = t
                b.w("vector", t)
                t = b.op("vector", "tensor_add", yb[jb][p][:], ystt[bi][:], tb1[bi][:])
                bl_done[bi] = t
                i += 1
            yb_tok = b.tok("vector")
            ppf = [None, None]
            for m in range(KC):
                pi = m % 2
                b.w("sync", xfree[pi])
                lx = b.dma("sync", xt[pi][:], x[m * 128:(m + 1) * 128, t0:t0 + TB])
                lf = b.dma("sync", ft[pi][:], self.scr["ffn0"][m, :, t0:t0 + TB])
                b.w("tensor", ppf[pi], yb_tok if m == 0 else None)
                for k in range(KC):
                    b.op("tensor", "matmul", pp[pi][:, :],
                         wo[:, (m * 8 + k) * 128:(m * 8 + k) * 128 + 128],
                         yb[jb][k][:], start=(k == 0), stop=(k == KC - 1))
                mm = b.tok("tensor")
                if m == KC - 1:
                    ybfree[jb] = mm
                b.w("vector", mm, lf)
                t = b.op("vector", "scalar_tensor_tensor", tf[pi][:], ft[pi][:], 0.2,
                         pp[pi][:, :], ALU.mult, ALU.add)
                ppf[pi] = t
                b.w("vector", t, lx, x1d[pi])
                t = b.op("vector", "scalar_tensor_tensor", x1t[pi][:], tf[pi][:],
                         ob[:, m:m + 1], xt[pi][:], ALU.add, ALU.add)
                xfree[pi] = t
                b.w("gpsimd", t)
                x1d[pi] = b.dma("gpsimd", self.scr["x1"][m, :, t0:t0 + TB], x1t[pi][:])

    def p4a(self, st):
        nc, b = self.nc, self.b
        en = st.enter_context
        wq, t1 = self.load_weight_bf16(st, "w1q", 8 * 8 * 128, "w1q")
        wk, t2 = self.load_weight_bf16(st, "w1k", 8 * 8 * 128, "w1k")
        wv, t3 = self.load_weight_bf16(st, "w1v", 8 * 8 * 128, "w1v")
        x1 = self.scr["x1"]
        xt = [en(nc.sbuf_tensor(f"p4_x{i}", [128, TB], F32)) for i in range(KC)]
        sq = en(nc.sbuf_tensor("p4_sq", [128, TB], F32))
        rows = en(nc.sbuf_tensor("p4_rows", [1, 6 * TB], F32))
        xnb = [en(nc.sbuf_tensor(f"p4_xn{i}", [128, TB], BF16)) for i in range(KC)]
        cosT = en(nc.sbuf_tensor("p4_cos", [128, 32 * NT], F32))
        sinT = en(nc.sbuf_tensor("p4_sin", [128, 32 * NT], F32))
        ivf = en(nc.sbuf_tensor("p4_ivf", [1, 32], F32))
        trow = en(nc.sbuf_tensor("p4_trow", [1, 128], F32))
        ang = en(nc.sbuf_tensor("p4_ang", [128, 32], F32))
        angi = en(nc.sbuf_tensor("p4_angi", [128, 32], I32))
        angf = en(nc.sbuf_tensor("p4_angf", [128, 32], F32))
        qtm = en(nc.sbuf_tensor("p4_qtm", [128, 1024], F32))
        rt = [en(nc.sbuf_tensor(f"p4_rt{i}", [128, 512], F32)) for i in range(4)]
        rout = en(nc.sbuf_tensor("p4_rout", [128, 1024], BF16))
        qfmb = en(nc.sbuf_tensor("p4_qfmb", [128, NP * 128], BF16))
        kfmb = en(nc.sbuf_tensor("p4_kfmb", [128, NP * 128], BF16))
        ev = en(nc.sbuf_tensor("p4_ev", [128, TB], BF16))
        ps_sx = en(nc.psum_tensor("p4_psx", [128, 512], F32))
        ps_sx2 = en(nc.psum_tensor("p4_psx2", [128, 512], F32))
        ps_r = en(nc.psum_tensor("p4_psr", [128, 512], F32))
        ps_m = en(nc.psum_tensor("p4_psm", [128, 512], F32))
        pp = [en(nc.psum_tensor(f"p4_pp{i}", [128, 512], F32)) for i in range(2)]
        ptb = en(nc.psum_tensor("p4_ptb", [128, 1024], BF16))
        PI = 3.141592653589793
        b.w("tensor", t1, t2, t3, self.ctok)
        ROW = lambda i: rows[0:1, i * TB:(i + 1) * TB]
        t = b.op("scalar", "activation", ivf[:], self.irow32[:], AF.Exp,
                 scale=-np.log(10000.0) / 32.0)
        ivt = t
        tab_tok = None
        prev_mm = None
        for tt in range(NT):
            b.w("gpsimd", prev_mm)
            g = b.op("gpsimd", "iota", trow[:], pattern=[[1, 128]], base=tt * 128,
                     channel_multiplier=0, allow_small_or_imprecise_dtypes=True)
            b.w("tensor", g, ivt, tab_tok)
            mm = b.op("tensor", "matmul", ps_sx[0:128, 0:32],
                      trow[:], ivf[:], start=True, stop=True)
            prev_mm = mm
            MAGIC = 12582912.0
            b.w("vector", mm)
            t = b.op("vector", "tensor_scalar", ang[:], ps_sx[0:128, 0:32],
                     1.0 / (2 * PI), None, ALU.mult)
            b.w("vector", t)
            t = b.op("vector", "tensor_scalar", angf[:], ang[:], MAGIC, MAGIC, ALU.add, ALU.subtract)
            b.w("vector", t)
            t = b.op("vector", "tensor_sub", angf[:], ang[:], angf[:])
            b.w("scalar", t)
            s1 = b.op("scalar", "activation", sinT[:, tt * 32:(tt + 1) * 32], angf[:],
                      AF.Sin, scale=2 * PI)
            b.w("vector", s1)
            t = b.op("vector", "tensor_scalar", ang[:], ang[:], -1.0, 0.25, ALU.mult, ALU.add)
            b.w("vector", t)
            t = b.op("vector", "tensor_scalar", angf[:], ang[:], MAGIC, MAGIC, ALU.add, ALU.subtract)
            b.w("vector", t)
            t = b.op("vector", "tensor_sub", angf[:], ang[:], angf[:])
            b.w("scalar", t)
            s2 = b.op("scalar", "activation", cosT[:, tt * 32:(tt + 1) * 32], angf[:],
                      AF.Sin, scale=2 * PI)
            tab_tok = s2
            b.w("tensor", s2)
        prev_xn = None
        prev_v = None
        for blk in range(NB):
            t0 = blk * TB
            b.w("sync", prev_xn)
            lt = [b.dma("sync", xt[i][:], x1[i, :, t0:t0 + TB]) for i in range(KC)]
            sq_free = None
            for i in range(KC):
                b.w("scalar", lt[i], sq_free)
                qt = b.op("scalar", "activation", sq[:], xt[i][:], AF.Square)
                b.w("tensor", qt)
                m2 = b.op("tensor", "matmul", ps_sx2[0:1, :], self.ones_col[:], sq[:],
                          start=(i == 0), stop=(i == KC - 1))
                sq_free = m2
            for i in range(KC):
                b.w("tensor", lt[i] if i == 0 else None)
                mt = b.op("tensor", "matmul", ps_sx[0:1, :], self.ones_col[:], xt[i][:],
                          start=(i == 0), stop=(i == KC - 1))
            b.w("vector", mt)
            t = b.op("vector", "tensor_scalar", ROW(0), ps_sx[0:1, :], 1.0 / D, None, ALU.mult)
            b.w("vector", t, m2)
            t = b.op("vector", "tensor_scalar", ROW(1), ps_sx2[0:1, :], 1.0 / D, None, ALU.mult)
            b.w("vector", t)
            t = b.op("vector", "tensor_mul", ROW(2), ROW(0), ROW(0))
            b.w("vector", t)
            t = b.op("vector", "tensor_sub", ROW(2), ROW(1), ROW(2))
            b.w("scalar", t)
            t = b.op("scalar", "activation", ROW(3), ROW(2), AF.Sqrt, bias=self.epsc[0:1, 0:1])
            b.w("vector", t)
            t = b.op("vector", "reciprocal", ROW(4), ROW(3))
            b.w("vector", t)
            rstd_tok = t
            t = b.op("vector", "tensor_mul", ROW(5), ROW(0), ROW(4))
            mur_tok = t
            b.w("tensor", rstd_tok, prev_xn)
            rep_r = b.op("tensor", "matmul", ps_r[:, :], self.ones_rowf[:], ROW(4),
                         start=True, stop=True)
            b.w("tensor", mur_tok)
            rep_m = b.op("tensor", "matmul", ps_m[:, :], self.ones_rowf[:], ROW(5),
                         start=True, stop=True)
            xn_toks = []
            for i in range(KC):
                b.w("vector", rep_r if i == 0 else None)
                t = b.op("vector", "tensor_mul", sq[:], xt[i][:], ps_r[:, :])
                b.w("vector", t, rep_m if i == 0 else None)
                t = b.op("vector", "tensor_sub", xnb[i][:], sq[:], ps_m[:, :])
                b.w("vector", t)
                xn_toks.append(t)
                b.w("gpsimd", t)
                b.dma("gpsimd", self.scr["xn1"][i, :, t0:t0 + TB], xnb[i][:])
            prev_xn = xn_toks[-1]
            ppf = [None, None]
            ev_dma = None
            for m in range(KC):
                pi = m % 2
                b.w("tensor", ppf[pi], xn_toks[-1] if m == 0 else None)
                for k in range(KC):
                    b.op("tensor", "matmul", pp[pi][:, :],
                         wv[:, (m * 8 + k) * 128:(m * 8 + k) * 128 + 128],
                         xnb[k][:], start=(k == 0), stop=(k == KC - 1))
                mm = b.tok("tensor")
                b.w("vector", mm, ev_dma, prev_v)
                t = b.op("vector", "tensor_copy", ev[:], pp[pi][:, :])
                ppf[pi] = t
                b.w("gpsimd", t)
                ev_dma = b.dma("gpsimd", self.scr["v1"][m, :, t0:t0 + TB], ev[:])
                prev_v = None
            for sub in range(4):
                ttile = blk * 4 + sub
                for (wmat, dst, buf, scaled) in ((wq, self.scr["q1"], qfmb, False),
                                                 (wk, self.scr["k1"], kfmb, True)):
                    for nb2 in range(2):
                        pi = nb2 % 2
                        b.w("tensor", ppf[pi])
                        for k in range(KC):
                            b.op("tensor", "matmul", pp[pi][:, :],
                                 xnb[k][:, sub * 128:(sub + 1) * 128],
                                 wmat[:, k * 1024 + nb2 * 512:k * 1024 + nb2 * 512 + 512],
                                 start=(k == 0), stop=(k == KC - 1))
                        mm = b.tok("tensor")
                        b.w("vector", mm)
                        t = b.op("vector", "tensor_copy", qtm[:, nb2 * 512:(nb2 + 1) * 512], pp[pi][:, :])
                        ppf[pi] = t
                    qv = qtm.ap().rearrange("p (h e r) -> p h e r", h=16, e=2)
                    ev_ = qv[:, :, 0, :]
                    ov_ = qv[:, :, 1, :]
                    cb = cosT[:, ttile * 32:(ttile + 1) * 32].unsqueeze(1).to_broadcast([128, 16, 32])
                    sb = sinT[:, ttile * 32:(ttile + 1) * 32].unsqueeze(1).to_broadcast([128, 16, 32])
                    r3 = [rt[j].ap().rearrange("p (h r) -> p h r", h=16) for j in range(4)]
                    b.w("vector", b.tok("vector"))
                    t = b.op("vector", "tensor_mul", r3[0], ev_, cb)
                    ta = b.op("vector", "tensor_mul", r3[1], ov_, sb)
                    tbb = b.op("vector", "tensor_mul", r3[2], ev_, sb)
                    tc = b.op("vector", "tensor_mul", r3[3], ov_, cb)
                    rv = rout.ap().rearrange("p (h e r) -> p h e r", h=16, e=2)
                    b.w("vector", ta)
                    t = b.op("vector", "tensor_sub", rv[:, :, 0, :], r3[0], r3[1])
                    b.w("vector", tc)
                    t = b.op("vector", "tensor_add", rv[:, :, 1, :], r3[2], r3[3])
                    rope_tok = t
                    b.w("tensor", rope_tok)
                    for pr in range(NP):
                        tt2 = b.op("tensor", "transpose", ptb[0:128, 0:128],
                                   rout[:, pr * 128:(pr + 1) * 128], self.idb[:])
                        b.w("vector", tt2)
                        if scaled:
                            cpt = b.op("vector", "tensor_scalar", buf[:, pr * 128:(pr + 1) * 128],
                                       ptb[0:128, 0:128], 0.125, None, ALU.mult)
                        else:
                            cpt = b.op("vector", "tensor_copy", buf[:, pr * 128:(pr + 1) * 128],
                                       ptb[0:128, 0:128])
                        b.w("tensor", cpt)
                    done = b.tok("vector")
                    b.w("gpsimd", done)
                    for pr in range(NP):
                        b.dma("gpsimd", dst[pr, :, ttile * 128:(ttile + 1) * 128],
                              buf[:, pr * 128:(pr + 1) * 128])
                    b.w("vector", b.dtok("gpsimd"))

    def p4b(self, st):
        nc, b = self.nc, self.b
        en = st.enter_context
        b.w("tensor", self.ctok)
        t = self.p1i_alloc(st, tag="b1", natt=4, npsc=4)
        t["evac_eng"] = "vector"
        t["q65s"] = [t["q65"], en(nc.sbuf_tensor("b_q65b", [65, T], BF16))]
        t["k65s"] = [t["k65"], en(nc.sbuf_tensor("b_k65b", [65, T], BF16))]
        t["vhs"] = [t["vh"], en(nc.sbuf_tensor("b_vhb", [64, T], BF16))]
        t["nrow"] = en(nc.sbuf_tensor("b_nrow", [1, T], BF16))
        t["mrow"] = en(nc.sbuf_tensor("b_mrow", [1, T], BF16))
        ko = b.op("gpsimd", "memset", t["k65s"][1][64:65, :], 1.0)
        b.w("gpsimd", ko)
        q1, k1, v1 = self.scr["q1"], self.scr["k1"], self.scr["v1"]
        y1 = self.scr["ysm1"]

        def bank(h):
            par = h % 2
            t["q65"], t["k65"], t["vh"] = (t["q65s"][par], t["k65s"][par],
                                           t["vhs"][par])
        prevs = [None, None]
        bank(0)
        self.p1i_prep_a(t, 0, q1, k1, v1)
        self.p1i_prep_b(t, 0)
        for h in range(H):
            bank(h)
            self.p1i_main(t, h)
            prevs[h % 2] = t["prev_tensor"]
            if h + 1 < H:
                bank(h + 1)
                t["load_guard"] = prevs[(h + 1) % 2]
                self.p1i_prep_a(t, h + 1, q1, k1, v1)
                self.p1i_prep_b(t, h + 1)
            bank(h)
            self.p1i_fin(t, h, y1)

    def p4c(self, st):
        nc, b = self.nc, self.b
        en = st.enter_context
        wo, wot = self.load_weight_bf16(st, "w1o", 8 * 8 * 128, "w1o")
        yt = [en(nc.sbuf_tensor(f"p4c_y{i}", [128, TB], BF16)) for i in range(KC)]
        xt = en(nc.sbuf_tensor("p4c_x", [128, TB], F32))
        ot = en(nc.sbuf_tensor("p4c_o", [128, TB], F32))
        pp = [en(nc.psum_tensor(f"p4c_pp{i}", [128, 512], F32)) for i in range(2)]
        b.w("tensor", wot, self.ctok)
        for blk in range(NB):
            t0 = blk * TB
            b.barrier()
            lt = [b.dma("sync", yt[i][:], self.scr["ysm1"][i, :, t0:t0 + TB]) for i in range(KC)]
            ppf = [None, None]
            for m in range(KC):
                pi = m % 2
                lx = b.dma("sync", xt[:], self.scr["x1"][m, :, t0:t0 + TB])
                b.w("tensor", ppf[pi], lt[-1] if m == 0 else None)
                for k in range(KC):
                    b.op("tensor", "matmul", pp[pi][:, :],
                         wo[:, (m * 8 + k) * 128:(m * 8 + k) * 128 + 128],
                         yt[k][:], start=(k == 0), stop=(k == KC - 1))
                mm = b.tok("tensor")
                b.w("vector", mm, lx)
                t = b.op("vector", "tensor_add", ot[:], xt[:], pp[pi][:, :])
                ppf[pi] = t
                b.w("gpsimd", t)
                od = b.dma("gpsimd", self.scr["x2p"][m, :, t0:t0 + TB], ot[:])
                b.w("vector", od)
                b.w("sync", t)

    def p5(self, st):
        nc, b = self.nc, self.b
        en = st.enter_context
        HW = T // 2
        NBH = HW // 512
        scl = en(nc.sbuf_tensor("p5_scl", [128, 32 * 4 + 8], F32))
        S1, S2, SH1, SH2, S3 = 0, 32, 64, 96, 128
        lsc = [b.dma("sync", scl[:, S1:S1 + 32], self.inp["f1s1"]),
               b.dma("sync", scl[:, S2:S2 + 32], self.inp["f1s2"]),
               b.dma("sync", scl[:, SH1:SH1 + 32], self.inp["f1sh1"]),
               b.dma("sync", scl[:, SH2:SH2 + 32], self.inp["f1sh2"]),
               b.dma("sync", scl[:, S3:S3 + 8], self.inp["f1s3"])]
        ht = [en(nc.sbuf_tensor(f"p5_h{i}", [128, HW], BF16)) for i in range(32)]
        at = en(nc.sbuf_tensor("p5_a", [128, HW], BF16))
        bt = en(nc.sbuf_tensor("p5_b", [128, HW], BF16))
        wst = en(nc.sbuf_tensor("p5_wst", [128, 2048], F32))
        whb2 = [en(nc.sbuf_tensor(f"p5_whb{i}", [128, 4096], BF16)) for i in range(2)]
        pp = [en(nc.psum_tensor(f"p5_pp{i}", [128, 512], F32)) for i in range(2)]
        b.w("tensor", lsc[-1], self.ctok)

        last_cast = [None]

        def stream_w(name, m, ncols, wf_tok, buf):
            whb = whb2[buf]
            cw = last_cast[0]
            for off in range(0, ncols, 2048):
                wdt = min(2048, ncols - off)
                b.w("sync", cw)
                lw = b.dma("sync", wst[:, 0:wdt],
                           self.inp[name][:, m * ncols + off:m * ncols + off + wdt])
                b.w("gpsimd", lw, wf_tok if off == 0 else None)
                cw = b.op("gpsimd", "tensor_copy", whb[:, off:off + wdt], wst[:, 0:wdt])
                b.w("gpsimd", cw)
            last_cast[0] = cw
            return cw

        for half in range(2):
            t0 = half * HW
            b.barrier()
            hstk = ExitStack()
            xnb = [hstk.enter_context(nc.sbuf_tensor(f"p5_xn{i}_{half}", [128, HW], BF16))
                   for i in range(KC)]
            lt = [b.dma("sync", xnb[i][:], self.scr["xn1"][i, :, t0:t0 + HW]) for i in range(KC)]
            ppf = [None, None]
            ab_tok = None
            sched = []
            for m in range(32):
                sched += [("f1w1", m), ("f1w2", m)]
            wtoks = [None, None]
            casts = [None, None]
            casts[0] = stream_w(sched[0][0], sched[0][1], 8 * 128, None, 0)
            for si, (wnm, m) in enumerate(sched):
                buf = si % 2
                if si + 1 < len(sched):
                    n_nm, n_m = sched[si + 1]
                    casts[1 - buf] = stream_w(n_nm, n_m, 8 * 128, wtoks[1 - buf], 1 - buf)
                whb = whb2[buf]
                for nb2 in range(NBH):
                    pi = nb2 % 2
                    csl = slice(nb2 * 512, (nb2 + 1) * 512)
                    b.w("tensor", ppf[pi], casts[buf] if nb2 == 0 else None,
                        lt[-1] if si == 0 and nb2 == 0 else None)
                    for k in range(KC):
                        b.op("tensor", "matmul", pp[pi][:, :], whb[:, k * 128:(k + 1) * 128],
                             xnb[k][:, csl], start=(k == 0), stop=(k == KC - 1))
                    mm = b.tok("tensor")
                    if wnm == "f1w1":
                        b.w("scalar", mm, ab_tok)
                        t = b.op("scalar", "activation", at[:, csl], pp[pi][:, :], AF.Silu,
                                 scale=scl[:, S1 + m:S1 + m + 1])
                        ppf[pi] = t
                        ab_tok = t
                    else:
                        b.w("vector", mm, ab_tok)
                        t = b.op("vector", "tensor_scalar", bt[:, csl], pp[pi][:, :],
                                 scl[:, S2 + m:S2 + m + 1], None, ALU.mult)
                        ppf[pi] = t
                        b.w("vector", t)
                        t = b.op("vector", "tensor_mul", ht[m][:, csl], at[:, csl], bt[:, csl])
                        ab_tok = t
                wtoks[buf] = b.tok("tensor")
            b.barrier()
            hstk.close()
            hstk2 = ExitStack()
            xt = hstk2.enter_context(nc.sbuf_tensor(f"p5_x_{half}", [128, HW], F32))
            ot = hstk2.enter_context(nc.sbuf_tensor(f"p5_o_{half}", [128, HW], F32))
            for (wname, sbase, dst_scr) in (("f1h1", SH1, self.scr["h1a"]),
                                            ("f1h2", SH2, self.scr["h1b"])):
                gel_tok = ab_tok
                st_dma = None
                wtoks = [None, None]
                casts = [None, None]
                casts[0] = stream_w(wname, 0, 32 * 128, wtoks[0], 0)
                for m in range(32):
                    buf = m % 2
                    if m + 1 < 32:
                        casts[1 - buf] = stream_w(wname, m + 1, 32 * 128, wtoks[1 - buf], 1 - buf)
                    whb = whb2[buf]
                    for nb2 in range(NBH):
                        pi = nb2 % 2
                        csl = slice(nb2 * 512, (nb2 + 1) * 512)
                        b.w("tensor", ppf[pi], casts[buf] if nb2 == 0 else None,
                            gel_tok if m == 0 and nb2 == 0 else None)
                        for k in range(32):
                            b.op("tensor", "matmul", pp[pi][:, :], whb[:, k * 128:(k + 1) * 128],
                                 ht[k][:, csl], start=(k == 0), stop=(k == 31))
                        mm = b.tok("tensor")
                        b.w("scalar", mm, st_dma if nb2 == 0 else None)
                        t = b.op("scalar", "activation", at[:, csl], pp[pi][:, :], AF.Gelu,
                                 scale=scl[:, sbase + m:sbase + m + 1])
                        ppf[pi] = t
                    b.w("gpsimd", b.tok("scalar"))
                    st_dma = b.dma("gpsimd", dst_scr[m, :, t0:t0 + HW], at[:])
                    wtoks[buf] = b.tok("tensor")
                b.w("sync", b.dtok("gpsimd"))
                rl = [b.dma("sync", ht[i][:], dst_scr[i, :, t0:t0 + HW]) for i in range(32)]
                b.w("tensor", rl[-1])
                b.w("vector", rl[-1])
                b.w("scalar", rl[-1])
                ab_tok = None
            wtoks = [None, None]
            casts = [None, None]
            casts[0] = stream_w("f1w3", 0, 32 * 128, wtoks[0], 0)
            for m in range(KC):
                buf = m % 2
                lx = b.dma("sync", xt[:], self.scr["x2p"][m, :, t0:t0 + HW])
                if m + 1 < KC:
                    casts[1 - buf] = stream_w("f1w3", m + 1, 32 * 128, wtoks[1 - buf], 1 - buf)
                whb = whb2[buf]
                for nb2 in range(NBH):
                    pi = nb2 % 2
                    csl = slice(nb2 * 512, (nb2 + 1) * 512)
                    b.w("tensor", ppf[pi], casts[buf] if nb2 == 0 else None)
                    for k in range(32):
                        b.op("tensor", "matmul", pp[pi][:, :], whb[:, k * 128:(k + 1) * 128],
                             ht[k][:, csl], start=(k == 0), stop=(k == 31))
                    mm = b.tok("tensor")
                    b.w("vector", mm, lx if nb2 == 0 else None)
                    t = b.op("vector", "scalar_tensor_tensor", ot[:, csl], pp[pi][:, :],
                             scl[:, S3 + m:S3 + m + 1], xt[:, csl], ALU.mult, ALU.add)
                    ppf[pi] = t
                vtok = b.tok("vector")
                b.w("gpsimd", vtok)
                od = b.dma("gpsimd", self.outx[m * 128:(m + 1) * 128, t0:t0 + HW], ot[:])
                b.w("vector", od)
                b.w("sync", vtok)
                wtoks[buf] = b.tok("tensor")
            b.barrier()
            hstk2.close()

    def finish(self):
        b = self.b
        for eng in ("sync", "scalar", "gpsimd"):
            if b.semc[b.dma_sem[eng]]:
                b.w(eng, b.dtok(eng))



def run(inputs, phases, dbg_out=(), cores=None, sim=False, dbg_in=None):
    xin = np.asarray(inputs["L0_attn_x"])
    w = prep_weights(inputs)
    core_ids = list(range(8)) if cores is None else cores
    dbg_in = dbg_in or {}
    kk = Kern(phases=phases, dbg_out=dbg_out, dbg_in=tuple(dbg_in.keys()))
    nc = kk.build()
    in_maps = []
    for c in core_ids:
        m = dict(w)
        m["x"] = np.ascontiguousarray(xin[c].T, dtype=np.float32)
        for k2, v2 in dbg_in.items():
            m[k2] = v2
        in_maps.append(m)
    if sim:
        from concourse.bass_interp import CoreSim
        nc.finalize()
        s = CoreSim(nc)
        for k2, v2 in in_maps[0].items():
            s.tensor(k2)[:] = v2
        s.simulate()
        names = ["xout"] + list(dbg_out)
        res = [{nm: np.array(s.tensor(nm)) for nm in names}]
    else:
        res = run_bass_kernel_spmd(nc, in_maps, core_ids=core_ids).results
    kk.close()
    return res


def kernel(**inputs):
    res = run(inputs, phases=PHASES_ALL)
    out = np.stack([r["xout"].T for r in res], axis=0)
    return np.ascontiguousarray(out, dtype=np.float32)
